# version 01
# baseline (speedup 1.0000x reference)
import numpy as np
from contextlib import ExitStack

import concourse.bass as bass
import concourse.mybir as mybir
from concourse.bass_utils import run_bass_kernel_spmd

F32 = mybir.dt.float32
BF16 = mybir.dt.bfloat16
AF = mybir.ActivationFunctionType
OP = mybir.AluOpType
AX = mybir.AxisListType

H, DK, DV, C = 16, 128, 128, 128
D, FF, S = 2048, 8192, 2048
NCH = S // C
NT = S // 128
NK = D // 128
NK2 = FF // 128
CLIP = 128.0 - 1e-6
EPS_Q = 1e-6

ENGINES = ("sync", "tensor", "vector", "scalar", "gpsimd")
QSEM = {"sync": "q_sync", "scalar": "q_scalar", "gpsimd": "q_gpsimd"}


def build_consts():
    gamma = 1.0 - np.exp2(-5.0 - np.arange(H, dtype=np.float64))
    inv = 1.0 / 10000.0 ** (np.arange(0, DK, 2, dtype=np.float64) / DK)
    ang = np.arange(S, dtype=np.float64)[:, None] * inv[None, :]
    cos, sin = np.cos(ang), np.sin(ang)
    qs = DK ** -0.5
    ropetab = np.stack([cos * qs, sin * qs, cos, sin], axis=1).astype(np.float32)
    idx = np.arange(C, dtype=np.float64)
    rel = idx[None, :] - idx[:, None]
    dmatT = np.where(rel[:, None, :] >= 0,
                     gamma[None, :, None] ** rel[:, None, :], 0.0)
    dmatT = np.ascontiguousarray(dmatT).astype(np.float32)
    crossdec = (gamma[None, :] ** (idx[:, None] + 1.0)).astype(np.float32)
    kdec = (gamma[None, :] ** (C - 1.0 - idx[:, None])).astype(np.float32)
    gC = np.broadcast_to(gamma ** C, (128, H)).astype(np.float32).copy()
    ident = np.eye(128, dtype=np.float32)
    ones = np.ones((128, 1), dtype=np.float32)
    ones_row = np.ones((1, 128), dtype=np.float32)
    return dict(ropetab=ropetab, dmatT=dmatT, crossdec=crossdec, kdec=kdec,
                gC=gC, ident=ident, ones=ones, ones_row=ones_row)


class P:

    def __init__(self):
        self.ops = []
        self.cnt = {}
        self.res = {}
        self.bar = []

    def v(self, sem):
        return self.cnt.get(sem, 0)

    def emit(self, eng, fn, waits=(), inc=None):
        ws = dict(self.bar)
        for (s, v) in waits:
            if v > ws.get(s, 0):
                ws[s] = v
        self.ops.append((eng, fn, tuple((s, v) for s, v in ws.items() if v > 0), inc))
        if inc is not None:
            self.cnt[inc[0]] = self.cnt.get(inc[0], 0) + inc[1]

    def mark(self, name, sem):
        self.res[name] = (sem, self.v(sem))

    def need(self, *names):
        return [self.res[n] for n in names if n in self.res]

    def dma(self, q, out, in_, waits=(), name=None, slot=None):
        sem = "d_" + (slot or (name or "anon"))
        self.emit(q, lambda e, o=out, i=in_: e.dma_start(out=o, in_=i),
                  waits, (sem, 16))
        if name:
            self.mark(name, sem)

    def op(self, eng, fn, waits=(), name=None):
        sem = "e_" + eng
        inc = (sem, 1)
        if eng in ("vector", "scalar", "gpsimd"):
            waits = list(waits) + [(sem, self.v(sem))]
        elif name is None:
            inc = None
        self.emit(eng, fn, waits, inc)
        if name:
            self.mark(name, sem)

    def alias(self, name, other):
        self.res[name] = self.res[other]

    def tmark(self, name):
        for j in range(len(self.ops) - 1, -1, -1):
            eng, fn, ws, inc = self.ops[j]
            if eng == "tensor":
                assert inc is None, f"last tensor op already incs ({name})"
                self.ops[j] = (eng, fn, ws, ("e_tensor", 1))
                self.cnt["e_tensor"] = self.cnt.get("e_tensor", 0) + 1
                self.mark(name, "e_tensor")
                return
        raise AssertionError(f"tmark({name}): no tensor op found")

    def barrier(self):
        self.bar = [(s, self.v(s)) for s in self.cnt]

    def realize(self, nc, stack):
        sems = {s: stack.enter_context(nc.semaphore(s)) for s in self.cnt}
        for (_, _, ws, _) in self.ops:
            for (s, v) in ws:
                assert v <= self.cnt[s], f"wait {s}>={v} max {self.cnt[s]}"
        block = stack.enter_context(nc.Block())
        ops = self.ops

        def mk(eng_name):
            def emitter(e):
                waited = {}
                for (eng2, fn, ws, inc) in ops:
                    if eng2 != eng_name:
                        continue
                    for (s, v) in ws:
                        if waited.get(s, 0) < v:
                            e.wait_ge(sems[s], v)
                            waited[s] = v
                    ins = fn(e)
                    if inc is not None:
                        ins.then_inc(sems[inc[0]], inc[1])
            return emitter

        for eng in ENGINES:
            getattr(block, eng)(mk(eng))
        return sems


def sqrt_heron(p, dst, a, t1, t2, waits=(), name=None):
    p.op("scalar", lambda e: e.activation(dst, a, AF.Sqrt), waits, name=name and name + "_s")
    p.op("vector", lambda e: e.reciprocal(t1, dst),
         waits=p.need(name + "_s") if name else ())
    p.op("vector", lambda e: e.tensor_mul(t2, a, t1))
    p.op("vector", lambda e: e.tensor_add(dst, dst, t2))
    p.op("vector", lambda e: e.tensor_scalar(dst, dst, 0.5, None, OP.mult), name=name)


def build_program(cfg):
    nc = bass.Bass("TRN2", target_bir_lowering=False, debug=False)
    p = P()
    dbg = set(cfg.get("dbg", ()))
    stages = cfg.get("stages",
                     {"pre", "a1", "a2", "scan", "a4", "b2a", "b2b", "b3", "b3b"})
    n_seq = cfg.get("n_seq", 2)

    def kind(nm):
        return "ExternalOutput" if nm in dbg else "Internal"

    x_h = nc.dram_tensor("x", [n_seq * S, D], F32, kind="ExternalInput").ap()
    w_h = {w: nc.dram_tensor(w, [D, D], F32, kind="ExternalInput").ap()
           for w in ("wq", "wk", "wv", "wg", "wo")}
    wfct_h = nc.dram_tensor("wfct", [D, FF], F32, kind="ExternalInput").ap()
    wpjt_h = nc.dram_tensor("wpjt", [FF, D], F32, kind="ExternalInput").ap()
    gnw_h = nc.dram_tensor("gnw", [128, 1], F32, kind="ExternalInput").ap()
    ctab_h = nc.dram_tensor("ropetab", [S, 4, 64], F32, kind="ExternalInput").ap()
    dmatT_h = nc.dram_tensor("dmatT", [128, H, 128], F32, kind="ExternalInput").ap()
    crossdec_h = nc.dram_tensor("crossdec", [128, H], F32, kind="ExternalInput").ap()
    kdec_h = nc.dram_tensor("kdec", [128, H], F32, kind="ExternalInput").ap()
    gC_h = nc.dram_tensor("gC", [128, H], F32, kind="ExternalInput").ap()
    ident_h = nc.dram_tensor("ident", [128, 128], F32, kind="ExternalInput").ap()
    ones_h = nc.dram_tensor("ones", [128, 1], F32, kind="ExternalInput").ap()
    onesr_h = nc.dram_tensor("ones_row", [1, 128], F32, kind="ExternalInput").ap()
    out_h = nc.dram_tensor("out", [n_seq * S, D], F32, kind="ExternalOutput").ap()

    tfc_h = nc.dram_tensor("s_tfc", [NK, 128, FF], F32, kind=kind("s_tfc")).ap()
    tpj_h = nc.dram_tensor("s_tpj", [NK2, 128, D], F32, kind=kind("s_tpj")).ap()
    gv_h = nc.dram_tensor("s_gv", [128, 4], F32, kind=kind("s_gv")).ap()
    xnf_h = nc.dram_tensor("s_xnf", [NK, 128, S], F32, kind=kind("s_xnf")).ap()
    sq_h = nc.dram_tensor("s_q", [S, D], F32, kind=kind("s_q")).ap()
    sk_h = nc.dram_tensor("s_k", [S, D], F32, kind=kind("s_k")).ap()
    sv_h = nc.dram_tensor("s_v", [S, D], F32, kind=kind("s_v")).ap()
    sg_h = nc.dram_tensor("s_g", [S, D], F32, kind=kind("s_g")).ap()
    sogf_h = nc.dram_tensor("s_ogf", [NK, 128, S], F32, kind=kind("s_ogf")).ap()
    sx2_h = nc.dram_tensor("s_x2", [S, D], F32, kind=kind("s_x2")).ap()
    sx2hf_h = nc.dram_tensor("s_x2hf", [NK, 128, S], F32, kind=kind("s_x2hf")).ap()
    sh2_h = nc.dram_tensor("s_h2", [S, FF], F32, kind=kind("s_h2")).ap()
    sx2h2f_h = nc.dram_tensor("s_x2h2f", [NK2, 128, S], F32, kind=kind("s_x2h2f")).ap()
    soutp_h = nc.dram_tensor("s_outp", [S, D], F32, kind=kind("s_outp")).ap()

    stack = ExitStack()
    e = stack.enter_context
    arena = e(nc.sbuf_tensor("arena", [128, 32768], F32))
    ctab = e(nc.sbuf_tensor("ctab_s", [128, NT, 4, 64], F32))
    dmat = e(nc.sbuf_tensor("dmat", [128, H, 128], F32))
    dec = e(nc.sbuf_tensor("dec", [128, 3, H], F32))
    gnw = e(nc.sbuf_tensor("gnw_s", [128, 1], F32))
    ident = e(nc.sbuf_tensor("ident_s", [128, 128], F32))
    ones = e(nc.sbuf_tensor("ones_s", [128, 1], F32))
    onesr = e(nc.sbuf_tensor("onesr", [1, 128], F32))
    gv = e(nc.sbuf_tensor("gv_s", [128, 4], F32))
    stats = e(nc.sbuf_tensor("stats", [128, 512], F32))
    stg = e(nc.sbuf_tensor("stg", [128, 8192], F32))
    stat2 = e(nc.sbuf_tensor("stat2", [128, 2, 16, 128], F32))
    pA = e(nc.psum_tensor([128, 2048], F32))
    pB = e(nc.psum_tensor([128, 2048], F32))

    AR, STG, ST = arena.ap(), stg.ap(), stats.ap()

    def arv(off, n):
        return AR[:, off:off + n]

    def hview(x):
        return x.rearrange("p (h f) -> p h f", h=H)

    p.dma("sync", ctab.ap(), ctab_h.rearrange("(t p) j f -> p t j f", p=128))
    p.dma("sync", dmat.ap(), dmatT_h)
    p.dma("sync", dec.ap()[:, 0, :], crossdec_h)
    p.dma("sync", dec.ap()[:, 1, :], kdec_h)
    p.dma("sync", dec.ap()[:, 2, :], gC_h)
    p.dma("sync", gnw.ap(), gnw_h)
    p.dma("sync", ident.ap(), ident_h)
    p.dma("sync", ones.ap(), ones_h)
    p.dma("sync", onesr.ap(), onesr_h, name="consts")
    p.barrier()

    crossB = dec.ap()[:, 0:1, :].rearrange("p o h -> p h o").broadcast_to([128, H, 128])
    kdecB = dec.ap()[:, 1:2, :].rearrange("p o h -> p h o").broadcast_to([128, H, 128])
    gCB = dec.ap()[:, 2:3, :].rearrange("p o h -> p h o").broadcast_to([128, H, 128])

    if "pre" in stages:
        wbuf = [arv(0, 8192), arv(8192, 8192)]
        acc = ST[:, 0:2]
        tmp1 = ST[:, 2:3]
        p.op("vector", lambda e: e.memset(acc, 0.0), name="accz")
        specs = (("fc", wfct_h, FF, NK), ("pj", wpjt_h, D, NK2))
        gt = 0
        for wi, (wn, src, width, ntl) in enumerate(specs):
            col = acc[:, wi:wi + 1]
            for t in range(ntl):
                bv = wbuf[gt % 2][:, 0:width]
                p.dma("sync", bv, src[t * 128:(t + 1) * 128, :],
                      waits=p.need(f"prd_{gt-2}"), name=f"pld_{gt}", slot=f"pw{gt%2}")
                p.op("vector", lambda e, bv=bv: e.tensor_reduce(
                        tmp1, bv, AX.X, OP.add, apply_absolute_value=True),
                     waits=p.need(f"pld_{gt}", "accz"))
                p.op("vector", lambda e, col=col: e.tensor_add(col, col, tmp1),
                     name=f"prd_{gt}")
                gt += 1
        p.op("tensor", lambda e: e.matmul(pA[0:1, 0:2], ones.ap(), acc,
                                          start=True, stop=True),
             waits=p.need("prd_78", "prd_79"))
        p.tmark("pps")
        p.op("vector", lambda e: e.tensor_copy(ST[0:1, 4:6], pA[0:1, 0:2]),
             waits=p.need("pps"), name="pg0")
        NW = float(D * FF)
        p.op("vector", lambda e: e.tensor_scalar(
                ST[0:1, 4:6], ST[0:1, 4:6], 1.0 / NW, None, OP.mult),
             waits=p.need("pg0"))
        p.op("vector", lambda e: e.tensor_scalar(
                ST[0:1, 6:8], ST[0:1, 4:6], EPS_Q, None, OP.add))
        p.op("vector", lambda e: e.reciprocal(ST[0:1, 6:8], ST[0:1, 6:8]),
             name="pg1")
        p.op("tensor", lambda e: e.matmul(pA[:, 0:4], onesr.ap(), ST[0:1, 4:8],
                                          start=True, stop=True),
             waits=p.need("pg1"))
        p.tmark("pbc")
        p.op("vector", lambda e: e.tensor_copy(gv.ap(), pA[:, 0:4]),
             waits=p.need("pbc"), name="gv")
        p.dma("gpsimd", gv_h, gv.ap(), waits=p.need("gv"), slot="gvs")
        ifc, ipj = gv.ap()[:, 2:3], gv.ap()[:, 3:4]
        tstage = [arv(16384, 8192), arv(24576, 8192)]
        gt = 0
        for (wn, src, width, ntl), inv, dst in zip(specs, (ifc, ipj), (tfc_h, tpj_h)):
            for t in range(ntl):
                bv = wbuf[gt % 2][:, 0:width]
                ts_ = tstage[gt % 2][:, 0:width]
                zb = STG[:, 0:width]
                p.dma("sync", bv, src[t * 128:(t + 1) * 128, :],
                      waits=p.need(f"p2z_{gt-2}", "prd_78", "prd_79"),
                      name=f"p2l_{gt}", slot=f"pw{gt%2}")
                p.op("scalar", lambda e, bv=bv, zb=zb, inv=inv: e.activation(
                        zb, bv, AF.Identity, bias=0.0, scale=inv),
                     waits=p.need(f"p2l_{gt}", "gv", f"p2b_{gt-1}"),
                     name=f"p2z_{gt}")
                p.op("vector", lambda e, zb=zb, ts_=ts_: e.tensor_scalar(
                        ts_, zb, -0.5, None, OP.is_lt),
                     waits=p.need(f"p2z_{gt}", f"p2s_{gt-2}"))
                p.op("vector", lambda e, zb=zb, ts_=ts_: e.scalar_tensor_tensor(
                        ts_, zb, 0.5, ts_, OP.is_gt, OP.subtract),
                     name=f"p2b_{gt}")
                p.dma("gpsimd", dst[t], ts_,
                      waits=p.need(f"p2b_{gt}"), name=f"p2s_{gt}", slot=f"pt{gt%2}")
                gt += 1
        p.barrier()

    for sq in range(n_seq):
        sx = f"s{sq}_"
        xseq = x_h[sq * S:(sq + 1) * S, :]
        oseq = out_h[sq * S:(sq + 1) * S, :]

        if "a1" in stages:
            xbuf = [arv(0, 2048), arv(2048, 2048)]
            xnb = [arv(4096, 2048), arv(6144, 2048)]
            scrap = arv(8192, 2048)
            for i in range(NT):
                xb = xbuf[i % 2]
                c = 16 * (i % 2)
                ssum, ssq, sm = ST[:, c:c+1], ST[:, c+1:c+2], ST[:, c+2:c+3]
                sva, srstd = ST[:, c+3:c+4], ST[:, c+4:c+5]
                snmr, st1, st2 = ST[:, c+5:c+6], ST[:, c+6:c+7], ST[:, c+7:c+8]
                p.dma("sync", xb, xseq[i * 128:(i + 1) * 128, :],
                      waits=p.need(f"{sx}a1xn{i-2}"), name=f"{sx}a1x{i}", slot=f"x{i%2}")
                wl = p.need(f"{sx}a1x{i}", f"{sx}a1xn{i-2}")
                p.op("vector", lambda e, xb=xb, ssum=ssum: e.reduce_sum(ssum, xb, AX.X),
                     waits=wl)
                p.op("scalar", lambda e, xb=xb, ssq=ssq: e.activation(
                        scrap, xb, AF.Square, accum_out=ssq),
                     waits=wl + p.need(f"{sx}a1sq{i-1}"), name=f"{sx}a1sq{i}")
                p.op("vector", lambda e, sm=sm, ssum=ssum: e.tensor_scalar(
                        sm, ssum, 1.0 / D, None, OP.mult))
                p.op("vector", lambda e, sva=sva, sm=sm: e.tensor_mul(sva, sm, sm),
                     waits=p.need(f"{sx}a1sq{i}"))
                p.op("vector", lambda e, sva=sva, ssq=ssq: e.scalar_tensor_tensor(
                        sva, ssq, 1.0 / D, sva, OP.mult, OP.subtract))
                p.op("vector", lambda e, sva=sva: e.tensor_scalar(
                        sva, sva, 1e-5, None, OP.add), name=f"{sx}a1v{i}")
                sqrt_heron(p, srstd, sva, st1, st2, waits=p.need(f"{sx}a1v{i}"),
                           name=f"{sx}a1h{i}")
                p.op("vector", lambda e, srstd=srstd: e.reciprocal(srstd, srstd))
                p.op("vector", lambda e, snmr=snmr, sm=sm, srstd=srstd: e.tensor_mul(
                        snmr, sm, srstd))
                p.op("vector", lambda e, snmr=snmr: e.tensor_scalar(
                        snmr, snmr, -1.0, None, OP.mult), name=f"{sx}a1st{i}")
                xt = xnb[i % 2]
                p.op("scalar", lambda e, xb=xb, xt=xt, snmr=snmr, srstd=srstd:
                     e.activation(xt, xb, AF.Identity, bias=snmr, scale=srstd),
                     waits=p.need(f"{sx}a1st{i}", f"{sx}a1tg{4*i-5}"),
                     name=f"{sx}a1xn{i}")
                for g in range(4):
                    gi = 4 * i + g
                    ps = (pA, pB)[gi % 2]
                    for j in range(4):
                        kk = g * 4 + j
                        p.op("tensor", lambda e, xt=xt, ps=ps, j=j, kk=kk: e.transpose(
                                ps[:, j * 128:(j + 1) * 128],
                                xt[:, kk * 128:(kk + 1) * 128], ident.ap()),
                             waits=(p.need(f"{sx}a1xn{i}") +
                                    p.need(f"{sx}a1ev{gi-2}")))
                    p.tmark(f"{sx}a1tg{gi}")
                    p.op("vector", lambda e, ps=ps, g=g: e.tensor_copy(
                            STG[:, g * 512:(g + 1) * 512], ps[:, 0:512]),
                         waits=p.need(f"{sx}a1tg{gi}", f"{sx}a1sp{i-1}"),
                         name=f"{sx}a1ev{gi}")
                p.dma("gpsimd",
                      xnf_h[:, :, i * 128:(i + 1) * 128].rearrange("k p t -> p k t"),
                      STG[:, 0:2048].rearrange("p (k t) -> p k t", t=128),
                      waits=p.need(f"{sx}a1ev{4*i+3}"), name=f"{sx}a1sp{i}", slot="a1sp")
            p.barrier()

        if "a2" in stages:
            wslot = arv(0, NK * 2048).rearrange("p (k o) -> p k o", k=NK)
            projs = (("wq", sq_h, 0), ("wk", sk_h, 2), ("wv", sv_h, None),
                     ("wg", sg_h, None))
            for pi, (wn, dst, coff) in enumerate(projs):
                p.dma("sync", wslot, w_h[wn].rearrange("(k p) o -> p k o", p=128),
                      waits=p.need(f"{sx}a2mmdone{pi-1}"), name=f"{sx}a2w{pi}", slot="wld")
                for i in range(NT):
                    ti = pi * NT + i
                    sb = stat2.ap()[:, i % 2, 0:16, :]
                    p.dma("scalar", sb,
                          xnf_h[:, :, i * 128:(i + 1) * 128].rearrange("k p t -> p k t"),
                          waits=p.need(f"{sx}a2mm{ti-2}"), name=f"{sx}a2st{ti}", slot=f"st{ti%2}")
                    ps = (pA, pB)[i % 2]
                    for kk in range(NK):
                        for g in range(4):
                            p.op("tensor", lambda e, ps=ps, sb=sb, kk=kk, g=g:
                                 e.matmul(ps[:, g * 512:(g + 1) * 512],
                                          sb[:, kk, :],
                                          wslot[:, kk, g * 512:(g + 1) * 512],
                                          start=(kk == 0), stop=(kk == NK - 1)),
                                 waits=p.need(f"{sx}a2w{pi}", f"{sx}a2st{ti}",
                                              f"{sx}a2ev{ti-2}"))
                    p.tmark(f"{sx}a2mm{ti}")
                    wmm = p.need(f"{sx}a2mm{ti}", f"{sx}a2sp{ti-1}")
                    if coff is None:
                        if wn == "wv":
                            p.op("vector", lambda e, ps=ps: e.tensor_copy(
                                    STG[:, 0:2048], ps[:, :]),
                                 waits=wmm, name=f"{sx}a2ev{ti}")
                        else:
                            p.op("scalar", lambda e, ps=ps: e.activation(
                                    STG[:, 0:2048], ps[:, :], AF.Silu),
                                 waits=wmm, name=f"{sx}a2ev{ti}")
                    else:
                        cB = ctab.ap()[:, i, coff, :].rearrange("p f -> p () f") \
                            .broadcast_to([128, H, 64])
                        sB = ctab.ap()[:, i, coff + 1, :].rearrange("p f -> p () f") \
                            .broadcast_to([128, H, 64])
                        rv = hview(STG[:, 6144:8192])
                        qc = hview(STG[:, 2048:4096])
                        qs_ = hview(STG[:, 4096:6144])
                        ov = hview(STG[:, 0:2048])
                        p.op("scalar", lambda e, ps=ps: e.activation(
                                STG[:, 6144:8192], ps[:, :], AF.Identity),
                             waits=wmm + p.need(f"{sx}a2ev{ti-1}"),
                             name=f"{sx}a2rv{ti}")
                        wrv = p.need(f"{sx}a2rv{ti}")
                        p.op("vector", lambda e, qc=qc, cB=cB, rv=rv: e.tensor_tensor(
                                qc[:, :, 0:64], rv[:, :, 0:64], cB, OP.mult),
                             waits=wrv)
                        p.op("vector", lambda e, qc=qc, cB=cB, rv=rv: e.tensor_tensor(
                                qc[:, :, 64:128], rv[:, :, 64:128], cB, OP.mult))
                        p.op("gpsimd", lambda e, qs_=qs_, sB=sB, rv=rv: e.tensor_tensor(
                                qs_[:, :, 0:64], rv[:, :, 64:128], sB, OP.mult),
                             waits=wrv + p.need(f"{sx}a2ev{ti-1}"),
                             name=f"{sx}a2s1_{ti}")
                        p.op("gpsimd", lambda e, qs_=qs_, sB=sB, rv=rv: e.tensor_tensor(
                                qs_[:, :, 64:128], rv[:, :, 0:64], sB, OP.mult),
                             name=f"{sx}a2s2_{ti}")
                        p.op("vector", lambda e, ov=ov, qc=qc, qs_=qs_: e.tensor_sub(
                                ov[:, :, 0:64], qc[:, :, 0:64], qs_[:, :, 0:64]),
                             waits=p.need(f"{sx}a2s1_{ti}"))
                        p.op("vector", lambda e, ov=ov, qc=qc, qs_=qs_: e.tensor_add(
                                ov[:, :, 64:128], qc[:, :, 64:128], qs_[:, :, 64:128]),
                             waits=p.need(f"{sx}a2s2_{ti}"), name=f"{sx}a2ev{ti}")
                    p.dma("gpsimd", dst[i * 128:(i + 1) * 128, :], STG[:, 0:2048],
                          waits=p.need(f"{sx}a2ev{ti}"), name=f"{sx}a2sp{ti}", slot="a2sp")
                p.mark(f"{sx}a2mmdone{pi}", "e_tensor")
            p.barrier()

        if "scan" in stages:
            qb = [arv(0, 2048), arv(2048, 2048)]
            kb = [arv(4096, 2048), arv(6144, 2048)]
            vb = [arv(8192, 2048), arv(10240, 2048)]
            gb = [arv(12288, 2048), arv(14336, 2048)]
            Sst = arv(16384, 2048)
            qf = arv(18432, 2048)
            kf = arv(20480, 2048)
            Asb = arv(22528, 2048)
            k2 = arv(24576, 2048)
            ocmb = arv(26624, 2048)
            ogst = arv(28672, 2048)
            scrap2 = arv(30720, 2048)
            p.op("vector", lambda e: e.memset(Sst, 0.0), name=f"{sx}scS")
            for i in range(NCH):
                w0 = i * 128
                p.dma("sync", qb[i % 2], sq_h[w0:w0 + 128, :],
                      waits=p.need(f"{sx}scTq{i-2}"), name=f"{sx}sclq{i}", slot=f"sq{i%2}")
                p.dma("sync", kb[i % 2], sk_h[w0:w0 + 128, :],
                      waits=p.need(f"{sx}sck2_{i-2}", f"{sx}scTk{i-2}"),
                      name=f"{sx}sclk{i}", slot=f"sk{i%2}")
                p.dma("sync", vb[i % 2], sv_h[w0:w0 + 128, :],
                      waits=p.need(f"{sx}scDel{i-2}"), name=f"{sx}sclv{i}", slot=f"sv{i%2}")
                p.dma("scalar", gb[i % 2], sg_h[w0:w0 + 128, :],
                      waits=p.need(f"{sx}scog{i-2}"), name=f"{sx}sclg{i}", slot=f"sg{i%2}")
                q_tm, k_tm = qb[i % 2], kb[i % 2]
                v_tm, sg_tm = vb[i % 2], gb[i % 2]
                for (src, ps, dst_, nm, wx) in (
                        (q_tm, pA, qf, "q", f"{sx}scSup{i-1}"),
                        (k_tm, pB, kf, "k", f"{sx}scogev{i-1}")):
                    for hh in range(H):
                        p.op("tensor", lambda e, src=src, ps=ps, hh=hh: e.transpose(
                                ps[:, hh * 128:(hh + 1) * 128],
                                src[:, hh * 128:(hh + 1) * 128], ident.ap()),
                             waits=p.need(f"{sx}scl{nm}{i}", wx,
                                          f"{sx}scT{nm}ev{i-1}"))
                    p.tmark(f"{sx}scT{nm}{i}")
                    p.op("vector", lambda e, ps=ps, dst_=dst_: e.tensor_copy(
                            dst_, ps[:, :]),
                         waits=p.need(f"{sx}scT{nm}{i}"), name=f"{sx}scT{nm}ev{i}")
                for hh in range(H):
                    p.op("tensor", lambda e, hh=hh: e.matmul(
                            pA[:, hh * 128:(hh + 1) * 128],
                            kf[:, hh * 128:(hh + 1) * 128],
                            qf[:, hh * 128:(hh + 1) * 128],
                            start=(hh % 4 == 0), stop=(hh % 4 == 3)),
                         waits=p.need(f"{sx}scTqev{i}", f"{sx}scTkev{i}"))
                p.tmark(f"{sx}scA{i}")
                p.op("vector", lambda e: e.tensor_tensor(
                        hview(Asb), hview(pA[:, :]), dmat.ap(), OP.mult),
                     waits=p.need(f"{sx}scA{i}"), name=f"{sx}scAm{i}")
                for hh in range(H):
                    p.op("tensor", lambda e, hh=hh, v_tm=v_tm: e.matmul(
                            pB[:, hh * 128:(hh + 1) * 128],
                            Asb[:, hh * 128:(hh + 1) * 128],
                            v_tm[:, hh * 128:(hh + 1) * 128],
                            start=(hh % 4 == 0), stop=(hh % 4 == 3)),
                         waits=p.need(f"{sx}scAm{i}", f"{sx}sclv{i}",
                                      f"{sx}scTkev{i}"))
                p.tmark(f"{sx}scI{i}")
                for hh in range(H):
                    p.op("tensor", lambda e, hh=hh: e.matmul(
                            pA[:, hh * 128:(hh + 1) * 128],
                            qf[:, hh * 128:(hh + 1) * 128],
                            Sst[:, hh * 128:(hh + 1) * 128],
                            start=(hh % 4 == 0), stop=(hh % 4 == 3)),
                         waits=p.need(f"{sx}scAm{i}",
                                      f"{sx}scS" if i == 0 else f"{sx}scSup{i-1}"))
                p.tmark(f"{sx}scX{i}")
                p.op("vector", lambda e: e.tensor_tensor(
                        hview(ocmb), hview(pA[:, :]), crossB, OP.mult),
                     waits=p.need(f"{sx}scX{i}"), name=f"{sx}scOc1{i}")
                p.op("vector", lambda e: e.tensor_add(ocmb, ocmb, pB[:, :]),
                     waits=p.need(f"{sx}scI{i}"), name=f"{sx}scOc{i}")
                p.op("gpsimd", lambda e, k_tm=k_tm: e.tensor_tensor(
                        hview(k2), hview(k_tm), kdecB, OP.mult),
                     waits=p.need(f"{sx}sclk{i}", f"{sx}scSup{i-1}"),
                     name=f"{sx}sck2_{i}")
                for hh in range(H):
                    p.op("tensor", lambda e, hh=hh, v_tm=v_tm: e.matmul(
                            pA[:, hh * 128:(hh + 1) * 128],
                            k2[:, hh * 128:(hh + 1) * 128],
                            v_tm[:, hh * 128:(hh + 1) * 128],
                            start=(hh % 4 == 0), stop=(hh % 4 == 3)),
                         waits=p.need(f"{sx}sck2_{i}", f"{sx}scOc1{i}"))
                p.tmark(f"{sx}scDel{i}")
                p.op("vector", lambda e: e.tensor_tensor(
                        hview(Sst), hview(Sst), gCB, OP.mult),
                     waits=p.need(f"{sx}scX{i}", f"{sx}scDel{i}"))
                p.op("vector", lambda e: e.tensor_add(Sst, Sst, pA[:, :]),
                     name=f"{sx}scSup{i}")
                c = 48
                st16 = ST[:, c:c + 16]
                sa = ST[:, c + 16:c + 32]
                sh1 = ST[:, c + 32:c + 48]
                sh2 = ST[:, c + 48:c + 64]
                p.op("vector", lambda e: e.tensor_mul(scrap2, ocmb, ocmb),
                     waits=p.need(f"{sx}scOc{i}", f"{sx}scsp{i-1}"))
                p.op("vector", lambda e, st16=st16: e.tensor_reduce(
                        st16, hview(scrap2), AX.X, OP.add),
                     waits=p.need(f"{sx}scrms{i-1}"))
                p.op("vector", lambda e, sa=sa, st16=st16: e.tensor_scalar(
                        sa, st16, 1.0 / DV, None, OP.mult, ))
                p.op("vector", lambda e, sa=sa: e.tensor_scalar(
                        sa, sa, 1e-5, None, OP.add), name=f"{sx}scva{i}")
                sqrt_heron(p, st16, sa, sh1, sh2, waits=p.need(f"{sx}scva{i}"),
                           name=f"{sx}scsq{i}")
                p.op("vector", lambda e, st16=st16: e.reciprocal(st16, st16),
                     name=f"{sx}scrms{i}")
                rstdB = st16.rearrange("p h -> p h ()").broadcast_to([128, H, 128])
                p.op("vector", lambda e, rstdB=rstdB: e.tensor_tensor(
                        hview(ogst), hview(ocmb), rstdB, OP.mult),
                     waits=p.need(f"{sx}scogT{i-1}"))
                p.op("vector", lambda e, sg_tm=sg_tm: e.tensor_mul(ogst, ogst, sg_tm),
                     waits=p.need(f"{sx}sclg{i}"), name=f"{sx}scogv{i}")
                for hh in range(H):
                    p.op("tensor", lambda e, hh=hh: e.transpose(
                            pB[:, hh * 128:(hh + 1) * 128],
                            ogst[:, hh * 128:(hh + 1) * 128], ident.ap()),
                         waits=p.need(f"{sx}scogv{i}", f"{sx}scOc{i}",
                                      f"{sx}scogev{i-1}"))
                p.tmark(f"{sx}scogT{i}")
                p.op("vector", lambda e: e.tensor_copy(scrap2, pB[:, :]),
                     waits=p.need(f"{sx}scogT{i}", f"{sx}scsp{i-1}"),
                     name=f"{sx}scogev{i}")
                p.dma("gpsimd",
                      sogf_h[:, :, w0:w0 + 128].rearrange("k p t -> p k t"),
                      scrap2.rearrange("p (k t) -> p k t", t=128),
                      waits=p.need(f"{sx}scogev{i}"), name=f"{sx}scsp{i}", slot="scsp")
                p.alias(f"{sx}scog{i}", f"{sx}scogv{i}")
            p.barrier()

        if "a4" in stages:
            wslot = arv(0, NK * 2048).rearrange("p (k o) -> p k o", k=NK)
            p.dma("sync", wslot, w_h["wo"].rearrange("(k p) o -> p k o", p=128),
                  name=f"{sx}a4w", slot="wld")
            p.op("vector", lambda e: e.tensor_scalar(
                    arv(0, NK * 2048), arv(0, NK * 2048), gnw.ap(), None, OP.mult),
                 waits=p.need(f"{sx}a4w"), name=f"{sx}a4wg")
            for i in range(NT):
                w0 = i * 128
                sb = stat2.ap()[:, i % 2, 0:16, :]
                p.dma("scalar", sb,
                      sogf_h[:, :, w0:w0 + 128].rearrange("k p t -> p k t"),
                      waits=p.need(f"{sx}a4mm{i-2}"), name=f"{sx}a4st{i}",
                      slot=f"st{i%2}")
                ps = (pA, pB)[i % 2]
                for kk in range(NK):
                    for g in range(4):
                        p.op("tensor", lambda e, ps=ps, sb=sb, kk=kk, g=g:
                             e.matmul(ps[:, g * 512:(g + 1) * 512],
                                      sb[:, kk, :],
                                      wslot[:, kk, g * 512:(g + 1) * 512],
                                      start=(kk == 0), stop=(kk == NK - 1)),
                             waits=p.need(f"{sx}a4wg", f"{sx}a4st{i}",
                                          f"{sx}a4ev{i-2}"))
                p.tmark(f"{sx}a4mm{i}")
                p.op("vector", lambda e, ps=ps: e.tensor_copy(
                        STG[:, 0:2048], ps[:, :]),
                     waits=p.need(f"{sx}a4mm{i}", f"{sx}a4sp{i-1}"),
                     name=f"{sx}a4ev{i}")
                p.dma("gpsimd", soutp_h[w0:w0 + 128, :], STG[:, 0:2048],
                      waits=p.need(f"{sx}a4ev{i}"), name=f"{sx}a4sp{i}",
                      slot="a2sp")
            p.barrier()

        if "a4" in stages:
            abuf = [arv(0, 2048), arv(2048, 2048)]
            xbuf = [arv(4096, 2048), arv(6144, 2048)]
            x2b = [arv(8192, 2048), arv(10240, 2048)]
            tb = [arv(12288, 2048), arv(14336, 2048)]
            xqb2 = [arv(16384, 2048), arv(18432, 2048)]
            for i in range(NT):
                w0 = i * 128
                ab, xb, x2v = abuf[i % 2], xbuf[i % 2], x2b[i % 2]
                tv, xqv = tb[i % 2], xqb2[i % 2]
                c = 64 + 24 * (i % 2)
                ssum, ssq = ST[:, c:c+1], ST[:, c+1:c+2]
                sm, sva, srstd = ST[:, c+2:c+3], ST[:, c+3:c+4], ST[:, c+4:c+5]
                st1, st2 = ST[:, c+5:c+6], ST[:, c+6:c+7]
                sm2, sv2 = ST[:, c+7:c+8], ST[:, c+8:c+9]
                sa2, sb2, sc2 = ST[:, c+9:c+10], ST[:, c+10:c+11], ST[:, c+11:c+12]
                smx, sc1, sg2 = ST[:, c+12:c+13], ST[:, c+13:c+14], ST[:, c+14:c+15]
                p.dma("sync", ab, soutp_h[w0:w0 + 128, :],
                      waits=p.need(f"{sx}b1ab{i-2}"), name=f"{sx}b1la{i}",
                      slot=f"x{i%2}")
                p.dma("sync", xb, xseq[w0:w0 + 128, :],
                      waits=p.need(f"{sx}a4x2{i-2}"), name=f"{sx}b1lx{i}",
                      slot=f"xb{i%2}")
                wl = p.need(f"{sx}b1la{i}")
                p.op("vector", lambda e, ab=ab, ssum=ssum: e.reduce_sum(
                        ssum, ab, AX.X), waits=wl)
                p.op("scalar", lambda e, ab=ab, ssq=ssq: e.activation(
                        STG[:, 0:2048], ab, AF.Square, accum_out=ssq),
                     waits=wl, name=f"{sx}a4sq{i}")
                p.op("vector", lambda e, sm=sm, ssum=ssum: e.tensor_scalar(
                        sm, ssum, 1.0 / D, None, OP.mult))
                p.op("vector", lambda e, sva=sva, sm=sm: e.tensor_mul(sva, sm, sm),
                     waits=p.need(f"{sx}a4sq{i}"))
                p.op("vector", lambda e, sva=sva, ssq=ssq: e.scalar_tensor_tensor(
                        sva, ssq, 1.0 / D, sva, OP.mult, OP.subtract))
                p.op("vector", lambda e, sva=sva: e.tensor_scalar(
                        sva, sva, 1e-5, None, OP.add), name=f"{sx}a4v{i}")
                sqrt_heron(p, srstd, sva, st1, st2, waits=p.need(f"{sx}a4v{i}"),
                           name=f"{sx}a4h{i}")
                p.op("vector", lambda e, srstd=srstd: e.reciprocal(srstd, srstd),
                     name=f"{sx}a4rs{i}")
                p.op("vector", lambda e, ab=ab, sm=sm, srstd=srstd, tv=tv:
                     e.tensor_scalar(tv, ab, sm, srstd, OP.subtract, OP.mult),
                     waits=p.need(f"{sx}a4rs{i}"), name=f"{sx}b1ab{i}")
                p.op("vector", lambda e, x2v=x2v, xb=xb, tv=tv: e.tensor_add(
                        x2v, tv, xb),
                     waits=p.need(f"{sx}b1lx{i}", f"{sx}a4spx2{i-2}"),
                     name=f"{sx}a4x2{i}")
                p.dma("gpsimd", sx2_h[w0:w0 + 128, :], x2v,
                      waits=p.need(f"{sx}a4x2{i}"), name=f"{sx}a4spx2{i}",
                      slot=f"x2sp{i%2}")
                wx2 = p.need(f"{sx}a4x2{i}")
                p.op("vector", lambda e, x2v=x2v, ssum=ssum: e.reduce_sum(
                        ssum, x2v, AX.X), waits=wx2)
                p.op("scalar", lambda e, x2v=x2v, ssq=ssq: e.activation(
                        STG[:, 0:2048], x2v, AF.Square, accum_out=ssq),
                     waits=wx2 + p.need(f"{sx}a4sq{i}"), name=f"{sx}b1sq{i}")
                p.op("vector", lambda e, sm2=sm2, ssum=ssum: e.tensor_scalar(
                        sm2, ssum, 1.0 / D, None, OP.mult))
                p.op("vector", lambda e, sv2=sv2, sm2=sm2: e.tensor_mul(sv2, sm2, sm2),
                     waits=p.need(f"{sx}b1sq{i}"))
                p.op("vector", lambda e, sv2=sv2, ssq=ssq: e.scalar_tensor_tensor(
                        sv2, ssq, 1.0 / D, sv2, OP.mult, OP.subtract))
                p.op("vector", lambda e, sa2=sa2, sv2=sv2: e.tensor_scalar(
                        sa2, sv2, 1e-5, None, OP.add), name=f"{sx}b1a{i}")
                p.op("vector", lambda e, sb2=sb2, sa2=sa2, sv2=sv2:
                     e.scalar_tensor_tensor(sb2, sa2, 1e-6, sv2, OP.mult, OP.add),
                     waits=p.need(f"{sx}b1a{i}"), name=f"{sx}b1b{i}")
                sqrt_heron(p, sc2, sb2, st1, st2, waits=p.need(f"{sx}b1b{i}"),
                           name=f"{sx}b1h{i}")
                p.op("vector", lambda e, sc2=sc2: e.reciprocal(sc2, sc2),
                     name=f"{sx}b1rc{i}")
                p.op("vector", lambda e, x2v=x2v, sm2=sm2, tv=tv: e.tensor_scalar(
                        tv, x2v, sm2, None, OP.subtract), name=f"{sx}b1t{i}")
                p.op("vector", lambda e, smx=smx, tv=tv: e.reduce_max(
                        smx, tv, AX.X, apply_absolute_value=True),
                     name=f"{sx}b1mx{i}")
                p.op("vector", lambda e, sc1=sc1, smx=smx: e.reciprocal(sc1, smx))
                p.op("vector", lambda e, sc1=sc1: e.tensor_scalar(
                        sc1, sc1, 128.0, None, OP.mult))
                p.op("vector", lambda e, sg2=sg2, sc2=sc2, smx=smx: e.tensor_mul(
                        sg2, sc2, smx), waits=p.need(f"{sx}b1rc{i}", f"{sx}b1mx{i}"))
                p.op("vector", lambda e, sg2=sg2: e.tensor_scalar(
                        sg2, sg2, 1.0 / 128.0, None, OP.mult), name=f"{sx}b1g{i}")
                p.op("vector", lambda e, tv=tv, sc1=sc1: e.tensor_scalar(
                        tv, tv, sc1, CLIP, OP.mult, OP.min), name=f"{sx}b1u{i}")
                p.op("vector", lambda e, tv=tv, xqv=xqv, sg2=sg2: e.tensor_scalar(
                        xqv, tv, -CLIP, sg2, OP.max, OP.mult),
                     waits=p.need(f"{sx}b1g{i}", f"{sx}b1u{i}"), name=f"{sx}b1xq{i}")
                pt = (pA, pB)[i % 2]
                for kk in range(16):
                    p.op("tensor", lambda e, pt=pt, kk=kk, xqv=xqv: e.transpose(
                            pt[:, kk * 128:(kk + 1) * 128],
                            xqv[:, kk * 128:(kk + 1) * 128],
                            ident.ap()),
                         waits=p.need(f"{sx}b1xq{i}", f"{sx}b1tev{i-2}"))
                p.tmark(f"{sx}b1tg{i}")
                p.op("vector", lambda e, pt=pt: e.tensor_copy(STG[:, 4096:6144],
                                                              pt[:, :]),
                     waits=p.need(f"{sx}b1tg{i}", f"{sx}b1sp{i-1}"),
                     name=f"{sx}b1tev{i}")
                p.dma("gpsimd",
                      sx2hf_h[:, :, w0:w0 + 128].rearrange("k p t -> p k t"),
                      STG[:, 4096:6144].rearrange("p (k t) -> p k t", t=128),
                      waits=p.need(f"{sx}b1tev{i}"), name=f"{sx}b1sp{i}",
                      slot="b1sp")
            p.barrier()

        if "b2a" in stages:
            tq = arv(0, NK * 2048).rearrange("p (k f) -> p k f", k=NK)
            for qq in range(4):
                p.dma("sync", tq,
                      tfc_h[:, :, qq * 2048:(qq + 1) * 2048]
                      .rearrange("k p f -> p k f"),
                      waits=p.need(f"{sx}b2mm{qq*NT-1}"), name=f"{sx}b2t{qq}", slot="wld")
                for i in range(NT):
                    ti = qq * NT + i
                    sb = stat2.ap()[:, i % 2, 0:16, :]
                    p.dma("scalar", sb,
                          sx2hf_h[:, :, i * 128:(i + 1) * 128]
                          .rearrange("k p t -> p k t"),
                          waits=p.need(f"{sx}b2mm{ti-2}"), name=f"{sx}b2st{ti}", slot=f"st{ti%2}")
                    ps = (pA, pB)[ti % 2]
                    for kk in range(NK):
                        for g in range(4):
                            p.op("tensor", lambda e, ps=ps, sb=sb, kk=kk, g=g, tq=tq:
                                 e.matmul(ps[:, g * 512:(g + 1) * 512],
                                          sb[:, kk, :],
                                          tq[:, kk, g * 512:(g + 1) * 512],
                                          start=(kk == 0), stop=(kk == NK - 1)),
                                 waits=p.need(f"{sx}b2t{qq}", f"{sx}b2st{ti}",
                                              f"{sx}b2ev{ti-2}"))
                    p.tmark(f"{sx}b2mm{ti}")
                    p.op("scalar", lambda e, ps=ps: e.activation(
                            STG[:, 0:2048], ps[:, :], AF.Gelu, scale=gv.ap()[:, 0:1]),
                         waits=p.need(f"{sx}b2mm{ti}", f"{sx}b2sp{ti-1}"),
                         name=f"{sx}b2ev{ti}")
                    p.dma("gpsimd",
                          sh2_h[i * 128:(i + 1) * 128, qq * 2048:(qq + 1) * 2048],
                          STG[:, 0:2048],
                          waits=p.need(f"{sx}b2ev{ti}"), name=f"{sx}b2sp{ti}", slot="b2sp")
            p.barrier()

        if "b2b" in stages:
            hbuf = [arv(0, 8192), arv(8192, 8192)]
            xqb = [arv(16384, 8192), arv(24576, 8192)]
            for i in range(NT):
                w0 = i * 128
                hb = hbuf[i % 2]
                xq = xqb[i % 2]
                c = 112 + 16 * (i % 2)
                ssum, ssq = ST[:, c:c+1], ST[:, c+1:c+2]
                sm2, sv2 = ST[:, c+2:c+3], ST[:, c+3:c+4]
                sb2, sc2 = ST[:, c+4:c+5], ST[:, c+5:c+6]
                smx, sc1, sg2 = ST[:, c+6:c+7], ST[:, c+7:c+8], ST[:, c+8:c+9]
                st1, st2 = ST[:, c+9:c+10], ST[:, c+10:c+11]
                p.dma("sync", hb, sh2_h[w0:w0 + 128, :],
                      waits=p.need(f"{sx}b2bt{i-2}"), name=f"{sx}b2bl{i}", slot=f"x{i%2}")
                wl = p.need(f"{sx}b2bl{i}")
                p.op("vector", lambda e, hb=hb, ssum=ssum: e.reduce_sum(
                        ssum, hb, AX.X), waits=wl + p.need(f"{sx}b2bxq{i-1}"))
                p.op("scalar", lambda e, hb=hb, ssq=ssq: e.activation(
                        STG, hb, AF.Square, accum_out=ssq),
                     waits=wl + p.need(f"{sx}b2bsq{i-1}", f"{sx}b2bsp{i-1}"),
                     name=f"{sx}b2bsq{i}")
                p.op("vector", lambda e, sm2=sm2, ssum=ssum: e.tensor_scalar(
                        sm2, ssum, 1.0 / FF, None, OP.mult))
                p.op("vector", lambda e, sv2=sv2, sm2=sm2: e.tensor_mul(
                        sv2, sm2, sm2), waits=p.need(f"{sx}b2bsq{i}"))
                p.op("vector", lambda e, sv2=sv2, ssq=ssq: e.scalar_tensor_tensor(
                        sv2, ssq, 1.0 / FF, sv2, OP.mult, OP.subtract),
                     name=f"{sx}b2bv{i}")
                p.op("vector", lambda e, sb2=sb2, sv2=sv2: e.tensor_scalar(
                        sb2, sv2, EPS_Q, None, OP.add), name=f"{sx}b2ba{i}")
                sqrt_heron(p, sc2, sb2, st1, st2, waits=p.need(f"{sx}b2ba{i}"),
                           name=f"{sx}b2bh{i}")
                p.op("vector", lambda e, sc2=sc2: e.reciprocal(sc2, sc2),
                     name=f"{sx}b2brc{i}")
                p.op("vector", lambda e, hb=hb, xq=xq, sm2=sm2: e.tensor_scalar(
                        xq, hb, sm2, None, OP.subtract), name=f"{sx}b2bt_{i}")
                p.op("vector", lambda e, xq=xq, smx=smx: e.reduce_max(
                        smx, xq, AX.X, apply_absolute_value=True),
                     name=f"{sx}b2bmx{i}")
                p.op("vector", lambda e, sc1=sc1, smx=smx: e.reciprocal(sc1, smx))
                p.op("vector", lambda e, sc1=sc1: e.tensor_scalar(
                        sc1, sc1, 128.0, None, OP.mult))
                p.op("vector", lambda e, sg2=sg2, sc2=sc2, smx=smx: e.tensor_mul(
                        sg2, sc2, smx), waits=p.need(f"{sx}b2brc{i}", f"{sx}b2bmx{i}"))
                p.op("vector", lambda e, sg2=sg2: e.tensor_scalar(
                        sg2, sg2, 1.0 / 128.0, None, OP.mult), name=f"{sx}b2bg{i}")
                p.op("vector", lambda e, xq=xq, sc1=sc1: e.tensor_scalar(
                        xq, xq, sc1, CLIP, OP.mult, OP.min))
                p.op("vector", lambda e, xq=xq, sg2=sg2: e.tensor_scalar(
                        xq, xq, -CLIP, sg2, OP.max, OP.mult),
                     waits=p.need(f"{sx}b2bg{i}"), name=f"{sx}b2bxq{i}")
                for g in range(16):
                    gi = 16 * i + g
                    ps = (pA, pB)[gi % 2]
                    for j in range(4):
                        kk = g * 4 + j
                        p.op("tensor", lambda e, ps=ps, j=j, kk=kk, xq=xq:
                             e.transpose(ps[:, j * 128:(j + 1) * 128],
                                         xq[:, kk * 128:(kk + 1) * 128], ident.ap()),
                             waits=p.need(f"{sx}b2bxq{i}", f"{sx}b2bev{gi-2}"))
                    p.tmark(f"{sx}b2btg{gi}")
                    p.op("vector", lambda e, ps=ps, g=g: e.tensor_copy(
                            STG[:, g * 512:(g + 1) * 512], ps[:, 0:512]),
                         waits=p.need(f"{sx}b2btg{gi}", f"{sx}b2bsp{i-1}"),
                         name=f"{sx}b2bev{gi}")
                p.dma("gpsimd",
                      sx2h2f_h[:, :, w0:w0 + 128].rearrange("k p t -> p k t"),
                      STG.rearrange("p (k t) -> p k t", t=128),
                      waits=p.need(f"{sx}b2bev{16*i+15}"), name=f"{sx}b2bsp{i}", slot="b2bsp")
                p.alias(f"{sx}b2bt{i}", f"{sx}b2bt_{i}")
            p.barrier()

        if "b3" in stages:
            tp8 = arv(0, NK2 * 256).rearrange("p (k f) -> p k f", k=NK2)
            for e8 in range(8):
                p.dma("sync", tp8,
                      tpj_h[:, :, e8 * 256:(e8 + 1) * 256]
                      .rearrange("k p f -> p k f"),
                      waits=p.need(f"{sx}b3mm{e8*NT-1}"), name=f"{sx}b3t{e8}", slot="wld")
                for i in range(NT):
                    ti = e8 * NT + i
                    sb = arv(16384 + (ti % 2) * 8192, 8192) \
                        .rearrange("p (k t) -> p k t", t=128)
                    p.dma("scalar", sb,
                          sx2h2f_h[:, :, i * 128:(i + 1) * 128]
                          .rearrange("k p t -> p k t"),
                          waits=p.need(f"{sx}b3mm{ti-2}"), name=f"{sx}b3st{ti}", slot=f"st{ti%2}")
                    ps = (pA, pB)[ti % 2]
                    for kk in range(NK2):
                        p.op("tensor", lambda e, ps=ps, sb=sb, kk=kk, tp8=tp8:
                             e.matmul(ps[:, 0:256], sb[:, kk, :], tp8[:, kk, :],
                                      start=(kk == 0), stop=(kk == NK2 - 1)),
                             waits=p.need(f"{sx}b3t{e8}", f"{sx}b3st{ti}",
                                          f"{sx}b3ev{ti-2}"))
                    p.tmark(f"{sx}b3mm{ti}")
                    p.op("vector", lambda e, ps=ps: e.tensor_scalar(
                            STG[:, 0:256], ps[:, 0:256], gv.ap()[:, 1:2], None,
                            OP.mult),
                         waits=p.need(f"{sx}b3mm{ti}", f"{sx}b3sp{ti-1}"),
                         name=f"{sx}b3ev{ti}")
                    p.dma("gpsimd",
                          soutp_h[i * 128:(i + 1) * 128, e8 * 256:(e8 + 1) * 256],
                          STG[:, 0:256],
                          waits=p.need(f"{sx}b3ev{ti}"), name=f"{sx}b3sp{ti}", slot="b3sp")
            p.barrier()

        if "b3b" in stages:
            obuf = [arv(0, 2048), arv(2048, 2048)]
            x2b = [arv(4096, 2048), arv(6144, 2048)]
            for i in range(NT):
                w0 = i * 128
                ob, x2v = obuf[i % 2], x2b[i % 2]
                c = 144 + 12 * (i % 2)
                ssum, ssq = ST[:, c:c+1], ST[:, c+1:c+2]
                sm, sva, srstd = ST[:, c+2:c+3], ST[:, c+3:c+4], ST[:, c+4:c+5]
                st1, st2 = ST[:, c+5:c+6], ST[:, c+6:c+7]
                p.dma("sync", ob, soutp_h[w0:w0 + 128, :],
                      waits=p.need(f"{sx}b3bo{i-2}"), name=f"{sx}b3bl{i}", slot=f"x{i%2}")
                p.dma("sync", x2v, sx2_h[w0:w0 + 128, :],
                      waits=p.need(f"{sx}b3bo{i-2}"), name=f"{sx}b3blx{i}", slot=f"xb{i%2}")
                wl = p.need(f"{sx}b3bl{i}")
                p.op("vector", lambda e, ob=ob, ssum=ssum: e.reduce_sum(
                        ssum, ob, AX.X), waits=wl + p.need(f"{sx}b3bo{i-1}"))
                p.op("scalar", lambda e, ob=ob, ssq=ssq: e.activation(
                        STG[:, 0:2048], ob, AF.Square, accum_out=ssq),
                     waits=wl + p.need(f"{sx}b3bsq{i-1}"), name=f"{sx}b3bsq{i}")
                p.op("vector", lambda e, sm=sm, ssum=ssum: e.tensor_scalar(
                        sm, ssum, 1.0 / D, None, OP.mult))
                p.op("vector", lambda e, sva=sva, sm=sm: e.tensor_mul(sva, sm, sm),
                     waits=p.need(f"{sx}b3bsq{i}"))
                p.op("vector", lambda e, sva=sva, ssq=ssq: e.scalar_tensor_tensor(
                        sva, ssq, 1.0 / D, sva, OP.mult, OP.subtract))
                p.op("vector", lambda e, sva=sva: e.tensor_scalar(
                        sva, sva, 1e-5, None, OP.add), name=f"{sx}b3bv{i}")
                sqrt_heron(p, srstd, sva, st1, st2, waits=p.need(f"{sx}b3bv{i}"),
                           name=f"{sx}b3bh{i}")
                p.op("vector", lambda e, srstd=srstd: e.reciprocal(srstd, srstd),
                     name=f"{sx}b3brs{i}")
                p.op("vector", lambda e, ob=ob, sm=sm, srstd=srstd: e.tensor_scalar(
                        STG[:, 2048:4096], ob, sm, srstd, OP.subtract, OP.mult),
                     waits=p.need(f"{sx}b3brs{i}", f"{sx}b3bo{i-1}"))
                p.op("vector", lambda e, x2v=x2v: e.tensor_add(
                        STG[:, 2048:4096], STG[:, 2048:4096], x2v),
                     waits=p.need(f"{sx}b3blx{i}", f"{sx}b3bst{i-1}"),
                     name=f"{sx}b3bst{i}")
                p.dma("gpsimd", oseq[w0:w0 + 128, :], STG[:, 2048:4096],
                      waits=p.need(f"{sx}b3bst{i}"), name=f"{sx}b3bo{i}", slot="outs")
            p.barrier()

    p.emit("gpsimd", lambda e: e.nop(),
           [(s, p.v(s)) for s in p.cnt if s.startswith("d_")], None)

    return nc, p, stack


def build_and_realize(cfg):
    nc, p, stack = build_program(cfg)
    with stack:
        p.realize(nc, stack)
    return nc, len(p.ops)


def make_in_map(inputs, core, n_seq=2):
    consts = build_consts()
    x = np.ascontiguousarray(inputs["L0_attn_x"][2 * core:2 * core + n_seq]
                             .reshape(n_seq * S, D))
    m = dict(x=x,
             wq=inputs["L0_attn_Wq"], wk=inputs["L0_attn_Wk"],
             wv=inputs["L0_attn_Wv"], wg=inputs["L0_attn_Wg"],
             wo=inputs["L0_attn_Wo"],
             wfct=np.ascontiguousarray(inputs["L0_ffn_w_fc"].T),
             wpjt=np.ascontiguousarray(inputs["L0_ffn_w_proj"].T),
             gnw=np.ascontiguousarray(inputs["L0_attn_g_norm_w"].reshape(128, 1)),
             **consts)
    return m


def kernel(**inputs):
    inputs = {k: np.asarray(v, dtype=np.float32) for k, v in inputs.items()}
    nc, nops = build_and_realize({})
    in_maps = [make_in_map(inputs, c) for c in range(8)]
    res = run_bass_kernel_spmd(nc, in_maps, core_ids=list(range(8)))
    out = np.concatenate([r["out"].reshape(2, S, D) for r in res.results], axis=0)
    return out


# version 02
# speedup vs baseline: 1.3194x; 1.3194x over previous
import numpy as np
from contextlib import ExitStack

import concourse.bass as bass
import concourse.mybir as mybir
from concourse.bass_utils import run_bass_kernel_spmd

F32 = mybir.dt.float32
BF16 = mybir.dt.bfloat16
AF = mybir.ActivationFunctionType
OP = mybir.AluOpType
AX = mybir.AxisListType

H, DK, DV, C = 16, 128, 128, 128
D, FF, S = 2048, 8192, 2048
NCH = S // C
NT = S // 128
NK = D // 128
NK2 = FF // 128
CLIP = 128.0 - 1e-6
EPS_Q = 1e-6

ENGINES = ("sync", "tensor", "vector", "scalar", "gpsimd")


def build_consts():
    gamma = 1.0 - np.exp2(-5.0 - np.arange(H, dtype=np.float64))
    inv = 1.0 / 10000.0 ** (np.arange(0, DK, 2, dtype=np.float64) / DK)
    ang = np.arange(S, dtype=np.float64)[:, None] * inv[None, :]
    cos, sin = np.cos(ang), np.sin(ang)
    qs = DK ** -0.5
    ropetab = np.stack([cos * qs, sin * qs, cos, sin], axis=1).astype(np.float32)
    idx = np.arange(C, dtype=np.float64)
    rel = idx[None, :] - idx[:, None]
    dmatT = np.where(rel[:, None, :] >= 0,
                     gamma[None, :, None] ** rel[:, None, :], 0.0)
    dmatT = np.ascontiguousarray(dmatT).astype(np.float32)
    crossdec = (gamma[None, :] ** (idx[:, None] + 1.0)).astype(np.float32)
    kdec = (gamma[None, :] ** (C - 1.0 - idx[:, None])).astype(np.float32)
    gC = np.broadcast_to(gamma ** C, (128, H)).astype(np.float32).copy()
    ident = np.eye(128, dtype=np.float32)
    ones = np.ones((128, 1), dtype=np.float32)
    ones_row = np.ones((1, 128), dtype=np.float32)
    return dict(ropetab=ropetab, dmatT=dmatT, crossdec=crossdec, kdec=kdec,
                gC=gC, ident=ident, ones=ones, ones_row=ones_row)


class P:

    def __init__(self):
        self.ops = []
        self.cnt = {}
        self.res = {}
        self.bar = []

    def v(self, sem):
        return self.cnt.get(sem, 0)

    def emit(self, eng, fn, waits=(), inc=None):
        ws = dict(self.bar)
        for (s, v) in waits:
            if v > ws.get(s, 0):
                ws[s] = v
        self.ops.append((eng, fn, tuple((s, v) for s, v in ws.items() if v > 0), inc))
        if inc is not None:
            self.cnt[inc[0]] = self.cnt.get(inc[0], 0) + inc[1]

    def mark(self, name, sem):
        self.res[name] = (sem, self.v(sem))

    def need(self, *names):
        return [self.res[n] for n in names if n in self.res]

    def dma(self, q, out, in_, waits=(), name=None, slot=None):
        sem = "d_" + (slot or (name or "anon"))
        self.emit(q, lambda e, o=out, i=in_: e.dma_start(out=o, in_=i),
                  waits, (sem, 16))
        if name:
            self.mark(name, sem)

    def op(self, eng, fn, waits=(), name=None):
        sem = "e_" + eng
        inc = (sem, 1)
        if eng in ("vector", "scalar", "gpsimd"):
            waits = list(waits) + [(sem, self.v(sem))]
        elif name is None:
            inc = None
        self.emit(eng, fn, waits, inc)
        if name:
            self.mark(name, sem)

    def alias(self, name, other):
        if other in self.res:
            self.res[name] = self.res[other]

    def tmark(self, name):
        for j in range(len(self.ops) - 1, -1, -1):
            eng, fn, ws, inc = self.ops[j]
            if eng == "tensor":
                assert inc is None, f"last tensor op already incs ({name})"
                self.ops[j] = (eng, fn, ws, ("e_tensor", 1))
                self.cnt["e_tensor"] = self.cnt.get("e_tensor", 0) + 1
                self.mark(name, "e_tensor")
                return
        raise AssertionError(f"tmark({name}): no tensor op found")

    def barrier(self):
        self.bar = [(s, self.v(s)) for s in self.cnt]

    def realize(self, nc, stack):
        sems = {s: stack.enter_context(nc.semaphore(s)) for s in self.cnt}
        for (_, _, ws, _) in self.ops:
            for (s, v) in ws:
                assert v <= self.cnt[s], f"wait {s}>={v} max {self.cnt[s]}"
        block = stack.enter_context(nc.Block())
        ops = self.ops

        def mk(eng_name):
            def emitter(e):
                waited = {}
                for (eng2, fn, ws, inc) in ops:
                    if eng2 != eng_name:
                        continue
                    for (s, v) in ws:
                        if waited.get(s, 0) < v:
                            e.wait_ge(sems[s], v)
                            waited[s] = v
                    ins = fn(e)
                    if inc is not None:
                        ins.then_inc(sems[inc[0]], inc[1])
            return emitter

        for eng in ENGINES:
            getattr(block, eng)(mk(eng))
        return sems


def sqrt_heron(p, dst, a, t1, t2, waits=(), name=None):
    p.op("scalar", lambda e: e.activation(dst, a, AF.Sqrt), waits,
         name=name and name + "_s")
    p.op("vector", lambda e: e.reciprocal(t1, dst),
         waits=p.need(name + "_s") if name else ())
    p.op("vector", lambda e: e.tensor_mul(t2, a, t1))
    p.op("vector", lambda e: e.tensor_add(dst, dst, t2))
    p.op("vector", lambda e: e.tensor_scalar(dst, dst, 0.5, None, OP.mult),
         name=name)


def ln_stats(p, src, ssum, ssq, sm, sva, srstd, st1, st2, eps, wl, tag, scrap,
             sq_extra=()):
    W = src.shape[-1]
    p.op("vector", lambda e: e.reduce_sum(ssum, src, AX.X), waits=wl)
    p.op("scalar", lambda e: e.activation(scrap, src, AF.Square, accum_out=ssq),
         waits=list(wl) + list(sq_extra), name=tag + "sq")
    p.op("vector", lambda e: e.tensor_scalar(sm, ssum, 1.0 / W, None, OP.mult))
    p.op("vector", lambda e: e.tensor_mul(sva, sm, sm), waits=p.need(tag + "sq"))
    p.op("vector", lambda e: e.scalar_tensor_tensor(
            sva, ssq, 1.0 / W, sva, OP.mult, OP.subtract))
    p.op("vector", lambda e: e.tensor_scalar(sva, sva, eps, None, OP.add),
         name=tag + "v")
    sqrt_heron(p, srstd, sva, st1, st2, waits=p.need(tag + "v"), name=tag + "h")
    p.op("vector", lambda e: e.reciprocal(srstd, srstd), name=tag + "rs")


def build_program(cfg):
    nc = bass.Bass("TRN2", target_bir_lowering=False, debug=False)
    p = P()
    dbg = set(cfg.get("dbg", ()))
    stages = cfg.get("stages",
                     {"pre", "a1", "a2", "scan", "a4", "b1", "b2a", "b2b",
                      "b3", "b3b"})
    n_seq = cfg.get("n_seq", 2)

    def kind(nm):
        return "ExternalOutput" if nm in dbg else "Internal"

    x_h = nc.dram_tensor("x", [n_seq * S, D], F32, kind="ExternalInput").ap()
    w_h = {w: nc.dram_tensor(w, [D, D], F32, kind="ExternalInput").ap()
           for w in ("wq", "wk", "wv", "wg", "wo")}
    wfct_h = nc.dram_tensor("wfct", [D, FF], F32, kind="ExternalInput").ap()
    wpjt_h = nc.dram_tensor("wpjt", [FF, D], F32, kind="ExternalInput").ap()
    gnw_h = nc.dram_tensor("gnw", [128, 1], F32, kind="ExternalInput").ap()
    ctab_h = nc.dram_tensor("ropetab", [S, 4, 64], F32, kind="ExternalInput").ap()
    dmatT_h = nc.dram_tensor("dmatT", [128, H, 128], F32, kind="ExternalInput").ap()
    crossdec_h = nc.dram_tensor("crossdec", [128, H], F32, kind="ExternalInput").ap()
    kdec_h = nc.dram_tensor("kdec", [128, H], F32, kind="ExternalInput").ap()
    gC_h = nc.dram_tensor("gC", [128, H], F32, kind="ExternalInput").ap()
    ident_h = nc.dram_tensor("ident", [128, 128], F32, kind="ExternalInput").ap()
    ones_h = nc.dram_tensor("ones", [128, 1], F32, kind="ExternalInput").ap()
    onesr_h = nc.dram_tensor("ones_row", [1, 128], F32, kind="ExternalInput").ap()
    out_h = nc.dram_tensor("out", [n_seq * S, D], F32, kind="ExternalOutput").ap()

    tfc_h = nc.dram_tensor("s_tfc", [NK, 128, FF], BF16, kind=kind("s_tfc")).ap()
    tpj_h = nc.dram_tensor("s_tpj", [NK2, 128, D], BF16, kind=kind("s_tpj")).ap()
    gv_h = nc.dram_tensor("s_gv", [128, 4], F32, kind=kind("s_gv")).ap()
    xnf_h = nc.dram_tensor("s_xnf", [2, NK, 128, S], BF16, kind=kind("s_xnf")).ap()
    sq_h = nc.dram_tensor("s_q", [S, D], F32, kind=kind("s_q")).ap()
    sk_h = nc.dram_tensor("s_k", [S, D], F32, kind=kind("s_k")).ap()
    sv_h = nc.dram_tensor("s_v", [S, D], F32, kind=kind("s_v")).ap()
    sg_h = nc.dram_tensor("s_g", [S, D], F32, kind=kind("s_g")).ap()
    sogf_h = nc.dram_tensor("s_ogf", [2, NK, 128, S], BF16, kind=kind("s_ogf")).ap()
    sx2_h = nc.dram_tensor("s_x2", [S, D], F32, kind=kind("s_x2")).ap()
    sx2hf_h = nc.dram_tensor("s_x2hf", [2, NK, 128, S], BF16,
                             kind=kind("s_x2hf")).ap()
    sh2_h = nc.dram_tensor("s_h2", [S, FF], F32, kind=kind("s_h2")).ap()
    sx2h2f_h = nc.dram_tensor("s_x2h2f", [2, NK2, 128, S], BF16,
                              kind=kind("s_x2h2f")).ap()
    soutp_h = nc.dram_tensor("s_outp", [S, D], F32, kind=kind("s_outp")).ap()

    stack = ExitStack()
    e = stack.enter_context
    arena = e(nc.sbuf_tensor("arena", [128, 32768], F32))
    ctab = e(nc.sbuf_tensor("ctab_s", [128, NT, 4, 64], F32))
    dmat = e(nc.sbuf_tensor("dmat", [128, H, 128], F32))
    dec = e(nc.sbuf_tensor("dec", [128, 3, H], F32))
    gnw = e(nc.sbuf_tensor("gnw_s", [128, 1], F32))
    ident = e(nc.sbuf_tensor("ident_s", [128, 128], F32))
    identb = e(nc.sbuf_tensor("identb", [128, 128], BF16))
    ones = e(nc.sbuf_tensor("ones_s", [128, 1], F32))
    onesr = e(nc.sbuf_tensor("onesr", [1, 128], F32))
    gv = e(nc.sbuf_tensor("gv_s", [128, 4], F32))
    stats = e(nc.sbuf_tensor("stats", [128, 256], F32))
    stg = e(nc.sbuf_tensor("stg", [128, 8192], F32))
    stat2 = e(nc.sbuf_tensor("stat2", [128, 2, 16, 128], F32))
    pA = e(nc.psum_tensor([128, 2048], F32))
    pB = e(nc.psum_tensor([128, 2048], F32))

    AR, STG, ST = arena.ap(), stg.ap(), stats.ap()
    STGb = STG.bitcast(BF16)
    pAb = pA[:, :].bitcast(BF16)
    pBb = pB[:, :].bitcast(BF16)

    def arv(off, n):
        return AR[:, off:off + n]

    def arvb(off, n):
        return AR[:, off:off + n].bitcast(BF16)

    def hview(x):
        return x.rearrange("p (h f) -> p h f", h=H)

    def statpair(par):
        return stat2.ap()[:, par, :, :].rearrange("p k t -> p (k t)") \
            .bitcast(BF16).rearrange("p (l k t) -> p l k t", l=2, t=128)

    p.dma("sync", ctab.ap(), ctab_h.rearrange("(t p) j f -> p t j f", p=128))
    p.dma("sync", dmat.ap(), dmatT_h)
    p.dma("sync", dec.ap()[:, 0, :], crossdec_h)
    p.dma("sync", dec.ap()[:, 1, :], kdec_h)
    p.dma("sync", dec.ap()[:, 2, :], gC_h)
    p.dma("sync", gnw.ap(), gnw_h)
    p.dma("sync", ident.ap(), ident_h)
    p.dma("sync", ones.ap(), ones_h)
    p.dma("sync", onesr.ap(), onesr_h, name="consts")
    p.dma("gpsimd", identb.ap(), ident_h, name="identb", slot="cb")
    p.barrier()

    crossB = dec.ap()[:, 0:1, :].rearrange("p o h -> p h o").broadcast_to([128, H, 128])
    kdecB = dec.ap()[:, 1:2, :].rearrange("p o h -> p h o").broadcast_to([128, H, 128])
    gCB = dec.ap()[:, 2:3, :].rearrange("p o h -> p h o").broadcast_to([128, H, 128])

    if "pre" in stages:
        wbuf = [arv(0, 8192), arv(8192, 8192)]
        acc = ST[:, 0:2]
        tmp1 = ST[:, 2:3]
        p.op("vector", lambda e: e.memset(acc, 0.0), name="accz")
        specs = (("fc", wfct_h, FF, NK), ("pj", wpjt_h, D, NK2))
        gt = 0
        for wi, (wn, src, width, ntl) in enumerate(specs):
            col = acc[:, wi:wi + 1]
            for t in range(ntl):
                bv = wbuf[gt % 2][:, 0:width]
                p.dma("sync", bv, src[t * 128:(t + 1) * 128, :],
                      waits=p.need(f"prd_{gt-2}"), name=f"pld_{gt}",
                      slot=f"pw{gt%2}")
                p.op("vector", lambda e, bv=bv: e.tensor_reduce(
                        tmp1, bv, AX.X, OP.add, apply_absolute_value=True),
                     waits=p.need(f"pld_{gt}", "accz"))
                p.op("vector", lambda e, col=col: e.tensor_add(col, col, tmp1),
                     name=f"prd_{gt}")
                gt += 1
        p.op("tensor", lambda e: e.matmul(pA[0:1, 0:2], ones.ap(), acc,
                                          start=True, stop=True),
             waits=p.need("prd_78", "prd_79"))
        p.tmark("pps")
        p.op("vector", lambda e: e.tensor_copy(ST[0:1, 4:6], pA[0:1, 0:2]),
             waits=p.need("pps"), name="pg0")
        NW = float(D * FF)
        p.op("vector", lambda e: e.tensor_scalar(
                ST[0:1, 4:6], ST[0:1, 4:6], 1.0 / NW, None, OP.mult),
             waits=p.need("pg0"))
        p.op("vector", lambda e: e.tensor_scalar(
                ST[0:1, 6:8], ST[0:1, 4:6], EPS_Q, None, OP.add))
        p.op("vector", lambda e: e.reciprocal(ST[0:1, 6:8], ST[0:1, 6:8]),
             name="pg1")
        p.op("tensor", lambda e: e.matmul(pA[:, 0:4], onesr.ap(), ST[0:1, 4:8],
                                          start=True, stop=True),
             waits=p.need("pg1"))
        p.tmark("pbc")
        p.op("vector", lambda e: e.tensor_copy(gv.ap(), pA[:, 0:4]),
             waits=p.need("pbc"), name="gv")
        p.dma("gpsimd", gv_h, gv.ap(), waits=p.need("gv"), slot="gvs")
        ifc, ipj = gv.ap()[:, 2:3], gv.ap()[:, 3:4]
        tstage = [arvb(16384, 4096), arvb(20480, 4096)]
        gt = 0
        for (wn, src, width, ntl), inv, dst in zip(specs, (ifc, ipj),
                                                   (tfc_h, tpj_h)):
            for t in range(ntl):
                bv = wbuf[gt % 2][:, 0:width]
                ts_ = tstage[gt % 2][:, 0:width]
                zb = STG[:, 0:width]
                p.dma("sync", bv, src[t * 128:(t + 1) * 128, :],
                      waits=p.need(f"p2z_{gt-2}", "prd_78", "prd_79"),
                      name=f"p2l_{gt}", slot=f"pw{gt%2}")
                p.op("scalar", lambda e, bv=bv, zb=zb, inv=inv: e.activation(
                        zb, bv, AF.Identity, bias=0.0, scale=inv),
                     waits=p.need(f"p2l_{gt}", "gv", f"p2b_{gt-1}"),
                     name=f"p2z_{gt}")
                p.op("vector", lambda e, zb=zb, ts_=ts_: e.tensor_scalar(
                        ts_, zb, -0.5, None, OP.is_lt),
                     waits=p.need(f"p2z_{gt}", f"p2s_{gt-2}"))
                p.op("vector", lambda e, zb=zb, ts_=ts_: e.scalar_tensor_tensor(
                        ts_, zb, 0.5, ts_, OP.is_gt, OP.subtract),
                     name=f"p2b_{gt}")
                p.dma("gpsimd", dst[t], ts_,
                      waits=p.need(f"p2b_{gt}"), name=f"p2s_{gt}",
                      slot=f"pt{gt%2}")
                gt += 1
        p.barrier()

    for sq in range(n_seq):
        sx = f"s{sq}_"
        xseq = x_h[sq * S:(sq + 1) * S, :]
        oseq = out_h[sq * S:(sq + 1) * S, :]

        def build_wpair(tag, wap, scale_ap=None, extra=()):
            whi = arvb(0, 16384).rearrange("p (k o) -> p k o", k=NK)
            wlo = arvb(16384, 16384).rearrange("p (k o) -> p k o", k=NK)
            wst = STG[:, 2048:4096]
            for kk in range(NK):
                p.dma("sync", wst, wap[kk * 128:(kk + 1) * 128, :],
                      waits=p.need(f"{tag}l{kk-1}") + (list(extra) if kk < 2 else []),
                      name=f"{tag}ld{kk}", slot="wld")
                wv = p.need(f"{tag}ld{kk}")
                if scale_ap is None:
                    p.op("vector", lambda e, kk=kk: e.tensor_copy(
                            whi[:, kk, :], wst), waits=wv + list(extra))
                    p.op("vector", lambda e, kk=kk: e.tensor_tensor(
                            wlo[:, kk, :], wst, whi[:, kk, :], OP.subtract),
                         name=f"{tag}l{kk}")
                else:
                    p.op("vector", lambda e, kk=kk: e.tensor_scalar(
                            whi[:, kk, :], wst, scale_ap, None, OP.mult),
                         waits=wv + list(extra))
                    p.op("vector", lambda e, kk=kk: e.scalar_tensor_tensor(
                            wlo[:, kk, :], wst, scale_ap, whi[:, kk, :],
                            OP.mult, OP.subtract),
                         name=f"{tag}l{kk}")
            p.mark(f"{tag}done", "e_vector")
            return whi, wlo

        if "a1" in stages:
            xbuf = [arv(0, 2048), arv(2048, 2048)]
            xnb = [arv(4096, 2048), arv(6144, 2048)]
            xnpb = [arvb(8192, 2048), arvb(10240, 2048)]
            scrap = arv(12288, 2048)
            for i in range(NT):
                xb = xbuf[i % 2]
                c = 16 * (i % 2)
                ssum, ssq, sm = ST[:, c:c+1], ST[:, c+1:c+2], ST[:, c+2:c+3]
                sva, srstd = ST[:, c+3:c+4], ST[:, c+4:c+5]
                snmr, st1, st2 = ST[:, c+5:c+6], ST[:, c+6:c+7], ST[:, c+7:c+8]
                p.dma("sync", xb, xseq[i * 128:(i + 1) * 128, :],
                      waits=p.need(f"{sx}a1xn{i-2}"), name=f"{sx}a1x{i}",
                      slot=f"x{i%2}")
                wl = p.need(f"{sx}a1x{i}")
                ln_stats(p, xb, ssum, ssq, sm, sva, srstd, st1, st2, 1e-5,
                         wl, f"{sx}a1s{i}", scrap)
                p.op("vector", lambda e, snmr=snmr, sm=sm, srstd=srstd:
                     e.tensor_mul(snmr, sm, srstd))
                p.op("vector", lambda e, snmr=snmr: e.tensor_scalar(
                        snmr, snmr, -1.0, None, OP.mult), name=f"{sx}a1st{i}")
                xt = xnb[i % 2]
                xp = xnpb[i % 2]
                p.op("scalar", lambda e, xb=xb, xt=xt, snmr=snmr, srstd=srstd:
                     e.activation(xt, xb, AF.Identity, bias=snmr, scale=srstd),
                     waits=p.need(f"{sx}a1st{i}", f"{sx}a1pr{i-2}"),
                     name=f"{sx}a1xn{i}")
                p.op("vector", lambda e, xt=xt, xp=xp: e.tensor_copy(
                        xp[:, 0:2048], xt),
                     waits=p.need(f"{sx}a1xn{i}", f"{sx}a1tg{i-2}"))
                p.op("vector", lambda e, xt=xt, xp=xp: e.tensor_tensor(
                        xp[:, 2048:4096], xt, xp[:, 0:2048], OP.subtract),
                     name=f"{sx}a1pr{i}")
                ps = (pAb, pBb)[i % 2]
                for l in range(2):
                    for kk in range(16):
                        p.op("tensor", lambda e, ps=ps, l=l, kk=kk, xp=xp:
                             e.transpose(
                                ps[:, l * 2048 + kk * 128:
                                   l * 2048 + (kk + 1) * 128],
                                xp[:, l * 2048 + kk * 128:
                                   l * 2048 + (kk + 1) * 128],
                                identb.ap()),
                             waits=p.need(f"{sx}a1pr{i}", f"{sx}a1ev{i-2}"))
                p.tmark(f"{sx}a1tg{i}")
                p.op("vector", lambda e, ps=ps: e.tensor_copy(
                        STGb[:, 0:4096], ps[:, 0:4096]),
                     waits=p.need(f"{sx}a1tg{i}", f"{sx}a1sp0_{i-1}",
                                  f"{sx}a1sp1_{i-1}"),
                     name=f"{sx}a1ev{i}")
                for l in range(2):
                    p.dma("gpsimd",
                          xnf_h[l, :, :, i * 128:(i + 1) * 128]
                          .rearrange("k p t -> p k t"),
                          STGb[:, l * 2048:(l + 1) * 2048]
                          .rearrange("p (k t) -> p k t", t=128),
                          waits=p.need(f"{sx}a1ev{i}"),
                          name=f"{sx}a1sp{l}_{i}", slot=f"a1sp{l}")
            p.barrier()

        if "a2" in stages:
            projs = (("wq", sq_h, 0), ("wk", sk_h, 2), ("wv", sv_h, None),
                     ("wg", sg_h, None))
            for pi, (wn, dst, coff) in enumerate(projs):
                whi, wlo = build_wpair(
                    f"{sx}a2w{pi}_", w_h[wn],
                    extra=p.need(f"{sx}a2mmdone{pi-1}", f"{sx}a2ev{pi*NT-1}"))
                for i in range(NT):
                    ti = pi * NT + i
                    sp = statpair(i % 2)
                    p.dma("scalar", sp,
                          xnf_h[:, :, :, i * 128:(i + 1) * 128]
                          .rearrange("l k p t -> p l k t"),
                          waits=p.need(f"{sx}a2mm{ti-2}"), name=f"{sx}a2st{ti}",
                          slot=f"st{ti%2}")
                    ps = (pA, pB)[i % 2]
                    for kk in range(NK):
                        wmmk = p.need(f"{sx}a2w{pi}_done", f"{sx}a2st{ti}",
                                      f"{sx}a2ev{ti-2}")
                        for idx_, (l, wv_) in enumerate(
                                ((0, whi), (0, wlo), (1, whi))):
                            for g in range(4):
                                p.op("tensor",
                                     lambda e, ps=ps, sp=sp, kk=kk, g=g, l=l,
                                     wv_=wv_, idx_=idx_: e.matmul(
                                        ps[:, g * 512:(g + 1) * 512],
                                        sp[:, l, kk, :],
                                        wv_[:, kk, g * 512:(g + 1) * 512],
                                        start=(kk == 0 and idx_ == 0),
                                        stop=(kk == NK - 1 and idx_ == 2)),
                                     waits=wmmk)
                    p.tmark(f"{sx}a2mm{ti}")
                    wmm = p.need(f"{sx}a2mm{ti}", f"{sx}a2sp{ti-1}")
                    if coff is None:
                        if wn == "wv":
                            p.op("vector", lambda e, ps=ps: e.tensor_copy(
                                    STG[:, 0:2048], ps[:, :]),
                                 waits=wmm, name=f"{sx}a2ev{ti}")
                        else:
                            p.op("scalar", lambda e, ps=ps: e.activation(
                                    STG[:, 0:2048], ps[:, :], AF.Silu),
                                 waits=wmm, name=f"{sx}a2ev{ti}")
                    else:
                        cB = ctab.ap()[:, i, coff, :].rearrange("p f -> p () f") \
                            .broadcast_to([128, H, 64])
                        sB = ctab.ap()[:, i, coff + 1, :].rearrange("p f -> p () f") \
                            .broadcast_to([128, H, 64])
                        rv = hview(STG[:, 6144:8192])
                        qc = hview(STG[:, 0:2048])
                        qs_ = hview(STG[:, 4096:6144])
                        ov = hview(STG[:, 0:2048])
                        p.op("scalar", lambda e, ps=ps: e.activation(
                                STG[:, 6144:8192], ps[:, :], AF.Identity),
                             waits=wmm + p.need(f"{sx}a2ev{ti-1}"),
                             name=f"{sx}a2rv{ti}")
                        wrv = p.need(f"{sx}a2rv{ti}")
                        p.op("vector", lambda e, qc=qc, cB=cB, rv=rv:
                             e.tensor_tensor(qc[:, :, 0:64], rv[:, :, 0:64],
                                             cB, OP.mult), waits=wrv + wmm)
                        p.op("vector", lambda e, qc=qc, cB=cB, rv=rv:
                             e.tensor_tensor(qc[:, :, 64:128], rv[:, :, 64:128],
                                             cB, OP.mult))
                        p.op("gpsimd", lambda e, qs_=qs_, sB=sB, rv=rv:
                             e.tensor_tensor(qs_[:, :, 0:64], rv[:, :, 64:128],
                                             sB, OP.mult),
                             waits=wrv + p.need(f"{sx}a2ev{ti-1}"),
                             name=f"{sx}a2s1_{ti}")
                        p.op("gpsimd", lambda e, qs_=qs_, sB=sB, rv=rv:
                             e.tensor_tensor(qs_[:, :, 64:128], rv[:, :, 0:64],
                                             sB, OP.mult),
                             name=f"{sx}a2s2_{ti}")
                        p.op("vector", lambda e, ov=ov, qc=qc, qs_=qs_:
                             e.tensor_sub(ov[:, :, 0:64], qc[:, :, 0:64],
                                          qs_[:, :, 0:64]),
                             waits=p.need(f"{sx}a2s1_{ti}"))
                        p.op("vector", lambda e, ov=ov, qc=qc, qs_=qs_:
                             e.tensor_add(ov[:, :, 64:128], qc[:, :, 64:128],
                                          qs_[:, :, 64:128]),
                             waits=p.need(f"{sx}a2s2_{ti}"),
                             name=f"{sx}a2ev{ti}")
                    p.dma("gpsimd", dst[i * 128:(i + 1) * 128, :],
                          STG[:, 0:2048],
                          waits=p.need(f"{sx}a2ev{ti}"), name=f"{sx}a2sp{ti}",
                          slot="a2sp")
                p.mark(f"{sx}a2mmdone{pi}", "e_tensor")
            p.barrier()

        if "scan" in stages:
            qb = [arv(0, 2048), arv(2048, 2048)]
            kb = [arv(4096, 2048), arv(6144, 2048)]
            vb = [arv(8192, 2048), arv(10240, 2048)]
            gb = [arv(12288, 2048), arv(14336, 2048)]
            Sst = arv(16384, 2048)
            qf = arv(18432, 2048)
            kf = arv(20480, 2048)
            Asb = arv(22528, 2048)
            k2 = arv(24576, 2048)
            ocmb = arv(26624, 2048)
            ogst = arv(28672, 2048)
            scrap2 = arv(30720, 2048)
            scrap2b = arvb(30720, 2048)
            p.op("vector", lambda e: e.memset(Sst, 0.0), name=f"{sx}scS")
            for i in range(NCH):
                w0 = i * 128
                p.dma("sync", qb[i % 2], sq_h[w0:w0 + 128, :],
                      waits=p.need(f"{sx}scTq{i-2}"), name=f"{sx}sclq{i}",
                      slot=f"sq{i%2}")
                p.dma("sync", kb[i % 2], sk_h[w0:w0 + 128, :],
                      waits=p.need(f"{sx}sck2_{i-2}", f"{sx}scTk{i-2}"),
                      name=f"{sx}sclk{i}", slot=f"sk{i%2}")
                p.dma("sync", vb[i % 2], sv_h[w0:w0 + 128, :],
                      waits=p.need(f"{sx}scDel{i-2}"), name=f"{sx}sclv{i}",
                      slot=f"sv{i%2}")
                p.dma("scalar", gb[i % 2], sg_h[w0:w0 + 128, :],
                      waits=p.need(f"{sx}scog{i-2}"), name=f"{sx}sclg{i}",
                      slot=f"sg{i%2}")
                q_tm, k_tm = qb[i % 2], kb[i % 2]
                v_tm, sg_tm = vb[i % 2], gb[i % 2]
                for (src, ps, dst_, nm, wx) in (
                        (q_tm, pA, qf, "q", f"{sx}scSup{i-1}"),
                        (k_tm, pB, kf, "k", f"{sx}scogev{i-1}")):
                    for hh in range(H):
                        p.op("tensor", lambda e, src=src, ps=ps, hh=hh:
                             e.transpose(ps[:, hh * 128:(hh + 1) * 128],
                                         src[:, hh * 128:(hh + 1) * 128],
                                         ident.ap()),
                             waits=p.need(f"{sx}scl{nm}{i}", wx,
                                          f"{sx}scT{nm}ev{i-1}"))
                    p.tmark(f"{sx}scT{nm}{i}")
                    p.op("vector", lambda e, ps=ps, dst_=dst_: e.tensor_copy(
                            dst_, ps[:, :]),
                         waits=p.need(f"{sx}scT{nm}{i}"),
                         name=f"{sx}scT{nm}ev{i}")
                for hh in range(H):
                    p.op("tensor", lambda e, hh=hh: e.matmul(
                            pA[:, hh * 128:(hh + 1) * 128],
                            kf[:, hh * 128:(hh + 1) * 128],
                            qf[:, hh * 128:(hh + 1) * 128],
                            start=(hh % 4 == 0), stop=(hh % 4 == 3)),
                         waits=p.need(f"{sx}scTqev{i}", f"{sx}scTkev{i}"))
                p.tmark(f"{sx}scA{i}")
                p.op("vector", lambda e: e.tensor_tensor(
                        hview(Asb), hview(pA[:, :]), dmat.ap(), OP.mult),
                     waits=p.need(f"{sx}scA{i}"), name=f"{sx}scAm{i}")
                for hh in range(H):
                    p.op("tensor", lambda e, hh=hh, v_tm=v_tm: e.matmul(
                            pB[:, hh * 128:(hh + 1) * 128],
                            Asb[:, hh * 128:(hh + 1) * 128],
                            v_tm[:, hh * 128:(hh + 1) * 128],
                            start=(hh % 4 == 0), stop=(hh % 4 == 3)),
                         waits=p.need(f"{sx}scAm{i}", f"{sx}sclv{i}",
                                      f"{sx}scTkev{i}"))
                p.tmark(f"{sx}scI{i}")
                for hh in range(H):
                    p.op("tensor", lambda e, hh=hh: e.matmul(
                            pA[:, hh * 128:(hh + 1) * 128],
                            qf[:, hh * 128:(hh + 1) * 128],
                            Sst[:, hh * 128:(hh + 1) * 128],
                            start=(hh % 4 == 0), stop=(hh % 4 == 3)),
                         waits=p.need(f"{sx}scAm{i}",
                                      f"{sx}scS" if i == 0 else f"{sx}scSup{i-1}"))
                p.tmark(f"{sx}scX{i}")
                p.op("vector", lambda e: e.tensor_tensor(
                        hview(ocmb), hview(pA[:, :]), crossB, OP.mult),
                     waits=p.need(f"{sx}scX{i}"), name=f"{sx}scOc1{i}")
                p.op("vector", lambda e: e.tensor_add(ocmb, ocmb, pB[:, :]),
                     waits=p.need(f"{sx}scI{i}"), name=f"{sx}scOc{i}")
                p.op("gpsimd", lambda e, k_tm=k_tm: e.tensor_tensor(
                        hview(k2), hview(k_tm), kdecB, OP.mult),
                     waits=p.need(f"{sx}sclk{i}", f"{sx}scSup{i-1}"),
                     name=f"{sx}sck2_{i}")
                for hh in range(H):
                    p.op("tensor", lambda e, hh=hh, v_tm=v_tm: e.matmul(
                            pA[:, hh * 128:(hh + 1) * 128],
                            k2[:, hh * 128:(hh + 1) * 128],
                            v_tm[:, hh * 128:(hh + 1) * 128],
                            start=(hh % 4 == 0), stop=(hh % 4 == 3)),
                         waits=p.need(f"{sx}sck2_{i}", f"{sx}scOc1{i}"))
                p.tmark(f"{sx}scDel{i}")
                p.op("vector", lambda e: e.tensor_tensor(
                        hview(Sst), hview(Sst), gCB, OP.mult),
                     waits=p.need(f"{sx}scX{i}", f"{sx}scDel{i}"))
                p.op("vector", lambda e: e.tensor_add(Sst, Sst, pA[:, :]),
                     name=f"{sx}scSup{i}")
                c = 48
                st16 = ST[:, c:c + 16]
                sa = ST[:, c + 16:c + 32]
                sh1 = ST[:, c + 32:c + 48]
                sh2v = ST[:, c + 48:c + 64]
                p.op("vector", lambda e: e.tensor_mul(scrap2, ocmb, ocmb),
                     waits=p.need(f"{sx}scOc{i}", f"{sx}scsp0_{i-1}",
                                  f"{sx}scsp1_{i-1}"))
                p.op("vector", lambda e, st16=st16: e.tensor_reduce(
                        st16, hview(scrap2), AX.X, OP.add),
                     waits=p.need(f"{sx}scrms{i-1}"))
                p.op("vector", lambda e, sa=sa, st16=st16: e.tensor_scalar(
                        sa, st16, 1.0 / DV, None, OP.mult))
                p.op("vector", lambda e, sa=sa: e.tensor_scalar(
                        sa, sa, 1e-5, None, OP.add), name=f"{sx}scva{i}")
                sqrt_heron(p, st16, sa, sh1, sh2v, waits=p.need(f"{sx}scva{i}"),
                           name=f"{sx}scsq{i}")
                p.op("vector", lambda e, st16=st16: e.reciprocal(st16, st16),
                     name=f"{sx}scrms{i}")
                rstdB = st16.rearrange("p h -> p h ()").broadcast_to([128, H, 128])
                p.op("vector", lambda e, rstdB=rstdB: e.tensor_tensor(
                        hview(ogst), hview(ocmb), rstdB, OP.mult),
                     waits=p.need(f"{sx}scogT{i-1}"))
                p.op("vector", lambda e, sg_tm=sg_tm: e.tensor_mul(
                        ogst, ogst, sg_tm),
                     waits=p.need(f"{sx}sclg{i}"), name=f"{sx}scogv{i}")
                p.alias(f"{sx}scog{i}", f"{sx}scogv{i}")
                p.op("vector", lambda e: e.tensor_copy(scrap2b[:, 0:2048], ogst),
                     waits=p.need(f"{sx}scogv{i}"))
                p.op("vector", lambda e: e.tensor_tensor(
                        scrap2b[:, 2048:4096], ogst, scrap2b[:, 0:2048],
                        OP.subtract), name=f"{sx}scogp{i}")
                for l in range(2):
                    for kk in range(16):
                        p.op("tensor", lambda e, l=l, kk=kk: e.transpose(
                                pBb[:, l * 2048 + kk * 128:
                                    l * 2048 + (kk + 1) * 128],
                                scrap2b[:, l * 2048 + kk * 128:
                                        l * 2048 + (kk + 1) * 128],
                                identb.ap()),
                             waits=p.need(f"{sx}scogp{i}", f"{sx}scOc{i}",
                                          f"{sx}scogev{i-1}"))
                p.tmark(f"{sx}scogT{i}")
                p.op("vector", lambda e: e.tensor_copy(STGb[:, 0:4096],
                                                       pBb[:, 0:4096]),
                     waits=p.need(f"{sx}scogT{i}", f"{sx}scsp0_{i-1}",
                                  f"{sx}scsp1_{i-1}"),
                     name=f"{sx}scogev{i}")
                for l in range(2):
                    p.dma("gpsimd",
                          sogf_h[l, :, :, w0:w0 + 128]
                          .rearrange("k p t -> p k t"),
                          STGb[:, l * 2048:(l + 1) * 2048]
                          .rearrange("p (k t) -> p k t", t=128),
                          waits=p.need(f"{sx}scogev{i}"),
                          name=f"{sx}scsp{l}_{i}", slot=f"scsp{l}")
            p.barrier()

        if "a4" in stages:
            whi, wlo = build_wpair(f"{sx}a4w_", w_h["wo"], scale_ap=gnw.ap())
            for i in range(NT):
                w0 = i * 128
                sp = statpair(i % 2)
                p.dma("scalar", sp,
                      sogf_h[:, :, :, w0:w0 + 128]
                      .rearrange("l k p t -> p l k t"),
                      waits=p.need(f"{sx}a4mm{i-2}"), name=f"{sx}a4st{i}",
                      slot=f"st{i%2}")
                ps = (pA, pB)[i % 2]
                for kk in range(NK):
                    wmmk = p.need(f"{sx}a4w_done", f"{sx}a4st{i}",
                                  f"{sx}a4ev{i-2}")
                    for idx_, (l, wv_) in enumerate(((0, whi), (0, wlo), (1, whi))):
                        for g in range(4):
                            p.op("tensor",
                                 lambda e, ps=ps, sp=sp, kk=kk, g=g, l=l,
                                 wv_=wv_, idx_=idx_: e.matmul(
                                    ps[:, g * 512:(g + 1) * 512],
                                    sp[:, l, kk, :],
                                    wv_[:, kk, g * 512:(g + 1) * 512],
                                    start=(kk == 0 and idx_ == 0),
                                    stop=(kk == NK - 1 and idx_ == 2)),
                                 waits=wmmk)
                p.tmark(f"{sx}a4mm{i}")
                p.op("vector", lambda e, ps=ps: e.tensor_copy(
                        STG[:, 0:2048], ps[:, :]),
                     waits=p.need(f"{sx}a4mm{i}", f"{sx}a4sp{i-1}"),
                     name=f"{sx}a4ev{i}")
                p.dma("gpsimd", soutp_h[w0:w0 + 128, :], STG[:, 0:2048],
                      waits=p.need(f"{sx}a4ev{i}"), name=f"{sx}a4sp{i}",
                      slot="a2sp")
            p.barrier()

        if "b1" in stages:
            abuf = [arv(0, 2048), arv(2048, 2048)]
            xbuf = [arv(4096, 2048), arv(6144, 2048)]
            x2b = [arv(8192, 2048), arv(10240, 2048)]
            tb = [arv(12288, 2048), arv(14336, 2048)]
            xqb2 = [arv(16384, 2048), arv(18432, 2048)]
            xqpb = [arvb(20480, 2048), arvb(22528, 2048)]
            scrapb1 = arv(24576, 2048)
            for i in range(NT):
                w0 = i * 128
                ab, xb, x2v = abuf[i % 2], xbuf[i % 2], x2b[i % 2]
                tv, xqv, xqp = tb[i % 2], xqb2[i % 2], xqpb[i % 2]
                c = 32 + 24 * (i % 2)
                ssum, ssq = ST[:, c:c+1], ST[:, c+1:c+2]
                sm, sva, srstd = ST[:, c+2:c+3], ST[:, c+3:c+4], ST[:, c+4:c+5]
                st1, st2 = ST[:, c+5:c+6], ST[:, c+6:c+7]
                sm2, sv2 = ST[:, c+7:c+8], ST[:, c+8:c+9]
                sa2, sb2, sc2 = ST[:, c+9:c+10], ST[:, c+10:c+11], ST[:, c+11:c+12]
                smx, sc1, sg2 = ST[:, c+12:c+13], ST[:, c+13:c+14], ST[:, c+14:c+15]
                p.dma("sync", ab, soutp_h[w0:w0 + 128, :],
                      waits=p.need(f"{sx}b1ab{i-2}"), name=f"{sx}b1la{i}",
                      slot=f"x{i%2}")
                p.dma("sync", xb, xseq[w0:w0 + 128, :],
                      waits=p.need(f"{sx}a4x2{i-2}"), name=f"{sx}b1lx{i}",
                      slot=f"xb{i%2}")
                wl = p.need(f"{sx}b1la{i}")
                ln_stats(p, ab, ssum, ssq, sm, sva, srstd, st1, st2, 1e-5,
                         wl, f"{sx}b1s{i}", scrapb1)
                p.op("vector", lambda e, ab=ab, sm=sm, srstd=srstd, tv=tv:
                     e.tensor_scalar(tv, ab, sm, srstd, OP.subtract, OP.mult),
                     name=f"{sx}b1ab{i}")
                p.op("vector", lambda e, x2v=x2v, xb=xb, tv=tv: e.tensor_add(
                        x2v, tv, xb),
                     waits=p.need(f"{sx}b1lx{i}", f"{sx}a4spx2{i-2}"),
                     name=f"{sx}a4x2{i}")
                p.dma("gpsimd", sx2_h[w0:w0 + 128, :], x2v,
                      waits=p.need(f"{sx}a4x2{i}"), name=f"{sx}a4spx2{i}",
                      slot=f"x2sp{i%2}")
                wx2 = p.need(f"{sx}a4x2{i}")
                p.op("vector", lambda e, x2v=x2v, ssum=ssum: e.reduce_sum(
                        ssum, x2v, AX.X), waits=wx2)
                p.op("scalar", lambda e, x2v=x2v, ssq=ssq: e.activation(
                        scrapb1, x2v, AF.Square, accum_out=ssq),
                     waits=wx2 + p.need(f"{sx}b1s{i}sq"), name=f"{sx}b1q{i}sq")
                p.op("vector", lambda e, sm2=sm2, ssum=ssum: e.tensor_scalar(
                        sm2, ssum, 1.0 / D, None, OP.mult))
                p.op("vector", lambda e, sv2=sv2, sm2=sm2: e.tensor_mul(
                        sv2, sm2, sm2), waits=p.need(f"{sx}b1q{i}sq"))
                p.op("vector", lambda e, sv2=sv2, ssq=ssq: e.scalar_tensor_tensor(
                        sv2, ssq, 1.0 / D, sv2, OP.mult, OP.subtract))
                p.op("vector", lambda e, sa2=sa2, sv2=sv2: e.tensor_scalar(
                        sa2, sv2, 1e-5, None, OP.add), name=f"{sx}b1a{i}")
                p.op("vector", lambda e, sb2=sb2, sa2=sa2, sv2=sv2:
                     e.scalar_tensor_tensor(sb2, sa2, 1e-6, sv2, OP.mult, OP.add),
                     waits=p.need(f"{sx}b1a{i}"), name=f"{sx}b1b{i}")
                sqrt_heron(p, sc2, sb2, st1, st2, waits=p.need(f"{sx}b1b{i}"),
                           name=f"{sx}b1h{i}")
                p.op("vector", lambda e, sc2=sc2: e.reciprocal(sc2, sc2),
                     name=f"{sx}b1rc{i}")
                p.op("vector", lambda e, x2v=x2v, sm2=sm2, tv=tv: e.tensor_scalar(
                        tv, x2v, sm2, None, OP.subtract), name=f"{sx}b1t{i}")
                p.op("vector", lambda e, smx=smx, tv=tv: e.reduce_max(
                        smx, tv, AX.X, apply_absolute_value=True),
                     name=f"{sx}b1mx{i}")
                p.op("vector", lambda e, sc1=sc1, smx=smx: e.reciprocal(sc1, smx))
                p.op("vector", lambda e, sc1=sc1: e.tensor_scalar(
                        sc1, sc1, 128.0, None, OP.mult))
                p.op("vector", lambda e, sg2=sg2, sc2=sc2, smx=smx: e.tensor_mul(
                        sg2, sc2, smx),
                     waits=p.need(f"{sx}b1rc{i}", f"{sx}b1mx{i}"))
                p.op("vector", lambda e, sg2=sg2: e.tensor_scalar(
                        sg2, sg2, 1.0 / 128.0, None, OP.mult), name=f"{sx}b1g{i}")
                p.op("vector", lambda e, tv=tv, sc1=sc1: e.tensor_scalar(
                        tv, tv, sc1, CLIP, OP.mult, OP.min), name=f"{sx}b1u{i}")
                p.op("vector", lambda e, tv=tv, xqv=xqv, sg2=sg2: e.tensor_scalar(
                        xqv, tv, -CLIP, sg2, OP.max, OP.mult),
                     waits=p.need(f"{sx}b1g{i}", f"{sx}b1u{i}"),
                     name=f"{sx}b1xq{i}")
                p.op("vector", lambda e, xqv=xqv, xqp=xqp: e.tensor_copy(
                        xqp[:, 0:2048], xqv),
                     waits=p.need(f"{sx}b1tg{i-2}"))
                p.op("vector", lambda e, xqv=xqv, xqp=xqp: e.tensor_tensor(
                        xqp[:, 2048:4096], xqv, xqp[:, 0:2048], OP.subtract),
                     name=f"{sx}b1pr{i}")
                ps = (pAb, pBb)[i % 2]
                for l in range(2):
                    for kk in range(16):
                        p.op("tensor", lambda e, ps=ps, l=l, kk=kk, xqp=xqp:
                             e.transpose(
                                ps[:, l * 2048 + kk * 128:
                                   l * 2048 + (kk + 1) * 128],
                                xqp[:, l * 2048 + kk * 128:
                                    l * 2048 + (kk + 1) * 128],
                                identb.ap()),
                             waits=p.need(f"{sx}b1pr{i}", f"{sx}b1tev{i-2}"))
                p.tmark(f"{sx}b1tg{i}")
                p.op("vector", lambda e, ps=ps: e.tensor_copy(
                        STGb[:, 0:4096], ps[:, 0:4096]),
                     waits=p.need(f"{sx}b1tg{i}", f"{sx}b1sp0_{i-1}",
                                  f"{sx}b1sp1_{i-1}"),
                     name=f"{sx}b1tev{i}")
                for l in range(2):
                    p.dma("gpsimd",
                          sx2hf_h[l, :, :, w0:w0 + 128]
                          .rearrange("k p t -> p k t"),
                          STGb[:, l * 2048:(l + 1) * 2048]
                          .rearrange("p (k t) -> p k t", t=128),
                          waits=p.need(f"{sx}b1tev{i}"),
                          name=f"{sx}b1sp{l}_{i}", slot=f"a1sp{l}")
            p.barrier()

        if "b2a" in stages:
            tq = arvb(0, 16384).rearrange("p (k f) -> p k f", k=NK)
            for qq in range(4):
                p.dma("sync", tq,
                      tfc_h[:, :, qq * 2048:(qq + 1) * 2048]
                      .rearrange("k p f -> p k f"),
                      waits=p.need(f"{sx}b2mm{qq*NT-1}"), name=f"{sx}b2t{qq}",
                      slot="wld")
                for i in range(NT):
                    ti = qq * NT + i
                    sp = statpair(ti % 2)
                    p.dma("scalar", sp,
                          sx2hf_h[:, :, :, i * 128:(i + 1) * 128]
                          .rearrange("l k p t -> p l k t"),
                          waits=p.need(f"{sx}b2mm{ti-2}"), name=f"{sx}b2st{ti}",
                          slot=f"st{ti%2}")
                    ps = (pA, pB)[ti % 2]
                    for kk in range(NK):
                        wmmk = p.need(f"{sx}b2t{qq}", f"{sx}b2st{ti}",
                                      f"{sx}b2ev{ti-2}")
                        for l in range(2):
                            for g in range(4):
                                p.op("tensor",
                                     lambda e, ps=ps, sp=sp, kk=kk, g=g, l=l,
                                     tq=tq: e.matmul(
                                        ps[:, g * 512:(g + 1) * 512],
                                        sp[:, l, kk, :],
                                        tq[:, kk, g * 512:(g + 1) * 512],
                                        start=(kk == 0 and l == 0),
                                        stop=(kk == NK - 1 and l == 1)),
                                     waits=wmmk)
                    p.tmark(f"{sx}b2mm{ti}")
                    p.op("scalar", lambda e, ps=ps: e.activation(
                            STG[:, 0:2048], ps[:, :], AF.Gelu,
                            scale=gv.ap()[:, 0:1]),
                         waits=p.need(f"{sx}b2mm{ti}", f"{sx}b2sp{ti-1}"),
                         name=f"{sx}b2ev{ti}")
                    p.dma("gpsimd",
                          sh2_h[i * 128:(i + 1) * 128,
                                qq * 2048:(qq + 1) * 2048],
                          STG[:, 0:2048],
                          waits=p.need(f"{sx}b2ev{ti}"), name=f"{sx}b2sp{ti}",
                          slot="b2sp")
            p.barrier()

        if "b2b" in stages:
            hbuf = [arv(0, 8192), arv(8192, 8192)]
            xqv = arv(16384, 8192)
            xqp = arvb(24576, 8192)
            for i in range(NT):
                w0 = i * 128
                hb = hbuf[i % 2]
                c = 96 + 16 * (i % 2)
                ssum, ssq = ST[:, c:c+1], ST[:, c+1:c+2]
                sm2, sv2 = ST[:, c+2:c+3], ST[:, c+3:c+4]
                sb2, sc2 = ST[:, c+4:c+5], ST[:, c+5:c+6]
                smx, sc1, sg2 = ST[:, c+6:c+7], ST[:, c+7:c+8], ST[:, c+8:c+9]
                st1, st2 = ST[:, c+9:c+10], ST[:, c+10:c+11]
                p.dma("sync", hb, sh2_h[w0:w0 + 128, :],
                      waits=p.need(f"{sx}b2bt_{i-2}"), name=f"{sx}b2bl{i}",
                      slot=f"x{i%2}")
                wl = p.need(f"{sx}b2bl{i}")
                p.op("vector", lambda e, hb=hb, ssum=ssum: e.reduce_sum(
                        ssum, hb, AX.X), waits=wl + p.need(f"{sx}b2bpr{i-1}"))
                p.op("scalar", lambda e, hb=hb, ssq=ssq: e.activation(
                        STG, hb, AF.Square, accum_out=ssq),
                     waits=wl + p.need(f"{sx}b2bsq{i-1}", f"{sx}b2bev{i-1}",
                                       f"{sx}b2bsp0_{i-1}", f"{sx}b2bsp1_{i-1}"),
                     name=f"{sx}b2bsq{i}")
                p.op("vector", lambda e, sm2=sm2, ssum=ssum: e.tensor_scalar(
                        sm2, ssum, 1.0 / FF, None, OP.mult))
                p.op("vector", lambda e, sv2=sv2, sm2=sm2: e.tensor_mul(
                        sv2, sm2, sm2), waits=p.need(f"{sx}b2bsq{i}"))
                p.op("vector", lambda e, sv2=sv2, ssq=ssq: e.scalar_tensor_tensor(
                        sv2, ssq, 1.0 / FF, sv2, OP.mult, OP.subtract),
                     name=f"{sx}b2bv{i}")
                p.op("vector", lambda e, sb2=sb2, sv2=sv2: e.tensor_scalar(
                        sb2, sv2, EPS_Q, None, OP.add), name=f"{sx}b2ba{i}")
                sqrt_heron(p, sc2, sb2, st1, st2, waits=p.need(f"{sx}b2ba{i}"),
                           name=f"{sx}b2bh{i}")
                p.op("vector", lambda e, sc2=sc2: e.reciprocal(sc2, sc2),
                     name=f"{sx}b2brc{i}")
                p.op("vector", lambda e, hb=hb, sm2=sm2: e.tensor_scalar(
                        xqv, hb, sm2, None, OP.subtract), name=f"{sx}b2bt_{i}")
                p.op("vector", lambda e, smx=smx: e.reduce_max(
                        smx, xqv, AX.X, apply_absolute_value=True),
                     name=f"{sx}b2bmx{i}")
                p.op("vector", lambda e, sc1=sc1, smx=smx: e.reciprocal(sc1, smx))
                p.op("vector", lambda e, sc1=sc1: e.tensor_scalar(
                        sc1, sc1, 128.0, None, OP.mult))
                p.op("vector", lambda e, sg2=sg2, sc2=sc2, smx=smx: e.tensor_mul(
                        sg2, sc2, smx),
                     waits=p.need(f"{sx}b2brc{i}", f"{sx}b2bmx{i}"))
                p.op("vector", lambda e, sg2=sg2: e.tensor_scalar(
                        sg2, sg2, 1.0 / 128.0, None, OP.mult), name=f"{sx}b2bg{i}")
                p.op("vector", lambda e, sc1=sc1: e.tensor_scalar(
                        xqv, xqv, sc1, CLIP, OP.mult, OP.min))
                p.op("vector", lambda e, sg2=sg2: e.tensor_scalar(
                        xqv, xqv, -CLIP, sg2, OP.max, OP.mult),
                     waits=p.need(f"{sx}b2bg{i}"), name=f"{sx}b2bxq{i}")
                p.op("vector", lambda e: e.tensor_copy(xqp[:, 0:8192], xqv),
                     waits=p.need(f"{sx}b2btg{4*(i-1)+3}"))
                p.op("vector", lambda e: e.tensor_tensor(
                        xqp[:, 8192:16384], xqv, xqp[:, 0:8192], OP.subtract),
                     name=f"{sx}b2bpr{i}")
                for g in range(4):
                    gi = 4 * i + g
                    ps = (pAb, pBb)[gi % 2]
                    for l in range(2):
                        for j in range(16):
                            kk = g * 16 + j
                            p.op("tensor",
                                 lambda e, ps=ps, l=l, j=j, kk=kk: e.transpose(
                                    ps[:, l * 2048 + j * 128:
                                       l * 2048 + (j + 1) * 128],
                                    xqp[:, l * 8192 + kk * 128:
                                        l * 8192 + (kk + 1) * 128],
                                    identb.ap()),
                                 waits=p.need(f"{sx}b2bpr{i}",
                                              f"{sx}b2bevg{gi-2}"))
                    p.tmark(f"{sx}b2btg{gi}")
                    p.op("vector", lambda e, ps=ps, g=g: e.tensor_copy(
                            STGb.rearrange("p (l g f) -> p l g f", l=2, g=4)
                            [:, :, g, :],
                            ps[:, 0:4096].rearrange("p (l f) -> p l f", l=2)),
                         waits=p.need(f"{sx}b2btg{gi}",
                                      f"{sx}b2bsp0_{i-1}", f"{sx}b2bsp1_{i-1}"),
                         name=f"{sx}b2bevg{gi}")
                p.alias(f"{sx}b2bev{i}", f"{sx}b2bevg{4*i+3}")
                for l in range(2):
                    p.dma("gpsimd",
                          sx2h2f_h[l, :, :, w0:w0 + 128]
                          .rearrange("k p t -> p k t"),
                          STGb[:, l * 8192:(l + 1) * 8192]
                          .rearrange("p (k t) -> p k t", t=128),
                          waits=p.need(f"{sx}b2bev{i}"),
                          name=f"{sx}b2bsp{l}_{i}", slot=f"a1sp{l}")
            p.barrier()

        if "b3" in stages:
            tp4 = arvb(0, 16384).rearrange("p (k f) -> p k f", k=NK2)
            spb = [arvb(16384, 8192).rearrange("p (l k t) -> p l k t", l=2, t=128),
                   arvb(24576, 8192).rearrange("p (l k t) -> p l k t", l=2, t=128)]
            for qq in range(4):
                p.dma("sync", tp4,
                      tpj_h[:, :, qq * 512:(qq + 1) * 512]
                      .rearrange("k p f -> p k f"),
                      waits=p.need(f"{sx}b3mm{qq*NT-1}"), name=f"{sx}b3t{qq}",
                      slot="wld")
                for i in range(NT):
                    ti = qq * NT + i
                    sp = spb[ti % 2]
                    p.dma("scalar", sp,
                          sx2h2f_h[:, :, :, i * 128:(i + 1) * 128]
                          .rearrange("l k p t -> p l k t"),
                          waits=p.need(f"{sx}b3mm{ti-2}"), name=f"{sx}b3st{ti}",
                          slot=f"st{ti%2}")
                    ps = (pA, pB)[ti % 2]
                    for kk in range(NK2):
                        wmmk = p.need(f"{sx}b3t{qq}", f"{sx}b3st{ti}",
                                      f"{sx}b3ev{ti-2}")
                        for l in range(2):
                            p.op("tensor",
                                 lambda e, ps=ps, sp=sp, kk=kk, l=l: e.matmul(
                                    ps[:, 0:512], sp[:, l, kk, :],
                                    tp4[:, kk, :],
                                    start=(kk == 0 and l == 0),
                                    stop=(kk == NK2 - 1 and l == 1)),
                                 waits=wmmk)
                    p.tmark(f"{sx}b3mm{ti}")
                    p.op("vector", lambda e, ps=ps: e.tensor_scalar(
                            STG[:, 0:512], ps[:, 0:512], gv.ap()[:, 1:2], None,
                            OP.mult),
                         waits=p.need(f"{sx}b3mm{ti}", f"{sx}b3sp{ti-1}"),
                         name=f"{sx}b3ev{ti}")
                    p.dma("gpsimd",
                          soutp_h[i * 128:(i + 1) * 128,
                                  qq * 512:(qq + 1) * 512],
                          STG[:, 0:512],
                          waits=p.need(f"{sx}b3ev{ti}"), name=f"{sx}b3sp{ti}",
                          slot="b3sp")
            p.barrier()

        if "b3b" in stages:
            obuf = [arv(0, 2048), arv(2048, 2048)]
            x2b = [arv(4096, 2048), arv(6144, 2048)]
            scrap3 = arv(8192, 2048)
            for i in range(NT):
                w0 = i * 128
                ob, x2v = obuf[i % 2], x2b[i % 2]
                c = 128 + 12 * (i % 2)
                ssum, ssq = ST[:, c:c+1], ST[:, c+1:c+2]
                sm, sva, srstd = ST[:, c+2:c+3], ST[:, c+3:c+4], ST[:, c+4:c+5]
                st1, st2 = ST[:, c+5:c+6], ST[:, c+6:c+7]
                p.dma("sync", ob, soutp_h[w0:w0 + 128, :],
                      waits=p.need(f"{sx}b3bo{i-2}"), name=f"{sx}b3bl{i}",
                      slot=f"x{i%2}")
                p.dma("sync", x2v, sx2_h[w0:w0 + 128, :],
                      waits=p.need(f"{sx}b3bo{i-2}"), name=f"{sx}b3blx{i}",
                      slot=f"xb{i%2}")
                wl = p.need(f"{sx}b3bl{i}")
                ln_stats(p, ob, ssum, ssq, sm, sva, srstd, st1, st2, 1e-5,
                         wl, f"{sx}b3b{i}", scrap3)
                p.op("vector", lambda e, ob=ob, sm=sm, srstd=srstd:
                     e.tensor_scalar(STG[:, 0:2048], ob, sm, srstd,
                                     OP.subtract, OP.mult),
                     waits=p.need(f"{sx}b3bo{i-1}"))
                p.op("vector", lambda e, x2v=x2v: e.tensor_add(
                        STG[:, 0:2048], STG[:, 0:2048], x2v),
                     waits=p.need(f"{sx}b3blx{i}"), name=f"{sx}b3bst{i}")
                p.dma("gpsimd", oseq[w0:w0 + 128, :], STG[:, 0:2048],
                      waits=p.need(f"{sx}b3bst{i}"), name=f"{sx}b3bo{i}",
                      slot="outs")
            p.barrier()

    p.emit("gpsimd", lambda e: e.nop(),
           [(s, p.v(s)) for s in p.cnt if s.startswith("d_")], None)

    return nc, p, stack


def build_and_realize(cfg):
    nc, p, stack = build_program(cfg)
    with stack:
        p.realize(nc, stack)
    return nc, len(p.ops)


def make_in_map(inputs, core, n_seq=2):
    consts = build_consts()
    x = np.ascontiguousarray(inputs["L0_attn_x"][2 * core:2 * core + n_seq]
                             .reshape(n_seq * S, D))
    m = dict(x=x,
             wq=inputs["L0_attn_Wq"], wk=inputs["L0_attn_Wk"],
             wv=inputs["L0_attn_Wv"], wg=inputs["L0_attn_Wg"],
             wo=inputs["L0_attn_Wo"],
             wfct=np.ascontiguousarray(inputs["L0_ffn_w_fc"].T),
             wpjt=np.ascontiguousarray(inputs["L0_ffn_w_proj"].T),
             gnw=np.ascontiguousarray(inputs["L0_attn_g_norm_w"].reshape(128, 1)),
             **consts)
    return m


def kernel(**inputs):
    inputs = {k: np.asarray(v, dtype=np.float32) for k, v in inputs.items()}
    nc, nops = build_and_realize({})
    in_maps = [make_in_map(inputs, c) for c in range(8)]
    res = run_bass_kernel_spmd(nc, in_maps, core_ids=list(range(8)))
    out = np.concatenate([r["out"].reshape(2, S, D) for r in res.results], axis=0)
    return out


# version 04
# speedup vs baseline: 1.3238x; 1.0033x over previous
import numpy as np
from contextlib import ExitStack

import concourse.bass as bass
import concourse.mybir as mybir
from concourse.bass_utils import run_bass_kernel_spmd

F32 = mybir.dt.float32
BF16 = mybir.dt.bfloat16
AF = mybir.ActivationFunctionType
OP = mybir.AluOpType
AX = mybir.AxisListType

H, DK, DV, C = 16, 128, 128, 128
D, FF, S = 2048, 8192, 2048
NCH = S // C
NT = S // 128
NK = D // 128
NK2 = FF // 128
CLIP = 128.0 - 1e-6
EPS_Q = 1e-6

ENGINES = ("sync", "tensor", "vector", "scalar", "gpsimd")


def build_consts():
    gamma = 1.0 - np.exp2(-5.0 - np.arange(H, dtype=np.float64))
    inv = 1.0 / 10000.0 ** (np.arange(0, DK, 2, dtype=np.float64) / DK)
    ang = np.arange(S, dtype=np.float64)[:, None] * inv[None, :]
    cos, sin = np.cos(ang), np.sin(ang)
    qs = DK ** -0.5
    ropetab = np.stack([cos * qs, sin * qs, cos, sin], axis=1).astype(np.float32)
    idx = np.arange(C, dtype=np.float64)
    rel = idx[None, :] - idx[:, None]
    dmatT = np.where(rel[:, None, :] >= 0,
                     gamma[None, :, None] ** rel[:, None, :], 0.0)
    dmatT = np.ascontiguousarray(dmatT).astype(np.float32)
    crossdec = (gamma[None, :] ** (idx[:, None] + 1.0)).astype(np.float32)
    kdec = (gamma[None, :] ** (C - 1.0 - idx[:, None])).astype(np.float32)
    gC = np.broadcast_to(gamma ** C, (128, H)).astype(np.float32).copy()
    ident = np.eye(128, dtype=np.float32)
    ones = np.ones((128, 1), dtype=np.float32)
    ones_row = np.ones((1, 128), dtype=np.float32)
    return dict(ropetab=ropetab, dmatT=dmatT, crossdec=crossdec, kdec=kdec,
                gC=gC, ident=ident, ones=ones, ones_row=ones_row)


class P:

    def __init__(self):
        self.ops = []
        self.cnt = {}
        self.res = {}
        self.bar = []

    def v(self, sem):
        return self.cnt.get(sem, 0)

    def emit(self, eng, fn, waits=(), inc=None):
        ws = dict(self.bar)
        for (s, v) in waits:
            if v > ws.get(s, 0):
                ws[s] = v
        self.ops.append((eng, fn, tuple((s, v) for s, v in ws.items() if v > 0), inc))
        if inc is not None:
            self.cnt[inc[0]] = self.cnt.get(inc[0], 0) + inc[1]

    def mark(self, name, sem):
        self.res[name] = (sem, self.v(sem))

    def need(self, *names):
        return [self.res[n] for n in names if n in self.res]

    def dma(self, q, out, in_, waits=(), name=None, slot=None):
        sem = "d_" + (slot or (name or "anon"))
        self.emit(q, lambda e, o=out, i=in_: e.dma_start(out=o, in_=i),
                  waits, (sem, 16))
        if name:
            self.mark(name, sem)

    def op(self, eng, fn, waits=(), name=None):
        sem = "e_" + eng
        inc = (sem, 1)
        if eng in ("vector", "scalar", "gpsimd"):
            waits = list(waits) + [(sem, self.v(sem))]
        elif name is None:
            inc = None
        self.emit(eng, fn, waits, inc)
        if name:
            self.mark(name, sem)

    def alias(self, name, other):
        if other in self.res:
            self.res[name] = self.res[other]

    def tmark(self, name):
        for j in range(len(self.ops) - 1, -1, -1):
            eng, fn, ws, inc = self.ops[j]
            if eng == "tensor":
                assert inc is None, f"last tensor op already incs ({name})"
                self.ops[j] = (eng, fn, ws, ("e_tensor", 1))
                self.cnt["e_tensor"] = self.cnt.get("e_tensor", 0) + 1
                self.mark(name, "e_tensor")
                return
        raise AssertionError(f"tmark({name}): no tensor op found")

    def barrier(self):
        self.bar = [(s, self.v(s)) for s in self.cnt]

    def realize(self, nc, stack):
        sems = {s: stack.enter_context(nc.semaphore(s)) for s in self.cnt}
        for (_, _, ws, _) in self.ops:
            for (s, v) in ws:
                assert v <= self.cnt[s], f"wait {s}>={v} max {self.cnt[s]}"
        block = stack.enter_context(nc.Block())
        ops = self.ops

        def mk(eng_name):
            def emitter(e):
                waited = {}
                for (eng2, fn, ws, inc) in ops:
                    if eng2 != eng_name:
                        continue
                    for (s, v) in ws:
                        if waited.get(s, 0) < v:
                            e.wait_ge(sems[s], v)
                            waited[s] = v
                    ins = fn(e)
                    if inc is not None:
                        ins.then_inc(sems[inc[0]], inc[1])
            return emitter

        for eng in ENGINES:
            getattr(block, eng)(mk(eng))
        return sems


def sqrt_heron(p, dst, a, t1, t2, waits=(), name=None):
    p.op("scalar", lambda e: e.activation(dst, a, AF.Sqrt), waits,
         name=name and name + "_s")
    p.op("vector", lambda e: e.reciprocal(t1, dst),
         waits=p.need(name + "_s") if name else ())
    p.op("vector", lambda e: e.tensor_mul(t2, a, t1))
    p.op("vector", lambda e: e.tensor_add(dst, dst, t2))
    p.op("vector", lambda e: e.tensor_scalar(dst, dst, 0.5, None, OP.mult),
         name=name)


def ln_stats(p, src, ssum, ssq, sm, sva, srstd, st1, st2, eps, wl, tag, scrap,
             sq_extra=()):
    W = src.shape[-1]
    p.op("vector", lambda e: e.reduce_sum(ssum, src, AX.X), waits=wl)
    p.op("scalar", lambda e: e.activation(scrap, src, AF.Square, accum_out=ssq),
         waits=list(wl) + list(sq_extra), name=tag + "sq")
    p.op("vector", lambda e: e.tensor_scalar(sm, ssum, 1.0 / W, None, OP.mult))
    p.op("vector", lambda e: e.tensor_mul(sva, sm, sm), waits=p.need(tag + "sq"))
    p.op("vector", lambda e: e.scalar_tensor_tensor(
            sva, ssq, 1.0 / W, sva, OP.mult, OP.subtract))
    p.op("vector", lambda e: e.tensor_scalar(sva, sva, eps, None, OP.add),
         name=tag + "v")
    sqrt_heron(p, srstd, sva, st1, st2, waits=p.need(tag + "v"), name=tag + "h")
    p.op("vector", lambda e: e.reciprocal(srstd, srstd), name=tag + "rs")


def build_program(cfg):
    nc = bass.Bass("TRN2", target_bir_lowering=False, debug=False)
    p = P()
    dbg = set(cfg.get("dbg", ()))
    stages = cfg.get("stages",
                     {"pre", "a1", "a2", "scan", "a4", "b1", "b2a", "b2b",
                      "b3", "b3b"})
    n_seq = cfg.get("n_seq", 2)

    def kind(nm):
        return "ExternalOutput" if nm in dbg else "Internal"

    x_h = nc.dram_tensor("x", [n_seq * S, D], F32, kind="ExternalInput").ap()
    w_h = {w: nc.dram_tensor(w, [D, D], F32, kind="ExternalInput").ap()
           for w in ("wq", "wk", "wv", "wg", "wo")}
    wfct_h = nc.dram_tensor("wfct", [D, FF], F32, kind="ExternalInput").ap()
    wpjt_h = nc.dram_tensor("wpjt", [FF, D], F32, kind="ExternalInput").ap()
    gnw_h = nc.dram_tensor("gnw", [128, 1], F32, kind="ExternalInput").ap()
    ctab_h = nc.dram_tensor("ropetab", [S, 4, 64], F32, kind="ExternalInput").ap()
    dmatT_h = nc.dram_tensor("dmatT", [128, H, 128], F32, kind="ExternalInput").ap()
    crossdec_h = nc.dram_tensor("crossdec", [128, H], F32, kind="ExternalInput").ap()
    kdec_h = nc.dram_tensor("kdec", [128, H], F32, kind="ExternalInput").ap()
    gC_h = nc.dram_tensor("gC", [128, H], F32, kind="ExternalInput").ap()
    ident_h = nc.dram_tensor("ident", [128, 128], F32, kind="ExternalInput").ap()
    ones_h = nc.dram_tensor("ones", [128, 1], F32, kind="ExternalInput").ap()
    onesr_h = nc.dram_tensor("ones_row", [1, 128], F32, kind="ExternalInput").ap()
    out_h = nc.dram_tensor("out", [n_seq * S, D], F32, kind="ExternalOutput").ap()

    tfc_h = nc.dram_tensor("s_tfc", [NK, 128, FF], BF16, kind=kind("s_tfc")).ap()
    tpj_h = nc.dram_tensor("s_tpj", [NK2, 128, D], BF16, kind=kind("s_tpj")).ap()
    gv_h = nc.dram_tensor("s_gv", [128, 4], F32, kind=kind("s_gv")).ap()
    xnf_h = nc.dram_tensor("s_xnf", [2, NK, 128, S], BF16, kind=kind("s_xnf")).ap()
    sq_h = nc.dram_tensor("s_q", [S, D], F32, kind=kind("s_q")).ap()
    sk_h = nc.dram_tensor("s_k", [S, D], F32, kind=kind("s_k")).ap()
    sv_h = nc.dram_tensor("s_v", [S, D], F32, kind=kind("s_v")).ap()
    sg_h = nc.dram_tensor("s_g", [S, D], F32, kind=kind("s_g")).ap()
    sogf_h = nc.dram_tensor("s_ogf", [2, NK, 128, S], BF16, kind=kind("s_ogf")).ap()
    sx2_h = nc.dram_tensor("s_x2", [S, D], F32, kind=kind("s_x2")).ap()
    sx2hf_h = nc.dram_tensor("s_x2hf", [2, NK, 128, S], BF16,
                             kind=kind("s_x2hf")).ap()
    sh2_h = nc.dram_tensor("s_h2", [S, FF], F32, kind=kind("s_h2")).ap()
    sx2h2f_h = nc.dram_tensor("s_x2h2f", [2, NK2, 128, S], BF16,
                              kind=kind("s_x2h2f")).ap()
    soutp_h = nc.dram_tensor("s_outp", [S, D], F32, kind=kind("s_outp")).ap()

    stack = ExitStack()
    e = stack.enter_context
    arena = e(nc.sbuf_tensor("arena", [128, 32768], F32))
    ctab = e(nc.sbuf_tensor("ctab_s", [128, NT, 4, 64], F32))
    dmat = e(nc.sbuf_tensor("dmat", [128, H, 128], F32))
    dec = e(nc.sbuf_tensor("dec", [128, 3, H], F32))
    gnw = e(nc.sbuf_tensor("gnw_s", [128, 1], F32))
    ident = e(nc.sbuf_tensor("ident_s", [128, 128], F32))
    identb = e(nc.sbuf_tensor("identb", [128, 128], BF16))
    ones = e(nc.sbuf_tensor("ones_s", [128, 1], F32))
    onesr = e(nc.sbuf_tensor("onesr", [1, 128], F32))
    gv = e(nc.sbuf_tensor("gv_s", [128, 4], F32))
    stats = e(nc.sbuf_tensor("stats", [128, 256], F32))
    stg = e(nc.sbuf_tensor("stg", [128, 8192], F32))
    stat2 = e(nc.sbuf_tensor("stat2", [128, 2, 16, 128], F32))
    pA = e(nc.psum_tensor([128, 2048], F32))
    pB = e(nc.psum_tensor([128, 2048], F32))

    AR, STG, ST = arena.ap(), stg.ap(), stats.ap()
    STGb = STG.bitcast(BF16)
    pAb = pA[:, :].bitcast(BF16)
    pBb = pB[:, :].bitcast(BF16)

    def arv(off, n):
        return AR[:, off:off + n]

    def arvb(off, n):
        return AR[:, off:off + n].bitcast(BF16)

    def hview(x):
        return x.rearrange("p (h f) -> p h f", h=H)

    def statpair(par):
        return stat2.ap()[:, par, :, :].rearrange("p k t -> p (k t)") \
            .bitcast(BF16).rearrange("p (l k t) -> p l k t", l=2, t=128)

    p.dma("sync", ctab.ap(), ctab_h.rearrange("(t p) j f -> p t j f", p=128))
    p.dma("sync", dmat.ap(), dmatT_h)
    p.dma("sync", dec.ap()[:, 0, :], crossdec_h)
    p.dma("sync", dec.ap()[:, 1, :], kdec_h)
    p.dma("sync", dec.ap()[:, 2, :], gC_h)
    p.dma("sync", gnw.ap(), gnw_h)
    p.dma("sync", ident.ap(), ident_h)
    p.dma("sync", ones.ap(), ones_h)
    p.dma("sync", onesr.ap(), onesr_h, name="consts")
    p.dma("gpsimd", identb.ap(), ident_h, name="identb", slot="cb")
    p.barrier()

    crossB = dec.ap()[:, 0:1, :].rearrange("p o h -> p h o").broadcast_to([128, H, 128])
    kdecB = dec.ap()[:, 1:2, :].rearrange("p o h -> p h o").broadcast_to([128, H, 128])
    gCB = dec.ap()[:, 2:3, :].rearrange("p o h -> p h o").broadcast_to([128, H, 128])

    if "pre" in stages:
        wbuf = [arv(0, 8192), arv(8192, 8192)]
        acc = ST[:, 0:2]
        tmp1 = ST[:, 2:3]
        p.op("vector", lambda e: e.memset(acc, 0.0), name="accz")
        specs = (("fc", wfct_h, FF, NK), ("pj", wpjt_h, D, NK2))
        gt = 0
        for wi, (wn, src, width, ntl) in enumerate(specs):
            col = acc[:, wi:wi + 1]
            for t in range(ntl):
                bv = wbuf[gt % 2][:, 0:width]
                p.dma("sync", bv, src[t * 128:(t + 1) * 128, :],
                      waits=p.need(f"prd_{gt-2}"), name=f"pld_{gt}",
                      slot=f"pw{gt%2}")
                p.op("vector", lambda e, bv=bv: e.tensor_reduce(
                        tmp1, bv, AX.X, OP.add, apply_absolute_value=True),
                     waits=p.need(f"pld_{gt}", "accz"))
                p.op("vector", lambda e, col=col: e.tensor_add(col, col, tmp1),
                     name=f"prd_{gt}")
                gt += 1
        p.op("tensor", lambda e: e.matmul(pA[0:1, 0:2], ones.ap(), acc,
                                          start=True, stop=True),
             waits=p.need("prd_78", "prd_79"))
        p.tmark("pps")
        p.op("vector", lambda e: e.tensor_copy(ST[0:1, 4:6], pA[0:1, 0:2]),
             waits=p.need("pps"), name="pg0")
        NW = float(D * FF)
        p.op("vector", lambda e: e.tensor_scalar(
                ST[0:1, 4:6], ST[0:1, 4:6], 1.0 / NW, None, OP.mult),
             waits=p.need("pg0"))
        p.op("vector", lambda e: e.tensor_scalar(
                ST[0:1, 6:8], ST[0:1, 4:6], EPS_Q, None, OP.add))
        p.op("vector", lambda e: e.reciprocal(ST[0:1, 6:8], ST[0:1, 6:8]),
             name="pg1")
        p.op("tensor", lambda e: e.matmul(pA[:, 0:4], onesr.ap(), ST[0:1, 4:8],
                                          start=True, stop=True),
             waits=p.need("pg1"))
        p.tmark("pbc")
        p.op("vector", lambda e: e.tensor_copy(gv.ap(), pA[:, 0:4]),
             waits=p.need("pbc"), name="gv")
        p.dma("gpsimd", gv_h, gv.ap(), waits=p.need("gv"), slot="gvs")
        ifc, ipj = gv.ap()[:, 2:3], gv.ap()[:, 3:4]
        tstage = [arvb(16384, 4096), arvb(20480, 4096)]
        gt = 0
        for (wn, src, width, ntl), inv, dst in zip(specs, (ifc, ipj),
                                                   (tfc_h, tpj_h)):
            for t in range(ntl):
                bv = wbuf[gt % 2][:, 0:width]
                ts_ = tstage[gt % 2][:, 0:width]
                zb = STG[:, 0:width]
                p.dma("sync", bv, src[t * 128:(t + 1) * 128, :],
                      waits=p.need(f"p2z_{gt-2}", "prd_78", "prd_79"),
                      name=f"p2l_{gt}", slot=f"pw{gt%2}")
                p.op("scalar", lambda e, bv=bv, zb=zb, inv=inv: e.activation(
                        zb, bv, AF.Identity, bias=0.0, scale=inv),
                     waits=p.need(f"p2l_{gt}", "gv", f"p2b_{gt-1}"),
                     name=f"p2z_{gt}")
                p.op("vector", lambda e, zb=zb, ts_=ts_: e.tensor_scalar(
                        ts_, zb, -0.5, None, OP.is_lt),
                     waits=p.need(f"p2z_{gt}", f"p2s_{gt-2}"))
                p.op("vector", lambda e, zb=zb, ts_=ts_: e.scalar_tensor_tensor(
                        ts_, zb, 0.5, ts_, OP.is_gt, OP.subtract),
                     name=f"p2b_{gt}")
                p.dma("gpsimd", dst[t], ts_,
                      waits=p.need(f"p2b_{gt}"), name=f"p2s_{gt}",
                      slot=f"pt{gt%2}")
                gt += 1
        p.barrier()

    for sq in range(n_seq):
        sx = f"s{sq}_"
        xseq = x_h[sq * S:(sq + 1) * S, :]
        oseq = out_h[sq * S:(sq + 1) * S, :]

        def build_wpair(tag, wap, scale_ap=None, extra=()):
            whi = arvb(0, 16384).rearrange("p (k o) -> p k o", k=NK)
            wlo = arvb(16384, 16384).rearrange("p (k o) -> p k o", k=NK)
            wst = STG[:, 2048:4096]
            for kk in range(NK):
                ex = list(extra) if kk < 2 else []
                p.dma("sync", wst, wap[kk * 128:(kk + 1) * 128, :],
                      waits=p.need(f"{tag}l{kk-1}") + ex,
                      name=f"{tag}ld{kk}", slot="wld")
                wv = p.need(f"{tag}ld{kk}")
                if scale_ap is None:
                    p.dma("gpsimd", whi[:, kk, :],
                          wap[kk * 128:(kk + 1) * 128, :],
                          waits=p.need(f"{tag}hc{kk-2}") + ex,
                          name=f"{tag}hc{kk}", slot=f"whc{kk%2}")
                    eng = ("vector", "gpsimd")[kk % 2]
                    p.op(eng, lambda e, kk=kk: e.tensor_tensor(
                            wlo[:, kk, :], wst, whi[:, kk, :], OP.subtract),
                         waits=wv + p.need(f"{tag}hc{kk}") + ex,
                         name=f"{tag}ls{kk}")
                    p.mark(f"{tag}l{kk}", "e_" + eng)
                else:
                    p.op("vector", lambda e, kk=kk: e.tensor_scalar(
                            whi[:, kk, :], wst, scale_ap, None, OP.mult),
                         waits=wv + ex)
                    p.op("vector", lambda e, kk=kk: e.scalar_tensor_tensor(
                            wlo[:, kk, :], wst, scale_ap, whi[:, kk, :],
                            OP.mult, OP.subtract),
                         name=f"{tag}l{kk}")
            return whi, wlo

        if "a1" in stages:
            xbuf = [arv(0, 2048), arv(2048, 2048)]
            xnb = [arv(4096, 2048), arv(6144, 2048)]
            xnpb = [arvb(8192, 2048), arvb(10240, 2048)]
            scrap = arv(12288, 2048)
            for i in range(NT):
                xb = xbuf[i % 2]
                c = 16 * (i % 2)
                ssum, ssq, sm = ST[:, c:c+1], ST[:, c+1:c+2], ST[:, c+2:c+3]
                sva, srstd = ST[:, c+3:c+4], ST[:, c+4:c+5]
                snmr, st1, st2 = ST[:, c+5:c+6], ST[:, c+6:c+7], ST[:, c+7:c+8]
                p.dma("sync", xb, xseq[i * 128:(i + 1) * 128, :],
                      waits=p.need(f"{sx}a1xn{i-2}"), name=f"{sx}a1x{i}",
                      slot=f"x{i%2}")
                wl = p.need(f"{sx}a1x{i}")
                ln_stats(p, xb, ssum, ssq, sm, sva, srstd, st1, st2, 1e-5,
                         wl, f"{sx}a1s{i}", scrap)
                p.op("vector", lambda e, snmr=snmr, sm=sm, srstd=srstd:
                     e.tensor_mul(snmr, sm, srstd))
                p.op("vector", lambda e, snmr=snmr: e.tensor_scalar(
                        snmr, snmr, -1.0, None, OP.mult), name=f"{sx}a1st{i}")
                xt = xnb[i % 2]
                xp = xnpb[i % 2]
                p.op("scalar", lambda e, xb=xb, xt=xt, snmr=snmr, srstd=srstd:
                     e.activation(xt, xb, AF.Identity, bias=snmr, scale=srstd),
                     waits=p.need(f"{sx}a1st{i}", f"{sx}a1pr{i-2}"),
                     name=f"{sx}a1xn{i}")
                p.op("vector", lambda e, xt=xt, xp=xp: e.tensor_copy(
                        xp[:, 0:2048], xt),
                     waits=p.need(f"{sx}a1xn{i}", f"{sx}a1tg{i-2}"))
                p.op("vector", lambda e, xt=xt, xp=xp: e.tensor_tensor(
                        xp[:, 2048:4096], xt, xp[:, 0:2048], OP.subtract),
                     name=f"{sx}a1pr{i}")
                ps = (pAb, pBb)[i % 2]
                for l in range(2):
                    for kk in range(16):
                        p.op("tensor", lambda e, ps=ps, l=l, kk=kk, xp=xp:
                             e.transpose(
                                ps[:, l * 2048 + kk * 128:
                                   l * 2048 + (kk + 1) * 128],
                                xp[:, l * 2048 + kk * 128:
                                   l * 2048 + (kk + 1) * 128],
                                identb.ap()),
                             waits=p.need(f"{sx}a1pr{i}", f"{sx}a1ev{i-2}"))
                p.tmark(f"{sx}a1tg{i}")
                p.op("vector", lambda e, ps=ps: e.tensor_copy(
                        STGb[:, 0:4096], ps[:, 0:4096]),
                     waits=p.need(f"{sx}a1tg{i}", f"{sx}a1sp0_{i-1}",
                                  f"{sx}a1sp1_{i-1}"),
                     name=f"{sx}a1ev{i}")
                for l in range(2):
                    p.dma("gpsimd",
                          xnf_h[l, :, :, i * 128:(i + 1) * 128]
                          .rearrange("k p t -> p k t"),
                          STGb[:, l * 2048:(l + 1) * 2048]
                          .rearrange("p (k t) -> p k t", t=128),
                          waits=p.need(f"{sx}a1ev{i}"),
                          name=f"{sx}a1sp{l}_{i}", slot=f"a1sp{l}")
            p.barrier()

        if "a2" in stages:
            projs = (("wq", sq_h, 0), ("wk", sk_h, 2), ("wv", sv_h, None),
                     ("wg", sg_h, None))
            for pi, (wn, dst, coff) in enumerate(projs):
                whi, wlo = build_wpair(
                    f"{sx}a2w{pi}_", w_h[wn],
                    extra=p.need(f"{sx}a2mmdone{pi-1}", f"{sx}a2ev{pi*NT-1}"))
                for i in range(NT):
                    ti = pi * NT + i
                    sp = statpair(i % 2)
                    p.dma("scalar", sp,
                          xnf_h[:, :, :, i * 128:(i + 1) * 128]
                          .rearrange("l k p t -> p l k t"),
                          waits=p.need(f"{sx}a2mm{ti-2}"), name=f"{sx}a2st{ti}",
                          slot=f"st{ti%2}")
                    ps = (pA, pB)[i % 2]
                    for kk in range(NK):
                        wmmk = p.need(f"{sx}a2w{pi}_l{kk}", f"{sx}a2st{ti}",
                                      f"{sx}a2ev{ti-2}")
                        for idx_, (l, wv_) in enumerate(
                                ((0, whi), (0, wlo), (1, whi))):
                            for g in range(4):
                                p.op("tensor",
                                     lambda e, ps=ps, sp=sp, kk=kk, g=g, l=l,
                                     wv_=wv_, idx_=idx_: e.matmul(
                                        ps[:, g * 512:(g + 1) * 512],
                                        sp[:, l, kk, :],
                                        wv_[:, kk, g * 512:(g + 1) * 512],
                                        start=(kk == 0 and idx_ == 0),
                                        stop=(kk == NK - 1 and idx_ == 2)),
                                     waits=wmmk)
                    p.tmark(f"{sx}a2mm{ti}")
                    wmm = p.need(f"{sx}a2mm{ti}", f"{sx}a2sp{ti-1}")
                    if coff is None:
                        if wn == "wv":
                            p.op("vector", lambda e, ps=ps: e.tensor_copy(
                                    STG[:, 0:2048], ps[:, :]),
                                 waits=wmm, name=f"{sx}a2ev{ti}")
                        else:
                            p.op("scalar", lambda e, ps=ps: e.activation(
                                    STG[:, 0:2048], ps[:, :], AF.Silu),
                                 waits=wmm, name=f"{sx}a2ev{ti}")
                    else:
                        cB = ctab.ap()[:, i, coff, :].rearrange("p f -> p () f") \
                            .broadcast_to([128, H, 64])
                        sB = ctab.ap()[:, i, coff + 1, :].rearrange("p f -> p () f") \
                            .broadcast_to([128, H, 64])
                        rv = hview(STG[:, 6144:8192])
                        qc = hview(STG[:, 0:2048])
                        qs_ = hview(STG[:, 4096:6144])
                        ov = hview(STG[:, 0:2048])
                        p.op("scalar", lambda e, ps=ps: e.activation(
                                STG[:, 6144:8192], ps[:, :], AF.Identity),
                             waits=wmm + p.need(f"{sx}a2ev{ti-1}"),
                             name=f"{sx}a2rv{ti}")
                        wrv = p.need(f"{sx}a2rv{ti}")
                        p.op("vector", lambda e, qc=qc, cB=cB, rv=rv:
                             e.tensor_tensor(qc[:, :, 0:64], rv[:, :, 0:64],
                                             cB, OP.mult), waits=wrv + wmm)
                        p.op("vector", lambda e, qc=qc, cB=cB, rv=rv:
                             e.tensor_tensor(qc[:, :, 64:128], rv[:, :, 64:128],
                                             cB, OP.mult))
                        p.op("gpsimd", lambda e, qs_=qs_, sB=sB, rv=rv:
                             e.tensor_tensor(qs_[:, :, 0:64], rv[:, :, 64:128],
                                             sB, OP.mult),
                             waits=wrv + p.need(f"{sx}a2ev{ti-1}"),
                             name=f"{sx}a2s1_{ti}")
                        p.op("gpsimd", lambda e, qs_=qs_, sB=sB, rv=rv:
                             e.tensor_tensor(qs_[:, :, 64:128], rv[:, :, 0:64],
                                             sB, OP.mult),
                             name=f"{sx}a2s2_{ti}")
                        p.op("vector", lambda e, ov=ov, qc=qc, qs_=qs_:
                             e.tensor_sub(ov[:, :, 0:64], qc[:, :, 0:64],
                                          qs_[:, :, 0:64]),
                             waits=p.need(f"{sx}a2s1_{ti}"))
                        p.op("vector", lambda e, ov=ov, qc=qc, qs_=qs_:
                             e.tensor_add(ov[:, :, 64:128], qc[:, :, 64:128],
                                          qs_[:, :, 64:128]),
                             waits=p.need(f"{sx}a2s2_{ti}"),
                             name=f"{sx}a2ev{ti}")
                    p.dma("gpsimd", dst[i * 128:(i + 1) * 128, :],
                          STG[:, 0:2048],
                          waits=p.need(f"{sx}a2ev{ti}"), name=f"{sx}a2sp{ti}",
                          slot="a2sp")
                p.mark(f"{sx}a2mmdone{pi}", "e_tensor")
            p.barrier()

        if "scan" in stages:
            qb = [arv(0, 2048), arv(2048, 2048)]
            kb = [arv(4096, 2048), arv(6144, 2048)]
            vb = [arv(8192, 2048), arv(10240, 2048)]
            gb = [arv(12288, 2048), arv(14336, 2048)]
            Sst = arv(16384, 2048)
            qf = arv(18432, 2048)
            kf = arv(20480, 2048)
            Asb = arv(22528, 2048)
            k2 = arv(24576, 2048)
            ocmb = arv(26624, 2048)
            ogst = arv(28672, 2048)
            scrap2 = arv(30720, 2048)
            scrap2b = arvb(30720, 2048)
            p.op("vector", lambda e: e.memset(Sst, 0.0), name=f"{sx}scS")
            for i in range(NCH):
                w0 = i * 128
                p.dma("sync", qb[i % 2], sq_h[w0:w0 + 128, :],
                      waits=p.need(f"{sx}scTq{i-2}"), name=f"{sx}sclq{i}",
                      slot=f"sq{i%2}")
                p.dma("sync", kb[i % 2], sk_h[w0:w0 + 128, :],
                      waits=p.need(f"{sx}sck2_{i-2}", f"{sx}scTk{i-2}"),
                      name=f"{sx}sclk{i}", slot=f"sk{i%2}")
                p.dma("sync", vb[i % 2], sv_h[w0:w0 + 128, :],
                      waits=p.need(f"{sx}scDel{i-2}"), name=f"{sx}sclv{i}",
                      slot=f"sv{i%2}")
                p.dma("scalar", gb[i % 2], sg_h[w0:w0 + 128, :],
                      waits=p.need(f"{sx}scog{i-2}"), name=f"{sx}sclg{i}",
                      slot=f"sg{i%2}")
                q_tm, k_tm = qb[i % 2], kb[i % 2]
                v_tm, sg_tm = vb[i % 2], gb[i % 2]
                for (src, ps, dst_, nm, wx) in (
                        (q_tm, pA, qf, "q", f"{sx}scSup{i-1}"),
                        (k_tm, pB, kf, "k", f"{sx}scogev{i-1}")):
                    for hh in range(H):
                        p.op("tensor", lambda e, src=src, ps=ps, hh=hh:
                             e.transpose(ps[:, hh * 128:(hh + 1) * 128],
                                         src[:, hh * 128:(hh + 1) * 128],
                                         ident.ap()),
                             waits=p.need(f"{sx}scl{nm}{i}", wx,
                                          f"{sx}scT{nm}ev{i-1}"))
                    p.tmark(f"{sx}scT{nm}{i}")
                    p.op("scalar", lambda e, ps=ps, dst_=dst_: e.activation(
                            dst_, ps[:, :], AF.Identity),
                         waits=p.need(f"{sx}scT{nm}{i}"),
                         name=f"{sx}scT{nm}ev{i}")
                for hh in range(H):
                    p.op("tensor", lambda e, hh=hh: e.matmul(
                            pA[:, hh * 128:(hh + 1) * 128],
                            kf[:, hh * 128:(hh + 1) * 128],
                            qf[:, hh * 128:(hh + 1) * 128],
                            start=(hh % 4 == 0), stop=(hh % 4 == 3)),
                         waits=p.need(f"{sx}scTqev{i}", f"{sx}scTkev{i}"))
                p.tmark(f"{sx}scA{i}")
                p.op("vector", lambda e: e.tensor_tensor(
                        hview(Asb), hview(pA[:, :]), dmat.ap(), OP.mult),
                     waits=p.need(f"{sx}scA{i}"), name=f"{sx}scAm{i}")
                for hh in range(H):
                    p.op("tensor", lambda e, hh=hh, v_tm=v_tm: e.matmul(
                            pB[:, hh * 128:(hh + 1) * 128],
                            Asb[:, hh * 128:(hh + 1) * 128],
                            v_tm[:, hh * 128:(hh + 1) * 128],
                            start=(hh % 4 == 0), stop=(hh % 4 == 3)),
                         waits=p.need(f"{sx}scAm{i}", f"{sx}sclv{i}",
                                      f"{sx}scTkev{i}"))
                p.tmark(f"{sx}scI{i}")
                for hh in range(H):
                    p.op("tensor", lambda e, hh=hh: e.matmul(
                            pA[:, hh * 128:(hh + 1) * 128],
                            qf[:, hh * 128:(hh + 1) * 128],
                            Sst[:, hh * 128:(hh + 1) * 128],
                            start=(hh % 4 == 0), stop=(hh % 4 == 3)),
                         waits=p.need(f"{sx}scAm{i}",
                                      f"{sx}scS" if i == 0 else f"{sx}scSup{i-1}"))
                p.tmark(f"{sx}scX{i}")
                p.op("vector", lambda e: e.tensor_tensor(
                        hview(ocmb), hview(pA[:, :]), crossB, OP.mult),
                     waits=p.need(f"{sx}scX{i}"), name=f"{sx}scOc1{i}")
                p.op("vector", lambda e: e.tensor_add(ocmb, ocmb, pB[:, :]),
                     waits=p.need(f"{sx}scI{i}"), name=f"{sx}scOc{i}")
                p.op("gpsimd", lambda e, k_tm=k_tm: e.tensor_tensor(
                        hview(k2), hview(k_tm), kdecB, OP.mult),
                     waits=p.need(f"{sx}sclk{i}", f"{sx}scSup{i-1}"),
                     name=f"{sx}sck2_{i}")
                for hh in range(H):
                    p.op("tensor", lambda e, hh=hh, v_tm=v_tm: e.matmul(
                            pA[:, hh * 128:(hh + 1) * 128],
                            k2[:, hh * 128:(hh + 1) * 128],
                            v_tm[:, hh * 128:(hh + 1) * 128],
                            start=(hh % 4 == 0), stop=(hh % 4 == 3)),
                         waits=p.need(f"{sx}sck2_{i}", f"{sx}scOc1{i}"))
                p.tmark(f"{sx}scDel{i}")
                p.op("vector", lambda e: e.tensor_tensor(
                        hview(Sst), hview(Sst), gCB, OP.mult),
                     waits=p.need(f"{sx}scX{i}", f"{sx}scDel{i}"))
                p.op("vector", lambda e: e.tensor_add(Sst, Sst, pA[:, :]),
                     name=f"{sx}scSup{i}")
                c = 48
                st16 = ST[:, c:c + 16]
                sa = ST[:, c + 16:c + 32]
                sh1 = ST[:, c + 32:c + 48]
                sh2v = ST[:, c + 48:c + 64]
                p.op("scalar", lambda e: e.activation(
                        scrap2, ocmb, AF.Square),
                     waits=p.need(f"{sx}scOc{i}", f"{sx}scsp0_{i-1}",
                                  f"{sx}scsp1_{i-1}", f"{sx}scogT{i-1}"),
                     name=f"{sx}scsq2_{i}")
                p.op("vector", lambda e, st16=st16: e.tensor_reduce(
                        st16, hview(scrap2), AX.X, OP.add),
                     waits=p.need(f"{sx}scrms{i-1}", f"{sx}scsq2_{i}"))
                p.op("vector", lambda e, sa=sa, st16=st16: e.tensor_scalar(
                        sa, st16, 1.0 / DV, None, OP.mult))
                p.op("vector", lambda e, sa=sa: e.tensor_scalar(
                        sa, sa, 1e-5, None, OP.add), name=f"{sx}scva{i}")
                sqrt_heron(p, st16, sa, sh1, sh2v, waits=p.need(f"{sx}scva{i}"),
                           name=f"{sx}scsq{i}")
                p.op("vector", lambda e, st16=st16: e.reciprocal(st16, st16),
                     name=f"{sx}scrms{i}")
                rstdB = st16.rearrange("p h -> p h ()").broadcast_to([128, H, 128])
                p.op("vector", lambda e, rstdB=rstdB: e.tensor_tensor(
                        hview(ogst), hview(ocmb), rstdB, OP.mult),
                     waits=p.need(f"{sx}scogT{i-1}"))
                p.op("vector", lambda e, sg_tm=sg_tm: e.tensor_mul(
                        ogst, ogst, sg_tm),
                     waits=p.need(f"{sx}sclg{i}"), name=f"{sx}scogv{i}")
                p.alias(f"{sx}scog{i}", f"{sx}scogv{i}")
                p.op("scalar", lambda e: e.activation(
                        scrap2b[:, 0:2048], ogst, AF.Identity),
                     waits=p.need(f"{sx}scogv{i}"), name=f"{sx}scogh{i}")
                p.op("vector", lambda e: e.tensor_tensor(
                        scrap2b[:, 2048:4096], ogst, scrap2b[:, 0:2048],
                        OP.subtract),
                     waits=p.need(f"{sx}scogh{i}"), name=f"{sx}scogp{i}")
                for l in range(2):
                    for kk in range(16):
                        p.op("tensor", lambda e, l=l, kk=kk: e.transpose(
                                pBb[:, l * 2048 + kk * 128:
                                    l * 2048 + (kk + 1) * 128],
                                scrap2b[:, l * 2048 + kk * 128:
                                        l * 2048 + (kk + 1) * 128],
                                identb.ap()),
                             waits=p.need(f"{sx}scogp{i}", f"{sx}scOc{i}",
                                          f"{sx}scogev{i-1}"))
                p.tmark(f"{sx}scogT{i}")
                p.op("vector", lambda e: e.tensor_copy(STGb[:, 0:4096],
                                                       pBb[:, 0:4096]),
                     waits=p.need(f"{sx}scogT{i}", f"{sx}scsp0_{i-1}",
                                  f"{sx}scsp1_{i-1}"),
                     name=f"{sx}scogev{i}")
                for l in range(2):
                    p.dma("gpsimd",
                          sogf_h[l, :, :, w0:w0 + 128]
                          .rearrange("k p t -> p k t"),
                          STGb[:, l * 2048:(l + 1) * 2048]
                          .rearrange("p (k t) -> p k t", t=128),
                          waits=p.need(f"{sx}scogev{i}"),
                          name=f"{sx}scsp{l}_{i}", slot=f"scsp{l}")
            p.barrier()

        if "a4" in stages:
            whi, wlo = build_wpair(f"{sx}a4w_", w_h["wo"], scale_ap=gnw.ap())
            for i in range(NT):
                w0 = i * 128
                sp = statpair(i % 2)
                p.dma("scalar", sp,
                      sogf_h[:, :, :, w0:w0 + 128]
                      .rearrange("l k p t -> p l k t"),
                      waits=p.need(f"{sx}a4mm{i-2}"), name=f"{sx}a4st{i}",
                      slot=f"st{i%2}")
                ps = (pA, pB)[i % 2]
                for kk in range(NK):
                    wmmk = p.need(f"{sx}a4w_l{kk}", f"{sx}a4st{i}",
                                  f"{sx}a4ev{i-2}")
                    for idx_, (l, wv_) in enumerate(((0, whi), (0, wlo), (1, whi))):
                        for g in range(4):
                            p.op("tensor",
                                 lambda e, ps=ps, sp=sp, kk=kk, g=g, l=l,
                                 wv_=wv_, idx_=idx_: e.matmul(
                                    ps[:, g * 512:(g + 1) * 512],
                                    sp[:, l, kk, :],
                                    wv_[:, kk, g * 512:(g + 1) * 512],
                                    start=(kk == 0 and idx_ == 0),
                                    stop=(kk == NK - 1 and idx_ == 2)),
                                 waits=wmmk)
                p.tmark(f"{sx}a4mm{i}")
                p.op("vector", lambda e, ps=ps: e.tensor_copy(
                        STG[:, 0:2048], ps[:, :]),
                     waits=p.need(f"{sx}a4mm{i}", f"{sx}a4sp{i-1}"),
                     name=f"{sx}a4ev{i}")
                p.dma("gpsimd", soutp_h[w0:w0 + 128, :], STG[:, 0:2048],
                      waits=p.need(f"{sx}a4ev{i}"), name=f"{sx}a4sp{i}",
                      slot="a2sp")
            p.barrier()

        if "b1" in stages:
            abuf = [arv(0, 2048), arv(2048, 2048)]
            xbuf = [arv(4096, 2048), arv(6144, 2048)]
            x2b = [arv(8192, 2048), arv(10240, 2048)]
            tb = [arv(12288, 2048), arv(14336, 2048)]
            xqb2 = [arv(16384, 2048), arv(18432, 2048)]
            xqpb = [arvb(20480, 2048), arvb(22528, 2048)]
            scrapb1 = arv(24576, 2048)
            for i in range(NT):
                w0 = i * 128
                ab, xb, x2v = abuf[i % 2], xbuf[i % 2], x2b[i % 2]
                tv, xqv, xqp = tb[i % 2], xqb2[i % 2], xqpb[i % 2]
                c = 32 + 24 * (i % 2)
                ssum, ssq = ST[:, c:c+1], ST[:, c+1:c+2]
                sm, sva, srstd = ST[:, c+2:c+3], ST[:, c+3:c+4], ST[:, c+4:c+5]
                st1, st2 = ST[:, c+5:c+6], ST[:, c+6:c+7]
                sm2, sv2 = ST[:, c+7:c+8], ST[:, c+8:c+9]
                sa2, sb2, sc2 = ST[:, c+9:c+10], ST[:, c+10:c+11], ST[:, c+11:c+12]
                smx, sc1, sg2 = ST[:, c+12:c+13], ST[:, c+13:c+14], ST[:, c+14:c+15]
                p.dma("sync", ab, soutp_h[w0:w0 + 128, :],
                      waits=p.need(f"{sx}b1ab{i-2}"), name=f"{sx}b1la{i}",
                      slot=f"x{i%2}")
                p.dma("sync", xb, xseq[w0:w0 + 128, :],
                      waits=p.need(f"{sx}a4x2{i-2}"), name=f"{sx}b1lx{i}",
                      slot=f"xb{i%2}")
                wl = p.need(f"{sx}b1la{i}")
                ln_stats(p, ab, ssum, ssq, sm, sva, srstd, st1, st2, 1e-5,
                         wl, f"{sx}b1s{i}", scrapb1)
                p.op("vector", lambda e, ab=ab, sm=sm, srstd=srstd, tv=tv:
                     e.tensor_scalar(tv, ab, sm, srstd, OP.subtract, OP.mult),
                     name=f"{sx}b1ab{i}")
                p.op("vector", lambda e, x2v=x2v, xb=xb, tv=tv: e.tensor_add(
                        x2v, tv, xb),
                     waits=p.need(f"{sx}b1lx{i}", f"{sx}a4spx2{i-2}"),
                     name=f"{sx}a4x2{i}")
                p.dma("gpsimd", sx2_h[w0:w0 + 128, :], x2v,
                      waits=p.need(f"{sx}a4x2{i}"), name=f"{sx}a4spx2{i}",
                      slot=f"x2sp{i%2}")
                wx2 = p.need(f"{sx}a4x2{i}")
                p.op("vector", lambda e, x2v=x2v, ssum=ssum: e.reduce_sum(
                        ssum, x2v, AX.X), waits=wx2)
                p.op("scalar", lambda e, x2v=x2v, ssq=ssq: e.activation(
                        scrapb1, x2v, AF.Square, accum_out=ssq),
                     waits=wx2 + p.need(f"{sx}b1s{i}sq"), name=f"{sx}b1q{i}sq")
                p.op("vector", lambda e, sm2=sm2, ssum=ssum: e.tensor_scalar(
                        sm2, ssum, 1.0 / D, None, OP.mult))
                p.op("vector", lambda e, sv2=sv2, sm2=sm2: e.tensor_mul(
                        sv2, sm2, sm2), waits=p.need(f"{sx}b1q{i}sq"))
                p.op("vector", lambda e, sv2=sv2, ssq=ssq: e.scalar_tensor_tensor(
                        sv2, ssq, 1.0 / D, sv2, OP.mult, OP.subtract))
                p.op("vector", lambda e, sa2=sa2, sv2=sv2: e.tensor_scalar(
                        sa2, sv2, 1e-5, None, OP.add), name=f"{sx}b1a{i}")
                p.op("vector", lambda e, sb2=sb2, sa2=sa2, sv2=sv2:
                     e.scalar_tensor_tensor(sb2, sa2, 1e-6, sv2, OP.mult, OP.add),
                     waits=p.need(f"{sx}b1a{i}"), name=f"{sx}b1b{i}")
                sqrt_heron(p, sc2, sb2, st1, st2, waits=p.need(f"{sx}b1b{i}"),
                           name=f"{sx}b1h{i}")
                p.op("vector", lambda e, sc2=sc2: e.reciprocal(sc2, sc2),
                     name=f"{sx}b1rc{i}")
                p.op("vector", lambda e, x2v=x2v, sm2=sm2, tv=tv: e.tensor_scalar(
                        tv, x2v, sm2, None, OP.subtract), name=f"{sx}b1t{i}")
                p.op("vector", lambda e, smx=smx, tv=tv: e.reduce_max(
                        smx, tv, AX.X, apply_absolute_value=True),
                     name=f"{sx}b1mx{i}")
                p.op("vector", lambda e, sc1=sc1, smx=smx: e.reciprocal(sc1, smx))
                p.op("vector", lambda e, sc1=sc1: e.tensor_scalar(
                        sc1, sc1, 128.0, None, OP.mult))
                p.op("vector", lambda e, sg2=sg2, sc2=sc2, smx=smx: e.tensor_mul(
                        sg2, sc2, smx),
                     waits=p.need(f"{sx}b1rc{i}", f"{sx}b1mx{i}"))
                p.op("vector", lambda e, sg2=sg2: e.tensor_scalar(
                        sg2, sg2, 1.0 / 128.0, None, OP.mult), name=f"{sx}b1g{i}")
                p.op("vector", lambda e, tv=tv, sc1=sc1: e.tensor_scalar(
                        tv, tv, sc1, CLIP, OP.mult, OP.min), name=f"{sx}b1u{i}")
                p.op("vector", lambda e, tv=tv, xqv=xqv, sg2=sg2: e.tensor_scalar(
                        xqv, tv, -CLIP, sg2, OP.max, OP.mult),
                     waits=p.need(f"{sx}b1g{i}", f"{sx}b1u{i}"),
                     name=f"{sx}b1xq{i}")
                p.op("vector", lambda e, xqv=xqv, xqp=xqp: e.tensor_copy(
                        xqp[:, 0:2048], xqv),
                     waits=p.need(f"{sx}b1tg{i-2}"))
                p.op("vector", lambda e, xqv=xqv, xqp=xqp: e.tensor_tensor(
                        xqp[:, 2048:4096], xqv, xqp[:, 0:2048], OP.subtract),
                     name=f"{sx}b1pr{i}")
                ps = (pAb, pBb)[i % 2]
                for l in range(2):
                    for kk in range(16):
                        p.op("tensor", lambda e, ps=ps, l=l, kk=kk, xqp=xqp:
                             e.transpose(
                                ps[:, l * 2048 + kk * 128:
                                   l * 2048 + (kk + 1) * 128],
                                xqp[:, l * 2048 + kk * 128:
                                    l * 2048 + (kk + 1) * 128],
                                identb.ap()),
                             waits=p.need(f"{sx}b1pr{i}", f"{sx}b1tev{i-2}"))
                p.tmark(f"{sx}b1tg{i}")
                p.op("vector", lambda e, ps=ps: e.tensor_copy(
                        STGb[:, 0:4096], ps[:, 0:4096]),
                     waits=p.need(f"{sx}b1tg{i}", f"{sx}b1sp0_{i-1}",
                                  f"{sx}b1sp1_{i-1}"),
                     name=f"{sx}b1tev{i}")
                for l in range(2):
                    p.dma("gpsimd",
                          sx2hf_h[l, :, :, w0:w0 + 128]
                          .rearrange("k p t -> p k t"),
                          STGb[:, l * 2048:(l + 1) * 2048]
                          .rearrange("p (k t) -> p k t", t=128),
                          waits=p.need(f"{sx}b1tev{i}"),
                          name=f"{sx}b1sp{l}_{i}", slot=f"a1sp{l}")
            p.barrier()

        if "b2a" in stages:
            tqb = [arvb(0, 16384).rearrange("p (k f) -> p k f", k=NK),
                   arvb(16384, 16384).rearrange("p (k f) -> p k f", k=NK)]
            for qq in range(4):
                tq = tqb[qq % 2]
                p.dma("sync", tq,
                      tfc_h[:, :, qq * 2048:(qq + 1) * 2048]
                      .rearrange("k p f -> p k f"),
                      waits=p.need(f"{sx}b2mm{(qq-1)*NT-1}"),
                      name=f"{sx}b2t{qq}", slot=f"wld{qq%2}")
                for i in range(NT):
                    ti = qq * NT + i
                    sp = statpair(ti % 2)
                    p.dma("scalar", sp,
                          sx2hf_h[:, :, :, i * 128:(i + 1) * 128]
                          .rearrange("l k p t -> p l k t"),
                          waits=p.need(f"{sx}b2mm{ti-2}"), name=f"{sx}b2st{ti}",
                          slot=f"st{ti%2}")
                    ps = (pA, pB)[ti % 2]
                    for kk in range(NK):
                        wmmk = p.need(f"{sx}b2t{qq}", f"{sx}b2st{ti}",
                                      f"{sx}b2ev{ti-2}")
                        for l in range(2):
                            for g in range(4):
                                p.op("tensor",
                                     lambda e, ps=ps, sp=sp, kk=kk, g=g, l=l,
                                     tq=tq: e.matmul(
                                        ps[:, g * 512:(g + 1) * 512],
                                        sp[:, l, kk, :],
                                        tq[:, kk, g * 512:(g + 1) * 512],
                                        start=(kk == 0 and l == 0),
                                        stop=(kk == NK - 1 and l == 1)),
                                     waits=wmmk)
                    p.tmark(f"{sx}b2mm{ti}")
                    p.op("scalar", lambda e, ps=ps: e.activation(
                            STG[:, 0:2048], ps[:, :], AF.Gelu,
                            scale=gv.ap()[:, 0:1]),
                         waits=p.need(f"{sx}b2mm{ti}", f"{sx}b2sp{ti-1}"),
                         name=f"{sx}b2ev{ti}")
                    p.dma("gpsimd",
                          sh2_h[i * 128:(i + 1) * 128,
                                qq * 2048:(qq + 1) * 2048],
                          STG[:, 0:2048],
                          waits=p.need(f"{sx}b2ev{ti}"), name=f"{sx}b2sp{ti}",
                          slot="b2sp")
            p.barrier()

        if "b2b" in stages:
            hbuf = [arv(0, 8192), arv(8192, 8192)]
            xqv = arv(16384, 8192)
            xqp = arvb(24576, 8192)
            for i in range(NT):
                w0 = i * 128
                hb = hbuf[i % 2]
                c = 96 + 16 * (i % 2)
                ssum, ssq = ST[:, c:c+1], ST[:, c+1:c+2]
                sm2, sv2 = ST[:, c+2:c+3], ST[:, c+3:c+4]
                sb2, sc2 = ST[:, c+4:c+5], ST[:, c+5:c+6]
                smx, sc1, sg2 = ST[:, c+6:c+7], ST[:, c+7:c+8], ST[:, c+8:c+9]
                st1, st2 = ST[:, c+9:c+10], ST[:, c+10:c+11]
                p.dma("sync", hb, sh2_h[w0:w0 + 128, :],
                      waits=p.need(f"{sx}b2bt_{i-2}"), name=f"{sx}b2bl{i}",
                      slot=f"x{i%2}")
                wl = p.need(f"{sx}b2bl{i}")
                p.op("vector", lambda e, hb=hb, ssum=ssum: e.reduce_sum(
                        ssum, hb, AX.X), waits=wl + p.need(f"{sx}b2bpr{i-1}"))
                p.op("scalar", lambda e, hb=hb, ssq=ssq: e.activation(
                        STG, hb, AF.Square, accum_out=ssq),
                     waits=wl + p.need(f"{sx}b2bsq{i-1}", f"{sx}b2bev{i-1}",
                                       f"{sx}b2bsp0_{i-1}", f"{sx}b2bsp1_{i-1}"),
                     name=f"{sx}b2bsq{i}")
                p.op("vector", lambda e, sm2=sm2, ssum=ssum: e.tensor_scalar(
                        sm2, ssum, 1.0 / FF, None, OP.mult))
                p.op("vector", lambda e, sv2=sv2, sm2=sm2: e.tensor_mul(
                        sv2, sm2, sm2), waits=p.need(f"{sx}b2bsq{i}"))
                p.op("vector", lambda e, sv2=sv2, ssq=ssq: e.scalar_tensor_tensor(
                        sv2, ssq, 1.0 / FF, sv2, OP.mult, OP.subtract),
                     name=f"{sx}b2bv{i}")
                p.op("vector", lambda e, sb2=sb2, sv2=sv2: e.tensor_scalar(
                        sb2, sv2, EPS_Q, None, OP.add), name=f"{sx}b2ba{i}")
                sqrt_heron(p, sc2, sb2, st1, st2, waits=p.need(f"{sx}b2ba{i}"),
                           name=f"{sx}b2bh{i}")
                p.op("vector", lambda e, sc2=sc2: e.reciprocal(sc2, sc2),
                     name=f"{sx}b2brc{i}")
                p.op("vector", lambda e, hb=hb, sm2=sm2: e.tensor_scalar(
                        xqv, hb, sm2, None, OP.subtract), name=f"{sx}b2bt_{i}")
                p.op("vector", lambda e, smx=smx: e.reduce_max(
                        smx, xqv, AX.X, apply_absolute_value=True),
                     name=f"{sx}b2bmx{i}")
                p.op("vector", lambda e, sc1=sc1, smx=smx: e.reciprocal(sc1, smx))
                p.op("vector", lambda e, sc1=sc1: e.tensor_scalar(
                        sc1, sc1, 128.0, None, OP.mult))
                p.op("vector", lambda e, sg2=sg2, sc2=sc2, smx=smx: e.tensor_mul(
                        sg2, sc2, smx),
                     waits=p.need(f"{sx}b2brc{i}", f"{sx}b2bmx{i}"))
                p.op("vector", lambda e, sg2=sg2: e.tensor_scalar(
                        sg2, sg2, 1.0 / 128.0, None, OP.mult), name=f"{sx}b2bg{i}")
                p.op("vector", lambda e, sc1=sc1: e.tensor_scalar(
                        xqv, xqv, sc1, CLIP, OP.mult, OP.min))
                p.op("vector", lambda e, sg2=sg2: e.tensor_scalar(
                        xqv, xqv, -CLIP, sg2, OP.max, OP.mult),
                     waits=p.need(f"{sx}b2bg{i}"), name=f"{sx}b2bxq{i}")
                p.op("vector", lambda e: e.tensor_copy(xqp[:, 0:8192], xqv),
                     waits=p.need(f"{sx}b2btg{4*(i-1)+3}"))
                p.op("vector", lambda e: e.tensor_tensor(
                        xqp[:, 8192:16384], xqv, xqp[:, 0:8192], OP.subtract),
                     name=f"{sx}b2bpr{i}")
                for g in range(4):
                    gi = 4 * i + g
                    ps = (pAb, pBb)[gi % 2]
                    for l in range(2):
                        for j in range(16):
                            kk = g * 16 + j
                            p.op("tensor",
                                 lambda e, ps=ps, l=l, j=j, kk=kk: e.transpose(
                                    ps[:, l * 2048 + j * 128:
                                       l * 2048 + (j + 1) * 128],
                                    xqp[:, l * 8192 + kk * 128:
                                        l * 8192 + (kk + 1) * 128],
                                    identb.ap()),
                                 waits=p.need(f"{sx}b2bpr{i}",
                                              f"{sx}b2bevg{gi-2}"))
                    p.tmark(f"{sx}b2btg{gi}")
                    p.op("vector", lambda e, ps=ps, g=g: e.tensor_copy(
                            STGb.rearrange("p (l g f) -> p l g f", l=2, g=4)
                            [:, :, g, :],
                            ps[:, 0:4096].rearrange("p (l f) -> p l f", l=2)),
                         waits=p.need(f"{sx}b2btg{gi}",
                                      f"{sx}b2bsp0_{i-1}", f"{sx}b2bsp1_{i-1}"),
                         name=f"{sx}b2bevg{gi}")
                p.alias(f"{sx}b2bev{i}", f"{sx}b2bevg{4*i+3}")
                for l in range(2):
                    p.dma("gpsimd",
                          sx2h2f_h[l, :, :, w0:w0 + 128]
                          .rearrange("k p t -> p k t"),
                          STGb[:, l * 8192:(l + 1) * 8192]
                          .rearrange("p (k t) -> p k t", t=128),
                          waits=p.need(f"{sx}b2bev{i}"),
                          name=f"{sx}b2bsp{l}_{i}", slot=f"a1sp{l}")
            p.barrier()

        if "b3" in stages:
            tp4 = arvb(0, 16384).rearrange("p (k f) -> p k f", k=NK2)
            spb = [arvb(16384, 8192).rearrange("p (l k t) -> p l k t", l=2, t=128),
                   arvb(24576, 8192).rearrange("p (l k t) -> p l k t", l=2, t=128)]
            for qq in range(4):
                p.dma("sync", tp4,
                      tpj_h[:, :, qq * 512:(qq + 1) * 512]
                      .rearrange("k p f -> p k f"),
                      waits=p.need(f"{sx}b3mm{qq*NT-1}"), name=f"{sx}b3t{qq}",
                      slot="wld")
                for i in range(NT):
                    ti = qq * NT + i
                    sp = spb[ti % 2]
                    p.dma("scalar", sp,
                          sx2h2f_h[:, :, :, i * 128:(i + 1) * 128]
                          .rearrange("l k p t -> p l k t"),
                          waits=p.need(f"{sx}b3mm{ti-2}"), name=f"{sx}b3st{ti}",
                          slot=f"st{ti%2}")
                    ps = (pA, pB)[ti % 2]
                    for kk in range(NK2):
                        wmmk = p.need(f"{sx}b3t{qq}", f"{sx}b3st{ti}",
                                      f"{sx}b3ev{ti-2}")
                        for l in range(2):
                            p.op("tensor",
                                 lambda e, ps=ps, sp=sp, kk=kk, l=l: e.matmul(
                                    ps[:, 0:512], sp[:, l, kk, :],
                                    tp4[:, kk, :],
                                    start=(kk == 0 and l == 0),
                                    stop=(kk == NK2 - 1 and l == 1)),
                                 waits=wmmk)
                    p.tmark(f"{sx}b3mm{ti}")
                    p.op("vector", lambda e, ps=ps: e.tensor_scalar(
                            STG[:, 0:512], ps[:, 0:512], gv.ap()[:, 1:2], None,
                            OP.mult),
                         waits=p.need(f"{sx}b3mm{ti}", f"{sx}b3sp{ti-1}"),
                         name=f"{sx}b3ev{ti}")
                    p.dma("gpsimd",
                          soutp_h[i * 128:(i + 1) * 128,
                                  qq * 512:(qq + 1) * 512],
                          STG[:, 0:512],
                          waits=p.need(f"{sx}b3ev{ti}"), name=f"{sx}b3sp{ti}",
                          slot="b3sp")
            p.barrier()

        if "b3b" in stages:
            obuf = [arv(0, 2048), arv(2048, 2048)]
            x2b = [arv(4096, 2048), arv(6144, 2048)]
            scrap3 = arv(8192, 2048)
            for i in range(NT):
                w0 = i * 128
                ob, x2v = obuf[i % 2], x2b[i % 2]
                c = 128 + 12 * (i % 2)
                ssum, ssq = ST[:, c:c+1], ST[:, c+1:c+2]
                sm, sva, srstd = ST[:, c+2:c+3], ST[:, c+3:c+4], ST[:, c+4:c+5]
                st1, st2 = ST[:, c+5:c+6], ST[:, c+6:c+7]
                p.dma("sync", ob, soutp_h[w0:w0 + 128, :],
                      waits=p.need(f"{sx}b3bo{i-2}"), name=f"{sx}b3bl{i}",
                      slot=f"x{i%2}")
                p.dma("sync", x2v, sx2_h[w0:w0 + 128, :],
                      waits=p.need(f"{sx}b3bo{i-2}"), name=f"{sx}b3blx{i}",
                      slot=f"xb{i%2}")
                wl = p.need(f"{sx}b3bl{i}")
                ln_stats(p, ob, ssum, ssq, sm, sva, srstd, st1, st2, 1e-5,
                         wl, f"{sx}b3b{i}", scrap3)
                p.op("vector", lambda e, ob=ob, sm=sm, srstd=srstd:
                     e.tensor_scalar(STG[:, 0:2048], ob, sm, srstd,
                                     OP.subtract, OP.mult),
                     waits=p.need(f"{sx}b3bo{i-1}"))
                p.op("vector", lambda e, x2v=x2v: e.tensor_add(
                        STG[:, 0:2048], STG[:, 0:2048], x2v),
                     waits=p.need(f"{sx}b3blx{i}"), name=f"{sx}b3bst{i}")
                p.dma("gpsimd", oseq[w0:w0 + 128, :], STG[:, 0:2048],
                      waits=p.need(f"{sx}b3bst{i}"), name=f"{sx}b3bo{i}",
                      slot="outs")
            p.barrier()

    p.emit("gpsimd", lambda e: e.nop(),
           [(s, p.v(s)) for s in p.cnt if s.startswith("d_")], None)

    return nc, p, stack


def build_and_realize(cfg):
    nc, p, stack = build_program(cfg)
    with stack:
        p.realize(nc, stack)
    return nc, len(p.ops)


def make_in_map(inputs, core, n_seq=2):
    consts = build_consts()
    x = np.ascontiguousarray(inputs["L0_attn_x"][2 * core:2 * core + n_seq]
                             .reshape(n_seq * S, D))
    m = dict(x=x,
             wq=inputs["L0_attn_Wq"], wk=inputs["L0_attn_Wk"],
             wv=inputs["L0_attn_Wv"], wg=inputs["L0_attn_Wg"],
             wo=inputs["L0_attn_Wo"],
             wfct=np.ascontiguousarray(inputs["L0_ffn_w_fc"].T),
             wpjt=np.ascontiguousarray(inputs["L0_ffn_w_proj"].T),
             gnw=np.ascontiguousarray(inputs["L0_attn_g_norm_w"].reshape(128, 1)),
             **consts)
    return m


def kernel(**inputs):
    inputs = {k: np.asarray(v, dtype=np.float32) for k, v in inputs.items()}
    nc, nops = build_and_realize({})
    in_maps = [make_in_map(inputs, c) for c in range(8)]
    res = run_bass_kernel_spmd(nc, in_maps, core_ids=list(range(8)))
    out = np.concatenate([r["out"].reshape(2, S, D) for r in res.results], axis=0)
    return out


# version 05
# speedup vs baseline: 1.3610x; 1.0281x over previous
import numpy as np
from contextlib import ExitStack

import concourse.bass as bass
import concourse.mybir as mybir
from concourse.bass_utils import run_bass_kernel_spmd

F32 = mybir.dt.float32
BF16 = mybir.dt.bfloat16
AF = mybir.ActivationFunctionType
OP = mybir.AluOpType
AX = mybir.AxisListType

H, DK, DV, C = 16, 128, 128, 128
D, FF, S = 2048, 8192, 2048
NCH = S // C
NT = S // 128
NK = D // 128
NK2 = FF // 128
CLIP = 128.0 - 1e-6
EPS_Q = 1e-6

ENGINES = ("sync", "tensor", "vector", "scalar", "gpsimd")


def build_consts():
    gamma = 1.0 - np.exp2(-5.0 - np.arange(H, dtype=np.float64))
    inv = 1.0 / 10000.0 ** (np.arange(0, DK, 2, dtype=np.float64) / DK)
    ang = np.arange(S, dtype=np.float64)[:, None] * inv[None, :]
    cos, sin = np.cos(ang), np.sin(ang)
    qs = DK ** -0.5
    ropetab = np.stack([cos * qs, sin * qs, cos, sin], axis=1)
    idx = np.arange(C, dtype=np.float64)
    rel = idx[None, :] - idx[:, None]
    dmatT = np.where(rel[:, None, :] >= 0,
                     gamma[None, :, None] ** rel[:, None, :], 0.0)
    dmatT = np.ascontiguousarray(dmatT)
    crossdec = gamma[None, :] ** (idx[:, None] + 1.0)
    kdec = gamma[None, :] ** (C - 1.0 - idx[:, None])
    gC = np.broadcast_to(gamma ** C, (128, H)).copy()
    out = {}
    for nm, v in (("ropetab", ropetab), ("dmatT", dmatT),
                  ("crossdec", crossdec), ("kdec", kdec), ("gC", gC)):
        hi = v.astype(np.float32)
        lo = (v - hi.astype(np.float64)).astype(np.float32)
        out[nm] = hi
        out[nm + "_lo"] = lo
    out["ident"] = np.eye(128, dtype=np.float32)
    out["ones"] = np.ones((128, 1), dtype=np.float32)
    out["ones_row"] = np.ones((1, 128), dtype=np.float32)
    return out


class P:

    def __init__(self):
        self.ops = []
        self.cnt = {}
        self.res = {}
        self.bar = []

    def v(self, sem):
        return self.cnt.get(sem, 0)

    def emit(self, eng, fn, waits=(), inc=None):
        ws = dict(self.bar)
        for (s, v) in waits:
            if v > ws.get(s, 0):
                ws[s] = v
        self.ops.append((eng, fn, tuple((s, v) for s, v in ws.items() if v > 0), inc))
        if inc is not None:
            self.cnt[inc[0]] = self.cnt.get(inc[0], 0) + inc[1]

    def mark(self, name, sem):
        self.res[name] = (sem, self.v(sem))

    def need(self, *names):
        return [self.res[n] for n in names if n in self.res]

    def dma(self, q, out, in_, waits=(), name=None, slot=None):
        sem = "d_" + (slot or (name or "anon"))
        self.emit(q, lambda e, o=out, i=in_: e.dma_start(out=o, in_=i),
                  waits, (sem, 16))
        if name:
            self.mark(name, sem)

    def op(self, eng, fn, waits=(), name=None):
        sem = "e_" + eng
        inc = (sem, 1)
        if eng in ("vector", "scalar", "gpsimd"):
            waits = list(waits) + [(sem, self.v(sem))]
        elif name is None:
            inc = None
        self.emit(eng, fn, waits, inc)
        if name:
            self.mark(name, sem)

    def alias(self, name, other):
        if other in self.res:
            self.res[name] = self.res[other]

    def tmark(self, name):
        for j in range(len(self.ops) - 1, -1, -1):
            eng, fn, ws, inc = self.ops[j]
            if eng == "tensor":
                assert inc is None, f"last tensor op already incs ({name})"
                self.ops[j] = (eng, fn, ws, ("e_tensor", 1))
                self.cnt["e_tensor"] = self.cnt.get("e_tensor", 0) + 1
                self.mark(name, "e_tensor")
                return
        raise AssertionError(f"tmark({name}): no tensor op found")

    def barrier(self):
        self.bar = [(s, self.v(s)) for s in self.cnt]

    def realize(self, nc, stack):
        sems = {s: stack.enter_context(nc.semaphore(s)) for s in self.cnt}
        for (_, _, ws, _) in self.ops:
            for (s, v) in ws:
                assert v <= self.cnt[s], f"wait {s}>={v} max {self.cnt[s]}"
        block = stack.enter_context(nc.Block())
        ops = self.ops

        def mk(eng_name):
            def emitter(e):
                waited = {}
                for (eng2, fn, ws, inc) in ops:
                    if eng2 != eng_name:
                        continue
                    for (s, v) in ws:
                        if waited.get(s, 0) < v:
                            e.wait_ge(sems[s], v)
                            waited[s] = v
                    ins = fn(e)
                    if inc is not None:
                        ins.then_inc(sems[inc[0]], inc[1])
            return emitter

        for eng in ENGINES:
            getattr(block, eng)(mk(eng))
        return sems


def sqrt_heron(p, dst, a, t1, t2, waits=(), name=None):
    p.op("scalar", lambda e: e.activation(dst, a, AF.Sqrt), waits,
         name=name and name + "_s")
    p.op("vector", lambda e: e.reciprocal(t1, dst),
         waits=p.need(name + "_s") if name else ())
    p.op("vector", lambda e: e.tensor_mul(t2, a, t1))
    p.op("vector", lambda e: e.tensor_add(dst, dst, t2))
    p.op("vector", lambda e: e.tensor_scalar(dst, dst, 0.5, None, OP.mult),
         name=name)


def ln_stats(p, src, ssum, ssq, sm, sva, srstd, st1, st2, eps, wl, tag, scrap,
             sq_extra=()):
    W = src.shape[-1]
    p.op("vector", lambda e: e.reduce_sum(ssum, src, AX.X), waits=wl)
    p.op("scalar", lambda e: e.activation(scrap, src, AF.Square, accum_out=ssq),
         waits=list(wl) + list(sq_extra), name=tag + "sq")
    p.op("vector", lambda e: e.tensor_scalar(sm, ssum, 1.0 / W, None, OP.mult))
    p.op("vector", lambda e: e.tensor_mul(sva, sm, sm), waits=p.need(tag + "sq"))
    p.op("vector", lambda e: e.scalar_tensor_tensor(
            sva, ssq, 1.0 / W, sva, OP.mult, OP.subtract))
    p.op("vector", lambda e: e.tensor_scalar(sva, sva, eps, None, OP.add),
         name=tag + "v")
    sqrt_heron(p, srstd, sva, st1, st2, waits=p.need(tag + "v"), name=tag + "h")
    p.op("vector", lambda e: e.reciprocal(srstd, srstd), name=tag + "rs")


def build_program(cfg):
    nc = bass.Bass("TRN2", target_bir_lowering=False, debug=False)
    p = P()
    dbg = set(cfg.get("dbg", ()))
    stages = cfg.get("stages",
                     {"pre", "a1", "a2", "scan", "a4", "b1", "b2a", "b2b",
                      "b3", "b3b"})
    n_seq = cfg.get("n_seq", 2)

    def kind(nm):
        return "ExternalOutput" if nm in dbg else "Internal"

    x_h = nc.dram_tensor("x", [n_seq * S, D], F32, kind="ExternalInput").ap()
    w_h = {w: nc.dram_tensor(w, [D, D], F32, kind="ExternalInput").ap()
           for w in ("wq", "wk", "wv", "wg", "wo")}
    wfct_h = nc.dram_tensor("wfct", [D, FF], F32, kind="ExternalInput").ap()
    wpjt_h = nc.dram_tensor("wpjt", [FF, D], F32, kind="ExternalInput").ap()
    gnw_h = nc.dram_tensor("gnw", [128, 1], F32, kind="ExternalInput").ap()
    ctab_h = nc.dram_tensor("ropetab", [S, 4, 64], F32, kind="ExternalInput").ap()
    ctabl_h = nc.dram_tensor("ropetab_lo", [S, 4, 64], F32, kind="ExternalInput").ap()
    dmatT_h = nc.dram_tensor("dmatT", [128, H, 128], F32, kind="ExternalInput").ap()
    dmatTl_h = nc.dram_tensor("dmatT_lo", [128, H, 128], F32, kind="ExternalInput").ap()
    crossdec_h = nc.dram_tensor("crossdec", [128, H], F32, kind="ExternalInput").ap()
    crossdecl_h = nc.dram_tensor("crossdec_lo", [128, H], F32, kind="ExternalInput").ap()
    kdec_h = nc.dram_tensor("kdec", [128, H], F32, kind="ExternalInput").ap()
    kdecl_h = nc.dram_tensor("kdec_lo", [128, H], F32, kind="ExternalInput").ap()
    gC_h = nc.dram_tensor("gC", [128, H], F32, kind="ExternalInput").ap()
    gCl_h = nc.dram_tensor("gC_lo", [128, H], F32, kind="ExternalInput").ap()
    ident_h = nc.dram_tensor("ident", [128, 128], F32, kind="ExternalInput").ap()
    ones_h = nc.dram_tensor("ones", [128, 1], F32, kind="ExternalInput").ap()
    onesr_h = nc.dram_tensor("ones_row", [1, 128], F32, kind="ExternalInput").ap()
    out_h = nc.dram_tensor("out", [n_seq * S, D], F32, kind="ExternalOutput").ap()

    tfc_h = nc.dram_tensor("s_tfc", [NK, 128, FF], BF16, kind=kind("s_tfc")).ap()
    tpj_h = nc.dram_tensor("s_tpj", [NK2, 128, D], BF16, kind=kind("s_tpj")).ap()
    gv_h = nc.dram_tensor("s_gv", [128, 4], F32, kind=kind("s_gv")).ap()
    xnf_h = nc.dram_tensor("s_xnf", [2, NK, 128, S], BF16, kind=kind("s_xnf")).ap()
    sq_h = nc.dram_tensor("s_q", [S, D], F32, kind=kind("s_q")).ap()
    sk_h = nc.dram_tensor("s_k", [S, D], F32, kind=kind("s_k")).ap()
    sv_h = nc.dram_tensor("s_v", [S, D], F32, kind=kind("s_v")).ap()
    sg_h = nc.dram_tensor("s_g", [S, D], F32, kind=kind("s_g")).ap()
    sogf_h = nc.dram_tensor("s_ogf", [2, NK, 128, S], BF16, kind=kind("s_ogf")).ap()
    sx2_h = nc.dram_tensor("s_x2", [S, D], F32, kind=kind("s_x2")).ap()
    sx2hf_h = nc.dram_tensor("s_x2hf", [2, NK, 128, S], BF16,
                             kind=kind("s_x2hf")).ap()
    sh2_h = nc.dram_tensor("s_h2", [S, FF], F32, kind=kind("s_h2")).ap()
    sx2h2f_h = nc.dram_tensor("s_x2h2f", [2, NK2, 128, S], BF16,
                              kind=kind("s_x2h2f")).ap()
    soutp_h = nc.dram_tensor("s_outp", [S, D], F32, kind=kind("s_outp")).ap()

    stack = ExitStack()
    e = stack.enter_context
    ident = e(nc.sbuf_tensor("ident_s", [128, 128], F32))
    ones = e(nc.sbuf_tensor("ones_s", [128, 1], F32))
    onesr = e(nc.sbuf_tensor("onesr", [1, 128], F32))
    gnw = e(nc.sbuf_tensor("gnw_s", [128, 1], F32))
    gv = e(nc.sbuf_tensor("gv_s", [128, 4], F32))
    stats = e(nc.sbuf_tensor("stats", [128, 256], F32))
    stat2b = e(nc.sbuf_tensor("stat2b", [128, 2, 2, 16, 128], BF16))
    pA = e(nc.psum_tensor([128, 2048], F32))
    pB = e(nc.psum_tensor([128, 2048], F32))
    ST = stats.ap()

    def hview(x):
        return x.rearrange("p (h f) -> p h f", h=H)

    p.dma("sync", gnw.ap(), gnw_h)
    p.dma("sync", ident.ap(), ident_h)
    p.dma("sync", ones.ap(), ones_h)
    p.dma("sync", onesr.ap(), onesr_h, name="consts")
    p.barrier()

    def assemble(tbl, hi_h, lo_h, scr, tag):
        p.dma("sync", tbl, hi_h, name=tag + "h", slot="cah")
        p.dma("sync", scr, lo_h, name=tag + "l", slot="cal")
        p.op("vector", lambda e: e.tensor_add(tbl, tbl, scr),
             waits=p.need(tag + "h", tag + "l"), name=tag)

    seqctr = [0]

    if "pre" in stages:
      with ExitStack() as ph:
        pe = ph.enter_context
        wbuf_t = [pe(nc.sbuf_tensor(f"pre_w{j}", [128, 8192], F32))
                  for j in range(2)]
        zb_t = pe(nc.sbuf_tensor("pre_z", [128, 8192], F32))
        tst_t = [pe(nc.sbuf_tensor(f"pre_t{j}", [128, 8192], BF16))
                 for j in range(2)]
        wbuf = [t.ap() for t in wbuf_t]
        zbf = zb_t.ap()
        tstage = [t.ap() for t in tst_t]
        acc = ST[:, 0:2]
        tmp1 = ST[:, 2:3]
        p.op("vector", lambda e: e.memset(acc, 0.0), name="accz")
        specs = (("fc", wfct_h, FF, NK), ("pj", wpjt_h, D, NK2))
        gt = 0
        for wi, (wn, src, width, ntl) in enumerate(specs):
            col = acc[:, wi:wi + 1]
            for t in range(ntl):
                bv = wbuf[gt % 2][:, 0:width]
                p.dma("sync", bv, src[t * 128:(t + 1) * 128, :],
                      waits=p.need(f"prd_{gt-2}"), name=f"pld_{gt}",
                      slot=f"pw{gt%2}")
                p.op("vector", lambda e, bv=bv: e.tensor_reduce(
                        tmp1, bv, AX.X, OP.add, apply_absolute_value=True),
                     waits=p.need(f"pld_{gt}", "accz"))
                p.op("vector", lambda e, col=col: e.tensor_add(col, col, tmp1),
                     name=f"prd_{gt}")
                gt += 1
        p.op("tensor", lambda e: e.matmul(pA[0:1, 0:2], ones.ap(), acc,
                                          start=True, stop=True),
             waits=p.need("prd_78", "prd_79"))
        p.tmark("pps")
        p.op("vector", lambda e: e.tensor_copy(ST[0:1, 4:6], pA[0:1, 0:2]),
             waits=p.need("pps"), name="pg0")
        NW = float(D * FF)
        p.op("vector", lambda e: e.tensor_scalar(
                ST[0:1, 4:6], ST[0:1, 4:6], 1.0 / NW, None, OP.mult),
             waits=p.need("pg0"))
        p.op("vector", lambda e: e.tensor_scalar(
                ST[0:1, 6:8], ST[0:1, 4:6], EPS_Q, None, OP.add))
        p.op("vector", lambda e: e.reciprocal(ST[0:1, 6:8], ST[0:1, 6:8]),
             name="pg1")
        p.op("tensor", lambda e: e.matmul(pA[:, 0:4], onesr.ap(), ST[0:1, 4:8],
                                          start=True, stop=True),
             waits=p.need("pg1"))
        p.tmark("pbc")
        p.op("vector", lambda e: e.tensor_copy(gv.ap(), pA[:, 0:4]),
             waits=p.need("pbc"), name="gv")
        p.dma("gpsimd", gv_h, gv.ap(), waits=p.need("gv"), slot="gvs")
        ifc, ipj = gv.ap()[:, 2:3], gv.ap()[:, 3:4]
        gt = 0
        for (wn, src, width, ntl), inv, dst in zip(specs, (ifc, ipj),
                                                   (tfc_h, tpj_h)):
            for t in range(ntl):
                bv = wbuf[gt % 2][:, 0:width]
                ts_ = tstage[gt % 2][:, 0:width]
                zb = zbf[:, 0:width]
                p.dma("sync", bv, src[t * 128:(t + 1) * 128, :],
                      waits=p.need(f"p2z_{gt-2}", "prd_78", "prd_79"),
                      name=f"p2l_{gt}", slot=f"pw{gt%2}")
                p.op("scalar", lambda e, bv=bv, zb=zb, inv=inv: e.activation(
                        zb, bv, AF.Identity, bias=0.0, scale=inv),
                     waits=p.need(f"p2l_{gt}", "gv", f"p2b_{gt-1}"),
                     name=f"p2z_{gt}")
                p.op("vector", lambda e, zb=zb, ts_=ts_: e.tensor_scalar(
                        ts_, zb, -0.5, None, OP.is_lt),
                     waits=p.need(f"p2z_{gt}", f"p2s_{gt-2}"))
                p.op("vector", lambda e, zb=zb, ts_=ts_: e.scalar_tensor_tensor(
                        ts_, zb, 0.5, ts_, OP.is_gt, OP.subtract),
                     name=f"p2b_{gt}")
                p.dma("gpsimd", dst[t], ts_,
                      waits=p.need(f"p2b_{gt}"), name=f"p2s_{gt}",
                      slot=f"pt{gt%2}")
                gt += 1
        p.barrier()

    for sq in range(n_seq):
        sx = f"s{sq}_"
        xseq = x_h[sq * S:(sq + 1) * S, :]
        oseq = out_h[sq * S:(sq + 1) * S, :]

        def build_wpair(tag, wap, wph, wpl, wst, scale_ap=None, extra=()):
            for kk in range(NK):
                ex = list(extra) if kk < 2 else []
                p.dma("sync", wst, wap[kk * 128:(kk + 1) * 128, :],
                      waits=p.need(f"{tag}l{kk-1}") + ex,
                      name=f"{tag}ld{kk}", slot="wld")
                wv = p.need(f"{tag}ld{kk}")
                if scale_ap is None:
                    p.dma("gpsimd", wph[:, kk, :],
                          wap[kk * 128:(kk + 1) * 128, :],
                          waits=p.need(f"{tag}hc{kk-2}") + ex,
                          name=f"{tag}hc{kk}", slot=f"whc{kk%2}")
                    eng = ("vector", "gpsimd")[kk % 2]
                    p.op(eng, lambda e, kk=kk: e.tensor_tensor(
                            wpl[:, kk, :], wst, wph[:, kk, :], OP.subtract),
                         waits=wv + p.need(f"{tag}hc{kk}") + ex,
                         name=f"{tag}ls{kk}")
                    p.mark(f"{tag}l{kk}", "e_" + eng)
                else:
                    p.op("vector", lambda e, kk=kk: e.tensor_scalar(
                            wph[:, kk, :], wst, scale_ap, None, OP.mult),
                         waits=wv + ex)
                    p.op("vector", lambda e, kk=kk: e.scalar_tensor_tensor(
                            wpl[:, kk, :], wst, scale_ap, wph[:, kk, :],
                            OP.mult, OP.subtract),
                         name=f"{tag}l{kk}")

        if "a1" in stages:
         def _ph(sx=sx, xseq=xseq, oseq=oseq, build_wpair=build_wpair):
          with ExitStack() as ph:
            pe_ = ph.enter_context
            xb_t = [pe_(nc.sbuf_tensor(f"a1x{sq}_{j}", [128, 2048], F32))
                    for j in range(2)]
            xn_t = [pe_(nc.sbuf_tensor(f"a1n{sq}_{j}", [128, 2048], F32))
                    for j in range(2)]
            scr_t = pe_(nc.sbuf_tensor(f"a1s{sq}", [128, 2048], F32))
            bst_t = [pe_(nc.sbuf_tensor(f"a1b{sq}_{j}", [128, 2, 2048], BF16))
                     for j in range(2)]
            xbuf = [t.ap() for t in xb_t]
            xnb = [t.ap() for t in xn_t]
            scrap = scr_t.ap()
            bst = [t.ap() for t in bst_t]
            for i in range(NT):
                xb = xbuf[i % 2]
                c = 16 * (i % 2)
                ssum, ssq, sm = ST[:, c:c+1], ST[:, c+1:c+2], ST[:, c+2:c+3]
                sva, srstd = ST[:, c+3:c+4], ST[:, c+4:c+5]
                snmr, st1, st2 = ST[:, c+5:c+6], ST[:, c+6:c+7], ST[:, c+7:c+8]
                p.dma("sync", xb, xseq[i * 128:(i + 1) * 128, :],
                      waits=p.need(f"{sx}a1xn{i-2}"), name=f"{sx}a1x{i}",
                      slot=f"x{i%2}")
                wl = p.need(f"{sx}a1x{i}")
                ln_stats(p, xb, ssum, ssq, sm, sva, srstd, st1, st2, 1e-5,
                         wl, f"{sx}a1s{i}", scrap)
                p.op("vector", lambda e, snmr=snmr, sm=sm, srstd=srstd:
                     e.tensor_mul(snmr, sm, srstd))
                p.op("vector", lambda e, snmr=snmr: e.tensor_scalar(
                        snmr, snmr, -1.0, None, OP.mult), name=f"{sx}a1st{i}")
                xt = xnb[i % 2]
                p.op("scalar", lambda e, xb=xb, xt=xt, snmr=snmr, srstd=srstd:
                     e.activation(xt, xb, AF.Identity, bias=snmr, scale=srstd),
                     waits=p.need(f"{sx}a1st{i}", f"{sx}a1tg{i-2}"),
                     name=f"{sx}a1xn{i}")
                ps = (pA, pB)[i % 2]
                for kk in range(16):
                    p.op("tensor", lambda e, ps=ps, kk=kk, xt=xt: e.transpose(
                            ps[:, kk * 128:(kk + 1) * 128],
                            xt[:, kk * 128:(kk + 1) * 128], ident.ap()),
                         waits=p.need(f"{sx}a1xn{i}", f"{sx}a1ev{i-2}"))
                p.tmark(f"{sx}a1tg{i}")
                bs = bst[i % 2]
                p.op("vector", lambda e, ps=ps, bs=bs: e.tensor_copy(
                        bs[:, 0, :], ps[:, :]),
                     waits=p.need(f"{sx}a1tg{i}", f"{sx}a1sp0_{i-2}",
                                  f"{sx}a1sp1_{i-2}"))
                p.op("vector", lambda e, ps=ps, bs=bs: e.tensor_tensor(
                        bs[:, 1, :], ps[:, :], bs[:, 0, :], OP.subtract),
                     name=f"{sx}a1ev{i}")
                for l in range(2):
                    p.dma("gpsimd",
                          xnf_h[l, :, :, i * 128:(i + 1) * 128]
                          .rearrange("k p t -> p k t"),
                          bs[:, l, :].rearrange("p (k t) -> p k t", t=128),
                          waits=p.need(f"{sx}a1ev{i}"),
                          name=f"{sx}a1sp{l}_{i}", slot=f"a1sp{l}{i%2}")
            p.barrier()

         _ph()
        if "a2" in stages:
         def _ph(sx=sx, xseq=xseq, oseq=oseq, build_wpair=build_wpair):
          with ExitStack() as ph:
            pe_ = ph.enter_context
            ctab_t = pe_(nc.sbuf_tensor(f"ctab{sq}", [128, NT, 4, 64], F32))
            ctab = ctab_t.ap()
            with nc.sbuf_tensor(f"clo{sq}", [128, NT, 4, 64], F32) as clo_t:
                clo = clo_t.ap()
                p.dma("sync", ctab,
                      ctab_h.rearrange("(t p) j f -> p t j f", p=128),
                      name=f"{sx}cth", slot="cah")
                p.dma("sync", clo,
                      ctabl_h.rearrange("(t p) j f -> p t j f", p=128),
                      name=f"{sx}ctl", slot="cal")
                p.op("vector", lambda e: e.tensor_add(ctab, ctab, clo),
                     waits=p.need(f"{sx}cth", f"{sx}ctl"), name=f"{sx}ctA")
            p.barrier()
            wph_t = pe_(nc.sbuf_tensor(f"a2wh{sq}", [128, NK, 2048], BF16))
            wpl_t = pe_(nc.sbuf_tensor(f"a2wl{sq}", [128, NK, 2048], BF16))
            wst_t = pe_(nc.sbuf_tensor(f"a2ws{sq}", [128, 2048], F32))
            qc_t = pe_(nc.sbuf_tensor(f"a2qc{sq}", [128, 2048], F32))
            qs_t = pe_(nc.sbuf_tensor(f"a2qs{sq}", [128, 2048], F32))
            rv_t = pe_(nc.sbuf_tensor(f"a2rv{sq}", [128, 2048], F32))
            wph = wph_t.ap()
            wpl = wpl_t.ap()
            wstv = wst_t.ap()
            qcv, qsv, rvv = qc_t.ap(), qs_t.ap(), rv_t.ap()
            projs = (("wq", sq_h, 0), ("wk", sk_h, 2), ("wv", sv_h, None),
                     ("wg", sg_h, None))
            for pi, (wn, dst, coff) in enumerate(projs):
                build_wpair(f"{sx}a2w{pi}_", w_h[wn], wph, wpl, wstv,
                            extra=p.need(f"{sx}a2mmdone{pi-1}",
                                         f"{sx}a2ev{pi*NT-1}"))
                for i in range(NT):
                    ti = pi * NT + i
                    sp = stat2b.ap()[:, i % 2]
                    p.dma("scalar", sp,
                          xnf_h[:, :, :, i * 128:(i + 1) * 128]
                          .rearrange("l k p t -> p l k t"),
                          waits=p.need(f"{sx}a2mm{ti-2}"), name=f"{sx}a2st{ti}",
                          slot=f"st{ti%2}")
                    ps = (pA, pB)[i % 2]
                    for kk in range(NK):
                        wmmk = p.need(f"{sx}a2w{pi}_l{kk}", f"{sx}a2st{ti}",
                                      f"{sx}a2ev{ti-2}")
                        for idx_, (l, wv_) in enumerate(
                                ((0, wph), (0, wpl), (1, wph))):
                            for g in range(4):
                                p.op("tensor",
                                     lambda e, ps=ps, sp=sp, kk=kk, g=g, l=l,
                                     wv_=wv_, idx_=idx_: e.matmul(
                                        ps[:, g * 512:(g + 1) * 512],
                                        sp[:, l, kk, :],
                                        wv_[:, kk, g * 512:(g + 1) * 512],
                                        start=(kk == 0 and idx_ == 0),
                                        stop=(kk == NK - 1 and idx_ == 2)),
                                     waits=wmmk)
                    p.tmark(f"{sx}a2mm{ti}")
                    wmm = p.need(f"{sx}a2mm{ti}", f"{sx}a2sp{ti-1}")
                    if coff is None:
                        if wn == "wv":
                            p.op("vector", lambda e, ps=ps: e.tensor_copy(
                                    qcv, ps[:, :]),
                                 waits=wmm, name=f"{sx}a2ev{ti}")
                        else:
                            p.op("scalar", lambda e, ps=ps: e.activation(
                                    qcv, ps[:, :], AF.Silu),
                                 waits=wmm, name=f"{sx}a2ev{ti}")
                    else:
                        cB = ctab[:, i, coff, :].rearrange("p f -> p () f") \
                            .broadcast_to([128, H, 64])
                        sB = ctab[:, i, coff + 1, :].rearrange("p f -> p () f") \
                            .broadcast_to([128, H, 64])
                        rv = hview(rvv)
                        qc = hview(qcv)
                        qs_ = hview(qsv)
                        ov = hview(qcv)
                        p.op("scalar", lambda e, ps=ps: e.activation(
                                rvv, ps[:, :], AF.Identity),
                             waits=wmm + p.need(f"{sx}a2pl{ti-1}"),
                             name=f"{sx}a2rv{ti}")
                        wrv = p.need(f"{sx}a2rv{ti}", f"{sx}ctA")
                        p.op("vector", lambda e, qc=qc, cB=cB, rv=rv:
                             e.tensor_tensor(qc[:, :, 0:64], rv[:, :, 0:64],
                                             cB, OP.mult), waits=wrv + wmm)
                        p.op("vector", lambda e, qc=qc, cB=cB, rv=rv:
                             e.tensor_tensor(qc[:, :, 64:128], rv[:, :, 64:128],
                                             cB, OP.mult))
                        p.op("gpsimd", lambda e, qs_=qs_, sB=sB, rv=rv:
                             e.tensor_tensor(qs_[:, :, 0:64], rv[:, :, 64:128],
                                             sB, OP.mult),
                             waits=wrv + p.need(f"{sx}a2ev{ti-1}"),
                             name=f"{sx}a2s1_{ti}")
                        p.op("gpsimd", lambda e, qs_=qs_, sB=sB, rv=rv:
                             e.tensor_tensor(qs_[:, :, 64:128], rv[:, :, 0:64],
                                             sB, OP.mult),
                             name=f"{sx}a2s2_{ti}")
                        p.op("vector", lambda e, ov=ov, qc=qc, qs_=qs_:
                             e.tensor_sub(ov[:, :, 0:64], qc[:, :, 0:64],
                                          qs_[:, :, 0:64]),
                             waits=p.need(f"{sx}a2s1_{ti}"))
                        p.op("vector", lambda e, ov=ov, qc=qc, qs_=qs_:
                             e.tensor_add(ov[:, :, 64:128], qc[:, :, 64:128],
                                          qs_[:, :, 64:128]),
                             waits=p.need(f"{sx}a2s2_{ti}"),
                             name=f"{sx}a2ev{ti}")
                        p.alias(f"{sx}a2pl{ti}", f"{sx}a2ev{ti}")
                    p.dma("gpsimd", dst[i * 128:(i + 1) * 128, :], qcv,
                          waits=p.need(f"{sx}a2ev{ti}"), name=f"{sx}a2sp{ti}",
                          slot="a2sp")
                p.mark(f"{sx}a2mmdone{pi}", "e_tensor")
            p.barrier()

         _ph()
        if "scan" in stages:
         def _ph(sx=sx, xseq=xseq, oseq=oseq, build_wpair=build_wpair):
          with ExitStack() as ph:
            pe_ = ph.enter_context
            dmat_t = pe_(nc.sbuf_tensor(f"dmat{sq}", [128, H, 128], F32))
            dec_t = pe_(nc.sbuf_tensor(f"dec{sq}", [128, 3, H], F32))
            dmat = dmat_t.ap()
            dec = dec_t.ap()
            with nc.sbuf_tensor(f"dlo{sq}", [128, H, 128], F32) as dlo_t:
                dlo = dlo_t.ap()
                assemble(dmat, dmatT_h, dmatTl_h, dlo, f"{sx}dmA")
                p.dma("sync", dec[:, 0, :], crossdec_h, name=f"{sx}dc0h", slot="ca2")
                p.dma("sync", dec[:, 1, :], kdec_h, name=f"{sx}dc1h", slot="ca4")
                p.dma("sync", dec[:, 2, :], gC_h, name=f"{sx}dc2h", slot="ca6")
                p.dma("sync", dlo[:, 0, 0:H], crossdecl_h,
                      waits=p.need(f"{sx}dmA"), name=f"{sx}dc0l", slot="ca3")
                p.dma("sync", dlo[:, 1, 0:H], kdecl_h, waits=p.need(f"{sx}dmA"), name=f"{sx}dc1l", slot="ca5")
                p.dma("sync", dlo[:, 2, 0:H], gCl_h, waits=p.need(f"{sx}dmA"), name=f"{sx}dc2l", slot="ca7")
                p.op("vector", lambda e: e.tensor_add(
                        dec[:, 0, :], dec[:, 0, :], dlo[:, 0, 0:H]),
                     waits=p.need(f"{sx}dc0h", f"{sx}dc0l"))
                p.op("vector", lambda e: e.tensor_add(
                        dec[:, 1, :], dec[:, 1, :], dlo[:, 1, 0:H]),
                     waits=p.need(f"{sx}dc1h", f"{sx}dc1l"))
                p.op("vector", lambda e: e.tensor_add(
                        dec[:, 2, :], dec[:, 2, :], dlo[:, 2, 0:H]),
                     waits=p.need(f"{sx}dc2h", f"{sx}dc2l"), name=f"{sx}dcA")
            p.barrier()
            crossB = dec[:, 0:1, :].rearrange("p o h -> p h o") \
                .broadcast_to([128, H, 128])
            kdecB = dec[:, 1:2, :].rearrange("p o h -> p h o") \
                .broadcast_to([128, H, 128])
            gCB = dec[:, 2:3, :].rearrange("p o h -> p h o") \
                .broadcast_to([128, H, 128])
            nmses = ["qb0", "qb1", "kb0", "kb1", "vb0", "vb1", "gb0", "gb1",
                     "Sst", "qf", "kf", "Asb", "k2", "ocmb", "ogst", "scr2"]
            bufs = {nm: pe_(nc.sbuf_tensor(f"sc{sq}_{nm}", [128, 2048], F32)).ap()
                    for nm in nmses}
            ogb_t = pe_(nc.sbuf_tensor(f"sc{sq}_ogb", [128, 2, 2048], BF16))
            ogb = ogb_t.ap()
            qb = [bufs["qb0"], bufs["qb1"]]
            kb = [bufs["kb0"], bufs["kb1"]]
            vb = [bufs["vb0"], bufs["vb1"]]
            gb = [bufs["gb0"], bufs["gb1"]]
            Sst, qf, kf = bufs["Sst"], bufs["qf"], bufs["kf"]
            Asb, k2, ocmb = bufs["Asb"], bufs["k2"], bufs["ocmb"]
            ogst, scrap2 = bufs["ogst"], bufs["scr2"]
            p.op("vector", lambda e: e.memset(Sst, 0.0), name=f"{sx}scS")
            for i in range(NCH):
                w0 = i * 128
                p.dma("sync", qb[i % 2], sq_h[w0:w0 + 128, :],
                      waits=p.need(f"{sx}scTq{i-2}"), name=f"{sx}sclq{i}",
                      slot=f"sq{i%2}")
                p.dma("sync", kb[i % 2], sk_h[w0:w0 + 128, :],
                      waits=p.need(f"{sx}sck2_{i-2}", f"{sx}scTk{i-2}"),
                      name=f"{sx}sclk{i}", slot=f"sk{i%2}")
                p.dma("sync", vb[i % 2], sv_h[w0:w0 + 128, :],
                      waits=p.need(f"{sx}scDel{i-2}"), name=f"{sx}sclv{i}",
                      slot=f"sv{i%2}")
                p.dma("scalar", gb[i % 2], sg_h[w0:w0 + 128, :],
                      waits=p.need(f"{sx}scog{i-2}"), name=f"{sx}sclg{i}",
                      slot=f"sg{i%2}")
                q_tm, k_tm = qb[i % 2], kb[i % 2]
                v_tm, sg_tm = vb[i % 2], gb[i % 2]
                for (src, ps, dst_, nm, wx) in (
                        (q_tm, pA, qf, "q", f"{sx}scSup{i-1}"),
                        (k_tm, pB, kf, "k", f"{sx}scogev{i-1}")):
                    for hh in range(H):
                        p.op("tensor", lambda e, src=src, ps=ps, hh=hh:
                             e.transpose(ps[:, hh * 128:(hh + 1) * 128],
                                         src[:, hh * 128:(hh + 1) * 128],
                                         ident.ap()),
                             waits=p.need(f"{sx}scl{nm}{i}", wx,
                                          f"{sx}scT{nm}ev{i-1}"))
                    p.tmark(f"{sx}scT{nm}{i}")
                    p.op("scalar", lambda e, ps=ps, dst_=dst_: e.activation(
                            dst_, ps[:, :], AF.Identity),
                         waits=p.need(f"{sx}scT{nm}{i}"),
                         name=f"{sx}scT{nm}ev{i}")
                for hh in range(H):
                    p.op("tensor", lambda e, hh=hh: e.matmul(
                            pA[:, hh * 128:(hh + 1) * 128],
                            kf[:, hh * 128:(hh + 1) * 128],
                            qf[:, hh * 128:(hh + 1) * 128],
                            start=(hh % 4 == 0), stop=(hh % 4 == 3)),
                         waits=p.need(f"{sx}scTqev{i}", f"{sx}scTkev{i}"))
                p.tmark(f"{sx}scA{i}")
                p.op("vector", lambda e: e.tensor_tensor(
                        hview(Asb), hview(pA[:, :]), dmat, OP.mult),
                     waits=p.need(f"{sx}scA{i}", f"{sx}dmA"),
                     name=f"{sx}scAm{i}")
                for hh in range(H):
                    p.op("tensor", lambda e, hh=hh, v_tm=v_tm: e.matmul(
                            pB[:, hh * 128:(hh + 1) * 128],
                            Asb[:, hh * 128:(hh + 1) * 128],
                            v_tm[:, hh * 128:(hh + 1) * 128],
                            start=(hh % 4 == 0), stop=(hh % 4 == 3)),
                         waits=p.need(f"{sx}scAm{i}", f"{sx}sclv{i}",
                                      f"{sx}scTkev{i}"))
                p.tmark(f"{sx}scI{i}")
                for hh in range(H):
                    p.op("tensor", lambda e, hh=hh: e.matmul(
                            pA[:, hh * 128:(hh + 1) * 128],
                            qf[:, hh * 128:(hh + 1) * 128],
                            Sst[:, hh * 128:(hh + 1) * 128],
                            start=(hh % 4 == 0), stop=(hh % 4 == 3)),
                         waits=p.need(f"{sx}scAm{i}",
                                      f"{sx}scS" if i == 0 else f"{sx}scSup{i-1}"))
                p.tmark(f"{sx}scX{i}")
                p.op("vector", lambda e: e.tensor_tensor(
                        hview(ocmb), hview(pA[:, :]), crossB, OP.mult),
                     waits=p.need(f"{sx}scX{i}", f"{sx}dcA"),
                     name=f"{sx}scOc1{i}")
                p.op("vector", lambda e: e.tensor_add(ocmb, ocmb, pB[:, :]),
                     waits=p.need(f"{sx}scI{i}"), name=f"{sx}scOc{i}")
                p.op("gpsimd", lambda e, k_tm=k_tm: e.tensor_tensor(
                        hview(k2), hview(k_tm), kdecB, OP.mult),
                     waits=p.need(f"{sx}sclk{i}", f"{sx}scSup{i-1}",
                                  f"{sx}dcA"),
                     name=f"{sx}sck2_{i}")
                for hh in range(H):
                    p.op("tensor", lambda e, hh=hh, v_tm=v_tm: e.matmul(
                            pA[:, hh * 128:(hh + 1) * 128],
                            k2[:, hh * 128:(hh + 1) * 128],
                            v_tm[:, hh * 128:(hh + 1) * 128],
                            start=(hh % 4 == 0), stop=(hh % 4 == 3)),
                         waits=p.need(f"{sx}sck2_{i}", f"{sx}scOc1{i}"))
                p.tmark(f"{sx}scDel{i}")
                p.op("vector", lambda e: e.tensor_tensor(
                        hview(Sst), hview(Sst), gCB, OP.mult),
                     waits=p.need(f"{sx}scX{i}", f"{sx}scDel{i}"))
                p.op("vector", lambda e: e.tensor_add(Sst, Sst, pA[:, :]),
                     name=f"{sx}scSup{i}")
                c = 48
                st16 = ST[:, c:c + 16]
                sa = ST[:, c + 16:c + 32]
                sh1 = ST[:, c + 32:c + 48]
                sh2v = ST[:, c + 48:c + 64]
                p.op("scalar", lambda e: e.activation(
                        scrap2, ocmb, AF.Square),
                     waits=p.need(f"{sx}scOc{i}", f"{sx}scogT{i-1}"),
                     name=f"{sx}scsq2_{i}")
                p.op("vector", lambda e, st16=st16: e.tensor_reduce(
                        st16, hview(scrap2), AX.X, OP.add),
                     waits=p.need(f"{sx}scrms{i-1}", f"{sx}scsq2_{i}"))
                p.op("vector", lambda e, sa=sa, st16=st16: e.tensor_scalar(
                        sa, st16, 1.0 / DV, None, OP.mult))
                p.op("vector", lambda e, sa=sa: e.tensor_scalar(
                        sa, sa, 1e-5, None, OP.add), name=f"{sx}scva{i}")
                sqrt_heron(p, st16, sa, sh1, sh2v, waits=p.need(f"{sx}scva{i}"),
                           name=f"{sx}scsq{i}")
                p.op("vector", lambda e, st16=st16: e.reciprocal(st16, st16),
                     name=f"{sx}scrms{i}")
                rstdB = st16.rearrange("p h -> p h ()").broadcast_to([128, H, 128])
                p.op("vector", lambda e, rstdB=rstdB: e.tensor_tensor(
                        hview(ogst), hview(ocmb), rstdB, OP.mult),
                     waits=p.need(f"{sx}scogT{i-1}"))
                p.op("vector", lambda e, sg_tm=sg_tm: e.tensor_mul(
                        ogst, ogst, sg_tm),
                     waits=p.need(f"{sx}sclg{i}"), name=f"{sx}scogv{i}")
                p.alias(f"{sx}scog{i}", f"{sx}scogv{i}")
                for kk in range(16):
                    p.op("tensor", lambda e, kk=kk: e.transpose(
                            pB[:, kk * 128:(kk + 1) * 128],
                            ogst[:, kk * 128:(kk + 1) * 128], ident.ap()),
                         waits=p.need(f"{sx}scogv{i}", f"{sx}scOc{i}",
                                      f"{sx}scogev{i-1}"))
                p.tmark(f"{sx}scogT{i}")
                p.op("vector", lambda e: e.tensor_copy(ogb[:, 0, :], pB[:, :]),
                     waits=p.need(f"{sx}scogT{i}", f"{sx}scsp0_{i-1}",
                                  f"{sx}scsp1_{i-1}"))
                p.op("vector", lambda e: e.tensor_tensor(
                        ogb[:, 1, :], pB[:, :], ogb[:, 0, :], OP.subtract),
                     name=f"{sx}scogev{i}")
                for l in range(2):
                    p.dma("gpsimd",
                          sogf_h[l, :, :, w0:w0 + 128]
                          .rearrange("k p t -> p k t"),
                          ogb[:, l, :].rearrange("p (k t) -> p k t", t=128),
                          waits=p.need(f"{sx}scogev{i}"),
                          name=f"{sx}scsp{l}_{i}", slot=f"scsp{l}")
            p.barrier()

         _ph()
        if "a4" in stages:
         def _ph(sx=sx, xseq=xseq, oseq=oseq, build_wpair=build_wpair):
          with ExitStack() as ph:
            pe_ = ph.enter_context
            wph = pe_(nc.sbuf_tensor(f"a4wh{sq}", [128, NK, 2048], BF16)).ap()
            wpl = pe_(nc.sbuf_tensor(f"a4wl{sq}", [128, NK, 2048], BF16)).ap()
            wstv = pe_(nc.sbuf_tensor(f"a4ws{sq}", [128, 2048], F32)).ap()
            evst = pe_(nc.sbuf_tensor(f"a4ev{sq}", [128, 2048], F32)).ap()
            build_wpair(f"{sx}a4w_", w_h["wo"], wph, wpl, wstv,
                        scale_ap=gnw.ap())
            for i in range(NT):
                w0 = i * 128
                sp = stat2b.ap()[:, i % 2]
                p.dma("scalar", sp,
                      sogf_h[:, :, :, w0:w0 + 128]
                      .rearrange("l k p t -> p l k t"),
                      waits=p.need(f"{sx}a4mm{i-2}"), name=f"{sx}a4st{i}",
                      slot=f"st{i%2}")
                ps = (pA, pB)[i % 2]
                for kk in range(NK):
                    wmmk = p.need(f"{sx}a4w_l{kk}", f"{sx}a4st{i}",
                                  f"{sx}a4ev{i-2}")
                    for idx_, (l, wv_) in enumerate(((0, wph), (0, wpl), (1, wph))):
                        for g in range(4):
                            p.op("tensor",
                                 lambda e, ps=ps, sp=sp, kk=kk, g=g, l=l,
                                 wv_=wv_, idx_=idx_: e.matmul(
                                    ps[:, g * 512:(g + 1) * 512],
                                    sp[:, l, kk, :],
                                    wv_[:, kk, g * 512:(g + 1) * 512],
                                    start=(kk == 0 and idx_ == 0),
                                    stop=(kk == NK - 1 and idx_ == 2)),
                                 waits=wmmk)
                p.tmark(f"{sx}a4mm{i}")
                p.op("vector", lambda e, ps=ps: e.tensor_copy(evst, ps[:, :]),
                     waits=p.need(f"{sx}a4mm{i}", f"{sx}a4sp{i-1}"),
                     name=f"{sx}a4ev{i}")
                p.dma("gpsimd", soutp_h[w0:w0 + 128, :], evst,
                      waits=p.need(f"{sx}a4ev{i}"), name=f"{sx}a4sp{i}",
                      slot="a2sp")
            p.barrier()

         _ph()
        if "b1" in stages:
         def _ph(sx=sx, xseq=xseq, oseq=oseq, build_wpair=build_wpair):
          with ExitStack() as ph:
            pe_ = ph.enter_context
            def mk2(nm):
                return [pe_(nc.sbuf_tensor(f"b1{sq}{nm}{j}", [128, 2048],
                                           F32)).ap() for j in range(2)]
            abuf, xbuf, x2b, tb, xqb2 = mk2("a"), mk2("x"), mk2("y"), mk2("t"), mk2("q")
            scrapb1 = pe_(nc.sbuf_tensor(f"b1{sq}s", [128, 2048], F32)).ap()
            bst = [pe_(nc.sbuf_tensor(f"b1{sq}b{j}", [128, 2, 2048], BF16)).ap()
                   for j in range(2)]
            for i in range(NT):
                w0 = i * 128
                ab, xb, x2v = abuf[i % 2], xbuf[i % 2], x2b[i % 2]
                tv, xqv = tb[i % 2], xqb2[i % 2]
                c = 32 + 24 * (i % 2)
                ssum, ssq = ST[:, c:c+1], ST[:, c+1:c+2]
                sm, sva, srstd = ST[:, c+2:c+3], ST[:, c+3:c+4], ST[:, c+4:c+5]
                st1, st2 = ST[:, c+5:c+6], ST[:, c+6:c+7]
                sm2, sv2 = ST[:, c+7:c+8], ST[:, c+8:c+9]
                sa2, sb2, sc2 = ST[:, c+9:c+10], ST[:, c+10:c+11], ST[:, c+11:c+12]
                smx, sc1, sg2 = ST[:, c+12:c+13], ST[:, c+13:c+14], ST[:, c+14:c+15]
                p.dma("sync", ab, soutp_h[w0:w0 + 128, :],
                      waits=p.need(f"{sx}b1ab{i-2}"), name=f"{sx}b1la{i}",
                      slot=f"x{i%2}")
                p.dma("sync", xb, xseq[w0:w0 + 128, :],
                      waits=p.need(f"{sx}a4x2{i-2}"), name=f"{sx}b1lx{i}",
                      slot=f"xb{i%2}")
                wl = p.need(f"{sx}b1la{i}")
                ln_stats(p, ab, ssum, ssq, sm, sva, srstd, st1, st2, 1e-5,
                         wl, f"{sx}b1s{i}", scrapb1)
                p.op("vector", lambda e, ab=ab, sm=sm, srstd=srstd, tv=tv:
                     e.tensor_scalar(tv, ab, sm, srstd, OP.subtract, OP.mult),
                     name=f"{sx}b1ab{i}")
                p.op("vector", lambda e, x2v=x2v, xb=xb, tv=tv: e.tensor_add(
                        x2v, tv, xb),
                     waits=p.need(f"{sx}b1lx{i}", f"{sx}a4spx2{i-2}"),
                     name=f"{sx}a4x2{i}")
                p.dma("gpsimd", sx2_h[w0:w0 + 128, :], x2v,
                      waits=p.need(f"{sx}a4x2{i}"), name=f"{sx}a4spx2{i}",
                      slot=f"x2sp{i%2}")
                wx2 = p.need(f"{sx}a4x2{i}")
                p.op("vector", lambda e, x2v=x2v, ssum=ssum: e.reduce_sum(
                        ssum, x2v, AX.X), waits=wx2)
                p.op("scalar", lambda e, x2v=x2v, ssq=ssq: e.activation(
                        scrapb1, x2v, AF.Square, accum_out=ssq),
                     waits=wx2 + p.need(f"{sx}b1s{i}sq"), name=f"{sx}b1q{i}sq")
                p.op("vector", lambda e, sm2=sm2, ssum=ssum: e.tensor_scalar(
                        sm2, ssum, 1.0 / D, None, OP.mult))
                p.op("vector", lambda e, sv2=sv2, sm2=sm2: e.tensor_mul(
                        sv2, sm2, sm2), waits=p.need(f"{sx}b1q{i}sq"))
                p.op("vector", lambda e, sv2=sv2, ssq=ssq: e.scalar_tensor_tensor(
                        sv2, ssq, 1.0 / D, sv2, OP.mult, OP.subtract))
                p.op("vector", lambda e, sa2=sa2, sv2=sv2: e.tensor_scalar(
                        sa2, sv2, 1e-5, None, OP.add), name=f"{sx}b1a{i}")
                p.op("vector", lambda e, sb2=sb2, sa2=sa2, sv2=sv2:
                     e.scalar_tensor_tensor(sb2, sa2, 1e-6, sv2, OP.mult, OP.add),
                     waits=p.need(f"{sx}b1a{i}"), name=f"{sx}b1b{i}")
                sqrt_heron(p, sc2, sb2, st1, st2, waits=p.need(f"{sx}b1b{i}"),
                           name=f"{sx}b1h{i}")
                p.op("vector", lambda e, sc2=sc2: e.reciprocal(sc2, sc2),
                     name=f"{sx}b1rc{i}")
                p.op("vector", lambda e, x2v=x2v, sm2=sm2, tv=tv: e.tensor_scalar(
                        tv, x2v, sm2, None, OP.subtract), name=f"{sx}b1t{i}")
                p.op("vector", lambda e, smx=smx, tv=tv: e.reduce_max(
                        smx, tv, AX.X, apply_absolute_value=True),
                     name=f"{sx}b1mx{i}")
                p.op("vector", lambda e, sc1=sc1, smx=smx: e.reciprocal(sc1, smx))
                p.op("vector", lambda e, sc1=sc1: e.tensor_scalar(
                        sc1, sc1, 128.0, None, OP.mult))
                p.op("vector", lambda e, sg2=sg2, sc2=sc2, smx=smx: e.tensor_mul(
                        sg2, sc2, smx),
                     waits=p.need(f"{sx}b1rc{i}", f"{sx}b1mx{i}"))
                p.op("vector", lambda e, sg2=sg2: e.tensor_scalar(
                        sg2, sg2, 1.0 / 128.0, None, OP.mult), name=f"{sx}b1g{i}")
                p.op("vector", lambda e, tv=tv, sc1=sc1: e.tensor_scalar(
                        tv, tv, sc1, CLIP, OP.mult, OP.min), name=f"{sx}b1u{i}")
                p.op("vector", lambda e, tv=tv, xqv=xqv, sg2=sg2: e.tensor_scalar(
                        xqv, tv, -CLIP, sg2, OP.max, OP.mult),
                     waits=p.need(f"{sx}b1g{i}", f"{sx}b1u{i}"),
                     name=f"{sx}b1xq{i}")
                ps = (pA, pB)[i % 2]
                for kk in range(16):
                    p.op("tensor", lambda e, ps=ps, kk=kk, xqv=xqv: e.transpose(
                            ps[:, kk * 128:(kk + 1) * 128],
                            xqv[:, kk * 128:(kk + 1) * 128], ident.ap()),
                         waits=p.need(f"{sx}b1xq{i}", f"{sx}b1ev{i-2}"))
                p.tmark(f"{sx}b1tg{i}")
                bs = bst[i % 2]
                p.op("vector", lambda e, ps=ps, bs=bs: e.tensor_copy(
                        bs[:, 0, :], ps[:, :]),
                     waits=p.need(f"{sx}b1tg{i}", f"{sx}b1sp0_{i-2}",
                                  f"{sx}b1sp1_{i-2}"))
                p.op("vector", lambda e, ps=ps, bs=bs: e.tensor_tensor(
                        bs[:, 1, :], ps[:, :], bs[:, 0, :], OP.subtract),
                     name=f"{sx}b1ev{i}")
                for l in range(2):
                    p.dma("gpsimd",
                          sx2hf_h[l, :, :, w0:w0 + 128]
                          .rearrange("k p t -> p k t"),
                          bs[:, l, :].rearrange("p (k t) -> p k t", t=128),
                          waits=p.need(f"{sx}b1ev{i}"),
                          name=f"{sx}b1sp{l}_{i}", slot=f"a1sp{l}{i%2}")
            p.barrier()

         _ph()
        if "b2a" in stages:
         def _ph(sx=sx, xseq=xseq, oseq=oseq, build_wpair=build_wpair):
          with ExitStack() as ph:
            pe_ = ph.enter_context
            tqb = [pe_(nc.sbuf_tensor(f"b2t{sq}_{j}", [128, NK, 2048],
                                      BF16)).ap() for j in range(2)]
            gst = pe_(nc.sbuf_tensor(f"b2g{sq}", [128, 2048], F32)).ap()
            for qq in range(4):
                tq = tqb[qq % 2]
                p.dma("sync", tq,
                      tfc_h[:, :, qq * 2048:(qq + 1) * 2048]
                      .rearrange("k p f -> p k f"),
                      waits=p.need(f"{sx}b2mm{(qq-1)*NT-1}"),
                      name=f"{sx}b2t{qq}", slot=f"wld{qq%2}")
                for i in range(NT):
                    ti = qq * NT + i
                    sp = stat2b.ap()[:, ti % 2]
                    p.dma("scalar", sp,
                          sx2hf_h[:, :, :, i * 128:(i + 1) * 128]
                          .rearrange("l k p t -> p l k t"),
                          waits=p.need(f"{sx}b2mm{ti-2}"), name=f"{sx}b2st{ti}",
                          slot=f"st{ti%2}")
                    ps = (pA, pB)[ti % 2]
                    for kk in range(NK):
                        wmmk = p.need(f"{sx}b2t{qq}", f"{sx}b2st{ti}",
                                      f"{sx}b2ev{ti-2}")
                        for l in range(2):
                            for g in range(4):
                                p.op("tensor",
                                     lambda e, ps=ps, sp=sp, kk=kk, g=g, l=l,
                                     tq=tq: e.matmul(
                                        ps[:, g * 512:(g + 1) * 512],
                                        sp[:, l, kk, :],
                                        tq[:, kk, g * 512:(g + 1) * 512],
                                        start=(kk == 0 and l == 0),
                                        stop=(kk == NK - 1 and l == 1)),
                                     waits=wmmk)
                    p.tmark(f"{sx}b2mm{ti}")
                    p.op("scalar", lambda e, ps=ps: e.activation(
                            gst, ps[:, :], AF.Gelu, scale=gv.ap()[:, 0:1]),
                         waits=p.need(f"{sx}b2mm{ti}", f"{sx}b2sp{ti-1}"),
                         name=f"{sx}b2ev{ti}")
                    p.dma("gpsimd",
                          sh2_h[i * 128:(i + 1) * 128,
                                qq * 2048:(qq + 1) * 2048],
                          gst,
                          waits=p.need(f"{sx}b2ev{ti}"), name=f"{sx}b2sp{ti}",
                          slot="b2sp")
            p.barrier()

         _ph()
        if "b2b" in stages:
         def _ph(sx=sx, xseq=xseq, oseq=oseq, build_wpair=build_wpair):
          with ExitStack() as ph:
            pe_ = ph.enter_context
            hb_t = [pe_(nc.sbuf_tensor(f"b2h{sq}_{j}", [128, 8192], F32))
                    for j in range(2)]
            scr_t = pe_(nc.sbuf_tensor(f"b2s{sq}", [128, 8192], F32))
            bst = [pe_(nc.sbuf_tensor(f"b2b{sq}_{j}", [128, 2, 2048], BF16)).ap()
                   for j in range(2)]
            hbuf = [t.ap() for t in hb_t]
            scrap8 = scr_t.ap()
            for i in range(NT):
                w0 = i * 128
                hb = hbuf[i % 2]
                c = 96 + 16 * (i % 2)
                ssum, ssq = ST[:, c:c+1], ST[:, c+1:c+2]
                sm2, sv2 = ST[:, c+2:c+3], ST[:, c+3:c+4]
                sb2, sc2 = ST[:, c+4:c+5], ST[:, c+5:c+6]
                smx, sc1, sg2 = ST[:, c+6:c+7], ST[:, c+7:c+8], ST[:, c+8:c+9]
                st1, st2 = ST[:, c+9:c+10], ST[:, c+10:c+11]
                p.dma("sync", hb, sh2_h[w0:w0 + 128, :],
                      waits=p.need(f"{sx}b2btg{4*(i-2)+3}"), name=f"{sx}b2bl{i}",
                      slot=f"x{i%2}")
                wl = p.need(f"{sx}b2bl{i}")
                p.op("vector", lambda e, hb=hb, ssum=ssum: e.reduce_sum(
                        ssum, hb, AX.X), waits=wl)
                p.op("scalar", lambda e, hb=hb, ssq=ssq: e.activation(
                        scrap8, hb, AF.Square, accum_out=ssq),
                     waits=wl + p.need(f"{sx}b2bsq{i-1}"),
                     name=f"{sx}b2bsq{i}")
                p.op("vector", lambda e, sm2=sm2, ssum=ssum: e.tensor_scalar(
                        sm2, ssum, 1.0 / FF, None, OP.mult))
                p.op("vector", lambda e, sv2=sv2, sm2=sm2: e.tensor_mul(
                        sv2, sm2, sm2), waits=p.need(f"{sx}b2bsq{i}"))
                p.op("vector", lambda e, sv2=sv2, ssq=ssq: e.scalar_tensor_tensor(
                        sv2, ssq, 1.0 / FF, sv2, OP.mult, OP.subtract),
                     name=f"{sx}b2bv{i}")
                p.op("vector", lambda e, sb2=sb2, sv2=sv2: e.tensor_scalar(
                        sb2, sv2, EPS_Q, None, OP.add), name=f"{sx}b2ba{i}")
                sqrt_heron(p, sc2, sb2, st1, st2, waits=p.need(f"{sx}b2ba{i}"),
                           name=f"{sx}b2bh{i}")
                p.op("vector", lambda e, sc2=sc2: e.reciprocal(sc2, sc2),
                     name=f"{sx}b2brc{i}")
                p.op("vector", lambda e, hb=hb, sm2=sm2: e.tensor_scalar(
                        hb, hb, sm2, None, OP.subtract), name=f"{sx}b2bt_{i}")
                p.op("vector", lambda e, hb=hb, smx=smx: e.reduce_max(
                        smx, hb, AX.X, apply_absolute_value=True),
                     name=f"{sx}b2bmx{i}")
                p.op("vector", lambda e, sc1=sc1, smx=smx: e.reciprocal(sc1, smx))
                p.op("vector", lambda e, sc1=sc1: e.tensor_scalar(
                        sc1, sc1, 128.0, None, OP.mult))
                p.op("vector", lambda e, sg2=sg2, sc2=sc2, smx=smx: e.tensor_mul(
                        sg2, sc2, smx),
                     waits=p.need(f"{sx}b2brc{i}", f"{sx}b2bmx{i}"))
                p.op("vector", lambda e, sg2=sg2: e.tensor_scalar(
                        sg2, sg2, 1.0 / 128.0, None, OP.mult), name=f"{sx}b2bg{i}")
                p.op("vector", lambda e, hb=hb, sc1=sc1: e.tensor_scalar(
                        hb, hb, sc1, CLIP, OP.mult, OP.min))
                p.op("vector", lambda e, hb=hb, sg2=sg2: e.tensor_scalar(
                        hb, hb, -CLIP, sg2, OP.max, OP.mult),
                     waits=p.need(f"{sx}b2bg{i}"), name=f"{sx}b2bxq{i}")
                for g in range(4):
                    gi = 4 * i + g
                    ps = (pA, pB)[gi % 2]
                    for j in range(16):
                        kk = g * 16 + j
                        p.op("tensor", lambda e, ps=ps, j=j, kk=kk, hb=hb:
                             e.transpose(ps[:, j * 128:(j + 1) * 128],
                                         hb[:, kk * 128:(kk + 1) * 128],
                                         ident.ap()),
                             waits=p.need(f"{sx}b2bxq{i}", f"{sx}b2bev{gi-2}"))
                    p.tmark(f"{sx}b2btg{gi}")
                    bs = bst[gi % 2]
                    p.op("vector", lambda e, ps=ps, bs=bs: e.tensor_copy(
                            bs[:, 0, :], ps[:, :]),
                         waits=p.need(f"{sx}b2btg{gi}", f"{sx}b2bsp0_{gi-2}",
                                      f"{sx}b2bsp1_{gi-2}"))
                    p.op("vector", lambda e, ps=ps, bs=bs: e.tensor_tensor(
                            bs[:, 1, :], ps[:, :], bs[:, 0, :], OP.subtract),
                         name=f"{sx}b2bev{gi}")
                    for l in range(2):
                        p.dma("gpsimd",
                              sx2h2f_h[l, 16 * g:16 * (g + 1), :, w0:w0 + 128]
                              .rearrange("k p t -> p k t"),
                              bs[:, l, :].rearrange("p (k t) -> p k t", t=128),
                              waits=p.need(f"{sx}b2bev{gi}"),
                              name=f"{sx}b2bsp{l}_{gi}", slot=f"a1sp{l}{gi%2}")
            p.barrier()

         _ph()
        if "b3" in stages:
         def _ph(sx=sx, xseq=xseq, oseq=oseq, build_wpair=build_wpair):
          with ExitStack() as ph:
            pe_ = ph.enter_context
            tp4 = pe_(nc.sbuf_tensor(f"b3t{sq}", [128, NK2, 512], BF16)).ap()
            spb = [pe_(nc.sbuf_tensor(f"b3s{sq}_{j}", [128, 2, NK2, 128],
                                      BF16)).ap() for j in range(2)]
            evst = pe_(nc.sbuf_tensor(f"b3e{sq}", [128, 512], F32)).ap()
            for qq in range(4):
                p.dma("sync", tp4,
                      tpj_h[:, :, qq * 512:(qq + 1) * 512]
                      .rearrange("k p f -> p k f"),
                      waits=p.need(f"{sx}b3mm{qq*NT-1}"), name=f"{sx}b3t{qq}",
                      slot="wld")
                for i in range(NT):
                    ti = qq * NT + i
                    sp = spb[ti % 2]
                    p.dma("scalar", sp,
                          sx2h2f_h[:, :, :, i * 128:(i + 1) * 128]
                          .rearrange("l k p t -> p l k t"),
                          waits=p.need(f"{sx}b3mm{ti-2}"), name=f"{sx}b3st{ti}",
                          slot=f"st{ti%2}")
                    ps = (pA, pB)[ti % 2]
                    for kk in range(NK2):
                        wmmk = p.need(f"{sx}b3t{qq}", f"{sx}b3st{ti}",
                                      f"{sx}b3ev{ti-2}")
                        for l in range(2):
                            p.op("tensor",
                                 lambda e, ps=ps, sp=sp, kk=kk, l=l: e.matmul(
                                    ps[:, 0:512], sp[:, l, kk, :],
                                    tp4[:, kk, :],
                                    start=(kk == 0 and l == 0),
                                    stop=(kk == NK2 - 1 and l == 1)),
                                 waits=wmmk)
                    p.tmark(f"{sx}b3mm{ti}")
                    p.op("vector", lambda e, ps=ps: e.tensor_scalar(
                            evst, ps[:, 0:512], gv.ap()[:, 1:2], None, OP.mult),
                         waits=p.need(f"{sx}b3mm{ti}", f"{sx}b3sp{ti-1}"),
                         name=f"{sx}b3ev{ti}")
                    p.dma("gpsimd",
                          soutp_h[i * 128:(i + 1) * 128,
                                  qq * 512:(qq + 1) * 512],
                          evst,
                          waits=p.need(f"{sx}b3ev{ti}"), name=f"{sx}b3sp{ti}",
                          slot="b3sp")
            p.barrier()

         _ph()
        if "b3b" in stages:
         def _ph(sx=sx, xseq=xseq, oseq=oseq, build_wpair=build_wpair):
          with ExitStack() as ph:
            pe_ = ph.enter_context
            def mk2b(nm):
                return [pe_(nc.sbuf_tensor(f"b3b{sq}{nm}{j}", [128, 2048],
                                           F32)).ap() for j in range(2)]
            obuf, x2b = mk2b("o"), mk2b("x")
            scrap3 = pe_(nc.sbuf_tensor(f"b3b{sq}s", [128, 2048], F32)).ap()
            ostg = pe_(nc.sbuf_tensor(f"b3b{sq}g", [128, 2048], F32)).ap()
            for i in range(NT):
                w0 = i * 128
                ob, x2v = obuf[i % 2], x2b[i % 2]
                c = 128 + 12 * (i % 2)
                ssum, ssq = ST[:, c:c+1], ST[:, c+1:c+2]
                sm, sva, srstd = ST[:, c+2:c+3], ST[:, c+3:c+4], ST[:, c+4:c+5]
                st1, st2 = ST[:, c+5:c+6], ST[:, c+6:c+7]
                p.dma("sync", ob, soutp_h[w0:w0 + 128, :],
                      waits=p.need(f"{sx}b3bo{i-2}"), name=f"{sx}b3bl{i}",
                      slot=f"x{i%2}")
                p.dma("sync", x2v, sx2_h[w0:w0 + 128, :],
                      waits=p.need(f"{sx}b3bo{i-2}"), name=f"{sx}b3blx{i}",
                      slot=f"xb{i%2}")
                wl = p.need(f"{sx}b3bl{i}")
                ln_stats(p, ob, ssum, ssq, sm, sva, srstd, st1, st2, 1e-5,
                         wl, f"{sx}b3b{i}", scrap3)
                p.op("vector", lambda e, ob=ob, sm=sm, srstd=srstd:
                     e.tensor_scalar(ostg, ob, sm, srstd,
                                     OP.subtract, OP.mult),
                     waits=p.need(f"{sx}b3bo{i-1}"))
                p.op("vector", lambda e, x2v=x2v: e.tensor_add(
                        ostg, ostg, x2v),
                     waits=p.need(f"{sx}b3blx{i}"), name=f"{sx}b3bst{i}")
                p.dma("gpsimd", oseq[w0:w0 + 128, :], ostg,
                      waits=p.need(f"{sx}b3bst{i}"), name=f"{sx}b3bo{i}",
                      slot="outs")
            p.barrier()

         _ph()
    p.emit("gpsimd", lambda e: e.nop(),
           [(s, p.v(s)) for s in p.cnt if s.startswith("d_")], None)

    return nc, p, stack


def build_and_realize(cfg):
    nc, p, stack = build_program(cfg)
    with stack:
        p.realize(nc, stack)
    return nc, len(p.ops)


def make_in_map(inputs, core, n_seq=2):
    consts = build_consts()
    x = np.ascontiguousarray(inputs["L0_attn_x"][2 * core:2 * core + n_seq]
                             .reshape(n_seq * S, D))
    m = dict(x=x,
             wq=inputs["L0_attn_Wq"], wk=inputs["L0_attn_Wk"],
             wv=inputs["L0_attn_Wv"], wg=inputs["L0_attn_Wg"],
             wo=inputs["L0_attn_Wo"],
             wfct=np.ascontiguousarray(inputs["L0_ffn_w_fc"].T),
             wpjt=np.ascontiguousarray(inputs["L0_ffn_w_proj"].T),
             gnw=np.ascontiguousarray(inputs["L0_attn_g_norm_w"].reshape(128, 1)),
             **consts)
    return m


def kernel(**inputs):
    inputs = {k: np.asarray(v, dtype=np.float32) for k, v in inputs.items()}
    nc, nops = build_and_realize({})
    in_maps = [make_in_map(inputs, c) for c in range(8)]
    res = run_bass_kernel_spmd(nc, in_maps, core_ids=list(range(8)))
    out = np.concatenate([r["out"].reshape(2, S, D) for r in res.results], axis=0)
    return out
